# version 22
# baseline (speedup 1.0000x reference)
import contextlib
import numpy as np
import ml_dtypes

import concourse.bass as bass
import concourse.mybir as mybir
from concourse.bass_utils import run_bass_kernel_spmd

F32 = mybir.dt.float32
BF16 = mybir.dt.bfloat16
U8 = mybir.dt.uint8
AF = mybir.ActivationFunctionType
ALU = mybir.AluOpType

T = 2048
D = 1024
NCH = 8
CW = 2049
H = 16
HD = 64
SC = 64
NSC = 32
FF = 4096

DEV = False
STAGE_LIMIT = 99
L0SUB = 4

ENGS = ("sync", "gpsimd", "tensor", "vector", "scalar")


def tcol(k, t0=0):
    return CW * k + 1 + t0


class Sem:
    __slots__ = ("h", "n", "name")

    def __init__(self, h, name):
        self.h = h
        self.n = 0
        self.name = name


class Em:

    def __init__(self, nc, stack):
        self.nc = nc
        self.stack = stack
        self.ops = {e: [] for e in ENGS}
        self._semi = 0
        self._last_out = {}

    def sem(self, name=None):
        self._semi += 1
        h = self.stack.enter_context(self.nc.semaphore(name or f"s{self._semi}"))
        return Sem(h, name or f"s{self._semi}")

    def emit(self, eng, fn):
        self.ops[eng].append(fn)

    def wait(self, eng, sem, val=None):
        v = sem.n if val is None else val
        self.emit(eng, lambda e, h=sem.h, v=v: e.wait_ge(h, v))

    _FENCED = ("vector", "scalar", "gpsimd")
    _NOFENCE_METHS = ("nop", "drain", "sem_inc", "wait_ge", "dma_start")

    def _ap_tensors(self, vals):
        names = set()
        for v in vals:
            t = getattr(v, "tensor", None)
            if t is not None:
                names.add(t.name)
        return names

    def do(self, eng, meth, *a, sem=None, inc=1, nofence=False, **kw):
        if sem is not None:
            sem.n += inc
        fence_capable = eng in self._FENCED and meth not in self._NOFENCE_METHS
        need_fence = False
        if eng in self._FENCED and meth not in ("nop", "drain", "sem_inc", "wait_ge"):
            outs = self._ap_tensors([kw.get("out"), a[0] if a else None])
            uses = self._ap_tensors(list(a) + list(kw.values()))
            hist = self._last_out.setdefault(eng, [])
            if any(prev & uses for prev in hist) and not nofence:
                need_fence = True
            if fence_capable:
                hist.append(outs)
            else:
                hist.append(set())
            del hist[:-2]

        def fn(e, meth=meth, a=a, kw=kw, s=sem, inc=inc, pre=need_fence):
            if pre:
                e.drain()
            ins = getattr(e, meth)(*a, **kw)
            if s is not None:
                ins.then_inc(s.h, inc)

        self.emit(eng, fn)

    def dma(self, eng, out, in_, sem=None):
        self.do(eng, "dma_start", out=out, in_=in_, sem=sem, inc=16)

    def mm(self, out, lhsT, rhs, start=True, stop=True, tp=None, sem=None):
        kw = dict(start=start, stop=stop)
        if tp is not None:
            kw["tile_position"] = tp
        self.do("tensor", "matmul", out, lhsT, rhs, sem=sem, **kw)

    def barrier(self, sems_from, engs_to):
        for e in engs_to:
            for s in sems_from:
                self.wait(e, s)


def host_constants():
    c = {}
    c["ID128"] = np.eye(128, dtype=np.float32)
    c["ONES"] = np.ones((128, 128), np.float32)
    r = np.zeros((64, 64), np.float32)
    for i in range(32):
        r[2 * i, 2 * i + 1] = -1.0
        r[2 * i + 1, 2 * i] = 1.0
    R = np.zeros((128, 128), np.float32)
    R[0:64, 0:64] = r
    R[64:128, 64:128] = r
    c["R2T"] = np.ascontiguousarray(R.T)
    inv = 1.0 / (10000.0 ** (np.arange(0, HD, 2, dtype=np.float64) / HD))
    tt = np.arange(T, dtype=np.float64)
    ang = tt[None, :] * inv[(np.arange(64) % 32)][:, None]
    cos = np.cos(ang).astype(np.float32)
    sin = np.sin(ang).astype(np.float32)
    c["COSQ"] = np.concatenate([cos, cos], 0)
    c["SINQ"] = np.concatenate([sin, sin], 0)
    p64 = np.arange(128) % 64
    c1 = np.arange(64)
    c["DELTA64"] = (c1[None, :] - 1 - p64[:, None]).astype(np.float32)
    c["TRIMASK64"] = (c1[None, :] > p64[:, None]).astype(np.float32)
    c["C1PAT"] = np.broadcast_to(c1[None, :], (128, 64)).astype(np.float32)
    c["P63"] = (63.0 - p64).astype(np.float32).reshape(128, 1)
    c["IOTA2D"] = (np.arange(128)[:, None] + 128.0 * np.arange(16)[None, :]).astype(np.float32)
    c["IOTAROW"] = np.arange(T, dtype=np.float32).reshape(1, T)
    u = np.arange(1024, dtype=np.float32)
    c["MPAT"] = np.maximum(u[None, :] - np.arange(128)[:, None], 0.0).astype(np.float32)
    vals = [0.0, 1.0, -1.0, 2.0, 0.5, 1.0 / D, 1e-5, 1.0 / (T * HD),
            -8.0 * np.log(2.0), -5.0 * np.log(2.0), 1.0 / 8.0, 64.0, 1.0 / T]
    c["IDUP"] = np.zeros((128, 128), np.float32)
    for i in range(64):
        c["IDUP"][i, 64 + i] = 1.0
    c["CONSTCOL"] = np.stack([np.full(128, v, np.float32) for v in vals], 1)
    c["_constidx"] = {v: i for i, v in enumerate(vals)}
    return c


_CONSTS = host_constants()
CIDX = _CONSTS["_constidx"]


def host_inputs(inputs, b):
    d = {}
    d["x"] = np.ascontiguousarray(inputs["L0_attn_out"][b]).astype(np.float32, copy=False)
    tr = lambda w: np.ascontiguousarray(np.asarray(w).T)
    d["WwT"] = tr(inputs["L0_attn_W_write"])
    d["WreadT"] = tr(inputs["L0_attn_W_read"])
    d["decay_row"] = np.asarray(inputs["L0_attn_decay"]).reshape(1, H)
    d["logalpha_row"] = np.asarray(inputs["L0_attn_log_alpha"]).reshape(1, H)
    d["ln0g"] = np.ascontiguousarray(np.asarray(inputs["L0_ffn_ln_gamma"]).reshape(NCH, 128).T)
    d["ln0b"] = np.ascontiguousarray(np.asarray(inputs["L0_ffn_ln_beta"]).reshape(NCH, 128).T)
    f1q = np.asarray(inputs["L0_ffn_fc1_q"]).astype(np.uint8)
    d["fc1q"] = np.ascontiguousarray(f1q.reshape(D, 8, 512).transpose(1, 0, 2))
    d["fc2q"] = np.asarray(inputs["L0_ffn_fc2_q"]).astype(np.uint8)
    f1s = np.repeat(np.asarray(inputs["L0_ffn_fc1_scales"]), 128, 0).astype(np.float32)
    d["fc1s"] = np.ascontiguousarray(f1s.reshape(D, 8, 512).transpose(1, 0, 2))
    d["fc2s"] = np.ascontiguousarray(np.repeat(np.asarray(inputs["L0_ffn_fc2_scales"]), 128, 0).astype(np.float32))
    d["b1"] = np.ascontiguousarray(np.asarray(inputs["L0_ffn_fc1_bias"]).reshape(FF // 128, 128).T)
    d["b2"] = np.ascontiguousarray(np.asarray(inputs["L0_ffn_fc2_bias"]).reshape(NCH, 128).T)
    for nm in ("wq", "wk", "wv", "wg", "wo"):
        d[nm + "T"] = tr(inputs[f"L1_attn_{nm}"])
    for nm in ("bq", "bk", "bv", "bg", "bo"):
        d[nm] = np.ascontiguousarray(np.asarray(inputs[f"L1_attn_{nm}"]).reshape(NCH, 128).T)
    d["bv_row"] = np.asarray(inputs["L1_attn_bv"]).reshape(1, D)
    d["wdT"] = tr(inputs["L1_attn_wd"])
    d["bd_row"] = np.asarray(inputs["L1_attn_bd"]).reshape(1, H)
    gmt = np.asarray(inputs["L1_attn_gamma"]).reshape(H, HD).T
    btt = np.asarray(inputs["L1_attn_beta"]).reshape(H, HD).T
    d["gamT"] = np.ascontiguousarray(np.concatenate([gmt[:, 0::2], gmt[:, 1::2]], 0))
    d["betT"] = np.ascontiguousarray(np.concatenate([btt[:, 0::2], btt[:, 1::2]], 0))
    d["WrT"] = tr(inputs["L1_ffn_W_r"])
    d["WkT"] = tr(inputs["L1_ffn_W_k"])
    d["WvT"] = tr(inputs["L1_ffn_W_v"])
    d["mur"] = np.ascontiguousarray(np.asarray(inputs["L1_ffn_mu_r"]).reshape(NCH, 128).T)
    d["muk"] = np.ascontiguousarray(np.asarray(inputs["L1_ffn_mu_k"]).reshape(NCH, 128).T)
    for k, v in _CONSTS.items():
        if k != "_constidx":
            d["c_" + k] = v
    return d


def build_nc(dev=False):
    nc = bass.Bass("TRN2", target_bir_lowering=False, detect_race_conditions=False)

    ap = {}

    def din(name, shape, dt=F32):
        ap[name] = nc.dram_tensor(name, list(shape), dt, kind="ExternalInput").ap()

    din("x", (T, D))
    din("WwT", (D, D)); din("WreadT", (D, D))
    din("decay_row", (1, H)); din("logalpha_row", (1, H))
    din("ln0g", (128, NCH)); din("ln0b", (128, NCH))
    din("fc1q", (8, D, 512), U8); din("fc2q", (FF, D), U8)
    din("fc1s", (8, D, 512)); din("fc2s", (FF, D))
    din("b1", (128, FF // 128)); din("b2", (128, NCH))
    for nm in ("wqT", "wkT", "wvT", "wgT", "woT"):
        din(nm, (D, D))
    for nm in ("bq", "bk", "bv", "bg", "bo"):
        din(nm, (128, NCH))
    din("wdT", (D, H)); din("bd_row", (1, H)); din("bv_row", (1, D))
    din("gamT", (128, H // 2)); din("betT", (128, H // 2))
    din("WrT", (D, D)); din("WkT", (D, D)); din("WvT", (D, D))
    din("mur", (128, NCH)); din("muk", (128, NCH))
    din("c_ID128", (128, 128)); din("c_ONES", (128, 128))
    din("c_R2T", (128, 128))
    din("c_COSQ", (128, T)); din("c_SINQ", (128, T))
    din("c_DELTA64", (128, 64)); din("c_TRIMASK64", (128, 64))
    din("c_C1PAT", (128, 64)); din("c_P63", (128, 1))
    din("c_IOTA2D", (128, 16)); din("c_IOTAROW", (1, T))
    din("c_MPAT", (128, 1024)); din("c_CONSTCOL", (128, 13)); din("c_IDUP", (128, 128))

    y = nc.dram_tensor("y", [T, D], F32, kind="ExternalOutput").ap()
    hscr = nc.dram_tensor("hscr", [FF, T], BF16, kind="Internal").ap()
    strmscr = nc.dram_tensor("strmscr", [128, NCH * CW], F32, kind="Internal").ap()
    dbg = {}
    if dev:
        for nm, shape in [("d_strm0", (128, NCH * CW)), ("d_strm1", (128, NCH * CW)),
                          ("d_strm2", (128, NCH * CW)), ("d_strm3", (128, NCH * CW)),
                          ("d_strm4", (128, NCH * CW))]:
            dbg[nm] = nc.dram_tensor(nm, list(shape), F32, kind="ExternalOutput").ap()
        for nm, shape, dt in [("d_vtok", (128, 8192), BF16), ("d_xptok", (128, 8192), BF16),
                              ("d_reads", (128, 8192), BF16), ("d_state", (128, 512), F32),
                              ("d_mht", (128, 1024), F32), ("d_smh", (128, 1024), BF16),
                              ("d_rep", (128, 64), F32), ("d_rowb", (1, 64), F32),
                              ("d_gp", (128, 1024), F32), ("d_gw", (128, 1024), BF16),
                              ("d_gc", (128, 512), F32)]:
            dbg[nm] = nc.dram_tensor(nm, list(shape), F32 if dt == F32 else F32, kind="ExternalOutput").ap()

    with contextlib.ExitStack() as st0:
        em = Em(nc, st0)
        sb = lambda n, s, d=F32: st0.enter_context(nc.sbuf_tensor(n, list(s), d))
        ps = lambda n, s: st0.enter_context(nc.psum_tensor(n, list(s), F32))

        CONST = sb("CONST", (128, 13))
        ID = sb("ID", (128, 128))
        ONES = sb("ONESS", (128, 128))
        R2T = sb("R2TS", (128, 128), BF16)
        XHAT = sb("XHAT", (128, NCH * CW), BF16)
        WARM = sb("WARM", (128, 1024), BF16)
        IDBF = sb("IDBF", (128, 128), BF16)
        LDS = sb("LDS", (128, 32))

        def warm(func):
            em.do("scalar", "activation", WARM[:, :],
                  CONST[:, 0:1].to_broadcast((128, 1024)), func)

        cc = lambda v: CONST[:, CIDX[v]:CIDX[v] + 1]
        cc1 = lambda v: CONST[0:1, CIDX[v]:CIDX[v] + 1]

        ld = em.sem("ld")
        ldc = em.sem("ldc")
        lx0 = em.sem("lx0")
        lx1 = em.sem("lx1")
        wld = em.sem("wld")
        main = em.sem("main")
        tdone = em.sem("tdone")
        vdone = em.sem("vdone")
        sdone = em.sem("sdone")
        stdone = em.sem("stdone")
        gdump = em.sem("gdump")

        for nm, buf in [("c_CONSTCOL", CONST), ("c_ID128", ID), ("c_ONES", ONES)]:
            em.dma("sync", buf[:, :], ap[nm][:, :], sem=ldc)
        em.dma("gpsimd", R2T[:, :], ap["c_R2T"][:, :], sem=wld)

        with contextlib.ExitStack() as stp:
            sbp = lambda n, s, d=F32: stp.enter_context(nc.sbuf_tensor(n, list(s), d))
            psp = lambda n, s: stp.enter_context(nc.psum_tensor(n, list(s), F32))
            strm_st = contextlib.ExitStack()
            STRM = strm_st.enter_context(nc.sbuf_tensor("STRM", [128, NCH * CW], F32))
            STRMSTK = [strm_st]

            xtok = sbp("xtok", (128, 2 * D))
            ptr = psp("ptr", (128, 1024))
            ptr2 = psp("ptr2", (128, 1024))

            em.wait("vector", ldc)
            em.do("vector", "tensor_copy", out=IDBF[:, :], in_=ID[:, :])
            em.do("vector", "memset", STRM[:, :].rearrange("p (k w) -> p k w", w=CW)[:, :, 0:1], 0.0, sem=vdone)

            td0 = tdone.n
            vd0 = vdone.n
            lxs = [lx0, lx1]
            for tc in range(16):
                if tc >= 2:
                    em.wait("sync", tdone, td0 + tc - 1)
                em.dma("sync", xtok[:, (tc % 2) * D:(tc % 2) * D + D],
                       ap["x"][128 * tc:128 * tc + 128, :], sem=lxs[tc % 2])
            em.barrier([ldc], ["tensor"])
            for tc in range(16):
                em.wait("tensor", lxs[tc % 2], 16 * (tc // 2 + 1))
                if tc >= 2:
                    em.wait("tensor", vdone, vd0 + tc - 1)
                pt = ptr if tc % 2 == 0 else ptr2
                for ec in range(NCH):
                    em.do("tensor", "transpose",
                          pt[:, 128 * ec:128 * ec + 128],
                          xtok[:, (tc % 2) * D + 128 * ec:(tc % 2) * D + 128 * ec + 128],
                          ID[:, :], sem=(tdone if ec == NCH - 1 else None))

            for tc in range(16):
                em.wait("vector", tdone, td0 + tc + 1)
                pt = ptr if tc % 2 == 0 else ptr2
                outap = STRM[:, :].rearrange("p (k w) -> p k w", w=CW)[:, :, 1 + 128 * tc:1 + 128 * tc + 128]
                em.do("vector", "tensor_copy", out=outap, in_=pt[:, :].rearrange("p (k w) -> p k w", w=128),
                      sem=vdone)
            em.wait("vector", vdone)
            em.do("vector", "tensor_copy", out=XHAT[:, :], in_=STRM[:, :], sem=vdone)

        if dev:
            em.wait("sync", vdone)
            em.dma("sync", dbg["d_strm0"][:, :], STRM[:, :], sem=stdone)
            em.barrier([stdone], ["vector", "tensor", "scalar"])

        def emit_ln(tag, affine=None, write_strm=True, write_xhat=True):
            with contextlib.ExitStack() as stl:
                ROWS = stl.enter_context(nc.sbuf_tensor(tag + "ROWS", [1, 4 * 2048], F32))
                YSQA = stl.enter_context(nc.sbuf_tensor(tag + "YSQA", [128, NCH * 512], F32))
                YSQB = stl.enter_context(nc.sbuf_tensor(tag + "YSQB", [128, NCH * 512], F32))
                pLNa = stl.enter_context(nc.psum_tensor(tag + "pLNa", [1, 512], F32))
                pLNb = stl.enter_context(nc.psum_tensor(tag + "pLNb", [1, 512], F32))
                pMUr = [stl.enter_context(nc.psum_tensor(tag + "pMU%d" % i, [128, 512], F32))
                        for i in range(2)]
                pRSr = [stl.enter_context(nc.psum_tensor(tag + "pRS%d" % i, [128, 512], F32))
                        for i in range(2)]
                MUB = [stl.enter_context(nc.sbuf_tensor(tag + "MUB%d" % i, [128, 512], F32))
                       for i in range(2)]
                RSB = [stl.enter_context(nc.sbuf_tensor(tag + "RSB%d" % i, [128, 512], F32))
                       for i in range(2)]
                gdn = em.sem(tag + "gdn")
                em.barrier([vdone], ["scalar", "tensor"])
                sq_s = {}
                sum_s = {}
                mr_s = {}
                rep_s = {}
                nv_s = {}
                ng_s = {}
                for qt in range(4):
                    q0 = 512 * qt
                    YSQ = YSQA if qt % 2 == 0 else YSQB
                    pMU = pMUr[qt % 2]
                    pRS = pRSr[qt % 2]
                    if qt >= 2:
                        em.wait("scalar", vdone, nv_s[qt - 2])
                        em.wait("scalar", gdn, ng_s[qt - 2])
                    for k in range(NCH):
                        em.do("scalar", "activation", YSQ[:, 512 * k:512 * k + 512],
                              STRM[:, tcol(k, q0):tcol(k, q0) + 512], AF.Square,
                              sem=(sdone if k == NCH - 1 else None))
                    sq_s[qt] = sdone.n
                    em.wait("tensor", sdone, sq_s[qt])
                    for k in range(NCH):
                        em.mm(pLNa[0:1, :], ONES[:, 0:1], STRM[:, tcol(k, q0):tcol(k, q0) + 512],
                              start=(k == 0), stop=(k == NCH - 1))
                    for k in range(NCH):
                        em.mm(pLNb[0:1, :], ONES[:, 0:1], YSQ[:, 512 * k:512 * k + 512],
                              start=(k == 0), stop=(k == NCH - 1),
                              sem=(tdone if k == NCH - 1 else None))
                    sum_s[qt] = tdone.n
                    em.wait("vector", tdone, sum_s[qt])
                    R0 = 2048 * qt
                    mu = ROWS[0:1, R0:R0 + 512]
                    m2 = ROWS[0:1, R0 + 512:R0 + 1024]
                    var = ROWS[0:1, R0 + 1024:R0 + 1536]
                    rstd = ROWS[0:1, R0 + 1536:R0 + 2048]
                    em.do("vector", "tensor_scalar", mu, pLNa[0:1, :],
                          CONST[0:1, CIDX[1.0 / D]:CIDX[1.0 / D] + 1], None, ALU.mult)
                    em.do("vector", "tensor_scalar", m2, pLNb[0:1, :],
                          CONST[0:1, CIDX[1.0 / D]:CIDX[1.0 / D] + 1], None, ALU.mult)
                    em.do("vector", "tensor_tensor", var, mu, mu, ALU.mult)
                    em.do("vector", "tensor_tensor", var, m2, var, ALU.subtract, sem=vdone)
                    em.wait("scalar", vdone)
                    em.do("scalar", "activation", rstd, var, AF.Ln,
                          bias=CONST[0:1, CIDX[1e-5]:CIDX[1e-5] + 1], scale=1.0)
                    em.do("scalar", "activation", rstd, rstd, AF.Exp, scale=-0.5, sem=sdone)
                    em.wait("vector", sdone)
                    em.do("vector", "tensor_tensor", mu, mu, rstd, ALU.mult, sem=vdone)
                    mr_s[qt] = vdone.n
                    em.wait("tensor", vdone, mr_s[qt])
                    if qt >= 2:
                        em.wait("tensor", vdone, nv_s[qt - 2])
                        em.wait("tensor", gdn, ng_s[qt - 2])
                    em.mm(pMU[:, :], ONES[0:1, :], mu, sem=None)
                    em.mm(pRS[:, :], ONES[0:1, :], rstd, sem=tdone)
                    rep_s[qt] = tdone.n
                    em.wait("vector", tdone, rep_s[qt])
                    if qt >= 2:
                        em.wait("vector", gdn, ng_s[qt - 2])
                    mub, rsb = MUB[qt % 2][:, :], RSB[qt % 2][:, :]
                    em.do("vector", "tensor_copy", out=mub, in_=pMU[:, :])
                    em.do("vector", "tensor_copy", out=rsb, in_=pRS[:, :], sem=vdone)
                    cp_s = vdone.n
                    em.wait("gpsimd", vdone, cp_s)
                    for k in range(NCH):
                        eng = "vector" if k % 2 == 0 else "gpsimd"
                        mu_t = pMU[:, :] if k % 2 == 0 else mub
                        rs_t = pRS[:, :] if k % 2 == 0 else rsb
                        sl = STRM[:, tcol(k, q0):tcol(k, q0) + 512]
                        t1 = YSQ[:, 512 * k:512 * k + 512]
                        em.do(eng, "tensor_tensor", t1, sl, rs_t, ALU.mult)
                        if affine is None:
                            if write_strm:
                                em.do(eng, "tensor_tensor", sl, t1, mu_t, ALU.subtract)
                                if write_xhat:
                                    em.do(eng, "tensor_copy",
                                          out=XHAT[:, tcol(k, q0):tcol(k, q0) + 512], in_=sl)
                            else:
                                em.do(eng, "tensor_tensor",
                                      XHAT[:, tcol(k, q0):tcol(k, q0) + 512], t1, mu_t,
                                      ALU.subtract)
                        else:
                            gcols, bcols = affine
                            em.do(eng, "tensor_tensor", t1, t1, mu_t, ALU.subtract)
                            em.do(eng, "tensor_scalar",
                                  XHAT[:, tcol(k, q0):tcol(k, q0) + 512], t1,
                                  gcols[:, k:k + 1], bcols[:, k:k + 1], ALU.mult, ALU.add)
                    em.do("vector", "engine_nop", sem=vdone)
                    nv_s[qt] = vdone.n
                    em.do("gpsimd", "engine_nop", sem=gdn)
                    ng_s[qt] = gdn.n
                em.wait("vector", gdn, ng_s[3])
                em.do("vector", "engine_nop", sem=vdone)

        if STAGE_LIMIT >= 2:
         with contextlib.ExitStack() as stp:
            sbp = lambda n, s, d=F32: stp.enter_context(nc.sbuf_tensor(n, list(s), d))
            WWT = sbp("WWT", (128, NCH * 1024), BF16)
            WRT = sbp("WRT", (128, NCH * 1024), BF16)
            WRTRAW = sbp("WRTRAW", (128, 1024))
            ROWB = sbp("ROWB", (1, 64))
            REP = sbp("REP", (128, 64))
            ALPH2 = sbp("ALPH2", (128, 8))
            MHT = sbp("MHT", (128, 1024))
            GP = sbp("GPt", (128, 1024))
            GW = sbp("GWt", (128, 1024), BF16)
            GC = sbp("GCt", (128, 512))
            DELTA = sbp("DELTA", (128, 64))
            TRIM = sbp("TRIM", (128, 64))
            C1P = sbp("C1P", (128, 64))
            P63 = sbp("P63", (128, 1))
            DROW = sbp("DROW", (1, 16))
            LAROW = sbp("LAROW", (1, 16))
            TROW = sbp("TROW", (1, 64))
            STATE = sbp("STATE", (128, 512))
            SMH = sbp("SMH", (128, 1024), BF16)
            VG = sbp("VG", (128, 1024), BF16)
            VTOK = sbp("VTOK", (128, 8 * 1024), BF16)
            XPTOK = sbp("XPTOK", (128, 8 * 1024), BF16)
            READS = sbp("READS", (128, 8 * 1024), BF16)

            em.barrier([tdone, vdone], ["gpsimd", "sync", "scalar", "vector"])
            for dk in range(NCH):
                em.dma("gpsimd", WWT[:, 1024 * dk:1024 * dk + 1024],
                       ap["WwT"][128 * dk:128 * dk + 128, :], sem=wld)
            for nm, buf in [("c_DELTA64", DELTA), ("c_TRIMASK64", TRIM), ("c_C1PAT", C1P),
                            ("c_P63", P63), ("decay_row", DROW), ("logalpha_row", LAROW)]:
                em.dma("sync", buf[:, :], ap[nm][:, :], sem=ld)

            em.barrier([ld], ["scalar"])
            warm(AF.Exp)
            em.do("scalar", "activation", TROW[0:1, 0:16], DROW[0:1, :], AF.Exp, scale=-1.0, sem=sdone)
            em.wait("vector", sdone)
            em.do("vector", "tensor_scalar", TROW[0:1, 0:16], TROW[0:1, 0:16],
                  CONST[0:1, CIDX[1.0]:CIDX[1.0] + 1], None, ALU.add)
            em.do("vector", "reciprocal", out=ROWB[0:1, 0:16], in_=TROW[0:1, 0:16], sem=vdone)
            em.wait("scalar", vdone)
            em.do("scalar", "activation", ROWB[0:1, 16:32], ROWB[0:1, 0:16], AF.Ln)
            em.do("scalar", "activation", ROWB[0:1, 32:48], ROWB[0:1, 16:32], AF.Exp, scale=64.0)
            em.do("scalar", "activation", ROWB[0:1, 48:64], LAROW[0:1, :], AF.Exp, sem=sdone)
            em.wait("tensor", sdone)
            with nc.psum_tensor("pREP", [128, 64], F32) as pREP:
                em.mm(pREP[:, :], ONES[0:1, :], ROWB[0:1, :], sem=tdone)
                em.wait("vector", tdone)
                em.do("vector", "tensor_copy", out=REP[:, :], in_=pREP[:, :], sem=vdone)
            for dk in range(8):
                em.do("vector", "tensor_copy", out=ALPH2[0:64, dk:dk + 1], in_=REP[0:64, 48 + 2 * dk:49 + 2 * dk])
                em.do("vector", "tensor_copy", out=ALPH2[64:128, dk:dk + 1], in_=REP[64:128, 49 + 2 * dk:50 + 2 * dk])
            for h in range(H):
                cb = (h % 2) * 512 + 64 * (h // 2)
                lgc = REP[:, 16 + h:17 + h]
                em.do("vector", "tensor_scalar", MHT[:, cb:cb + 64], DELTA[:, :], lgc, None, ALU.mult)
                em.do("vector", "tensor_scalar", GP[:, cb:cb + 64], C1P[:, :], lgc, None, ALU.mult)
                em.do("vector", "tensor_scalar", GW[:, 64 * h:64 * h + 64],
                      P63[:, 0:1].to_broadcast((128, 64)), lgc, None, ALU.mult)
            em.do("vector", "engine_nop", sem=vdone)
            em.wait("scalar", vdone)
            em.do("scalar", "activation", MHT[:, :], MHT[:, :], AF.Exp)
            em.do("scalar", "activation", GP[:, :], GP[:, :], AF.Exp)
            em.do("scalar", "activation", GW[:, :], GW[:, :], AF.Exp, sem=sdone)
            em.wait("vector", sdone)
            for h in range(H):
                cb = (h % 2) * 512 + 64 * (h // 2)
                rh = (h % 2) * 64
                em.do("vector", "tensor_tensor", MHT[:, cb:cb + 64], MHT[:, cb:cb + 64], TRIM[:, :], ALU.mult)
                em.do("vector", "tensor_scalar", GC[rh:rh + 64, 64 * (h // 2):64 * (h // 2) + 64],
                      ONES[rh:rh + 64, 0:64], REP[rh:rh + 64, 32 + h:33 + h], None, ALU.mult)
            em.do("vector", "memset", STATE[:, :], 0.0)
            em.do("vector", "engine_nop", sem=vdone)

            ld_w = em.sem("ld_w")
            em.barrier([vdone], ["sync"])
            for dk in range(8):
                em.dma("sync", WRTRAW[:, :], ap["WreadT"][128 * dk:128 * dk + 128, :], sem=ld_w)
                em.wait("vector", ld_w)
                em.do("vector", "tensor_scalar", WRT[:, 1024 * dk:1024 * dk + 1024],
                      WRTRAW[:, :], ALPH2[:, dk:dk + 1], None, ALU.mult, sem=vdone)
                if dk < 7:
                    em.wait("sync", vdone)

            ssnap = {}
            smsnap = {}
            isnap = {}
            gsnap = {}
            usnap = {}
            stsnap = {}
            for ih in (range(2) if L0SUB >= 1 else []):
                T0h = 1024 * ih
                with contextlib.ExitStack() as stq:
                    pV = stq.enter_context(nc.psum_tensor(f"pV{ih}", [128, 1024], F32))
                    pV2 = stq.enter_context(nc.psum_tensor(f"pV2{ih}", [128, 1024], F32))
                    pX = stq.enter_context(nc.psum_tensor(f"pX{ih}", [128, 1024], BF16))
                    pX2 = stq.enter_context(nc.psum_tensor(f"pX2{ih}", [128, 1024], BF16))
                    em.barrier([vdone, wld], ["tensor"])
                    vd0 = vdone.n
                    td0 = tdone.n
                    for tc in range(8):
                        pv = pV if tc % 2 == 0 else pV2
                        px = pX if tc % 2 == 0 else pX2
                        if tc >= 2:
                            em.wait("tensor", vdone, vd0 + 2 * (tc - 1))
                        t0 = T0h + 128 * tc
                        for ec in range(NCH):
                            for dvh in range(2):
                                em.mm(pv[:, 512 * dvh:512 * dvh + 512],
                                      XHAT[:, tcol(ec, t0):tcol(ec, t0) + 128],
                                      WWT[:, 1024 * ec + 512 * dvh:1024 * ec + 512 * dvh + 512],
                                      start=(ec == 0), stop=(ec == NCH - 1),
                                      sem=(tdone if (ec == NCH - 1 and dvh == 1) else None))
                        for ec in (range(NCH) if L0SUB >= 2 else []):
                            em.do("tensor", "transpose", px[:, 128 * ec:128 * ec + 128],
                                  XHAT[:, tcol(ec, t0 - 1):tcol(ec, t0 - 1) + 128], IDBF[:, :],
                                  sem=(tdone if ec == NCH - 1 else None))
                        if L0SUB >= 2:
                            em.wait("vector", tdone, td0 + 2 * tc + 1)
                            em.do("vector", "tensor_copy", out=VTOK[:, 1024 * tc:1024 * tc + 1024], in_=pv[:, :])
                            em.wait("vector", tdone, td0 + 2 * tc + 2)
                            em.do("vector", "tensor_copy", out=XPTOK[:, 1024 * tc:1024 * tc + 1024], in_=px[:, :],
                                  sem=vdone, inc=2)
                        else:
                            em.wait("vector", tdone, td0 + tc + 1)
                            em.do("vector", "tensor_copy", out=VTOK[:, 1024 * tc:1024 * tc + 1024], in_=pv[:, :],
                                  sem=vdone, inc=2)
                if L0SUB < 3:
                    continue
                with contextlib.ExitStack() as stq:
                    pS = stq.enter_context(nc.psum_tensor(f"pS{ih}", [128, 1024], F32))
                    pRi = stq.enter_context(nc.psum_tensor(f"pRi{ih}", [128, 1024], F32))
                    pRa = stq.enter_context(nc.psum_tensor(f"pRa{ih}", [128, 1024], F32))
                    pU = stq.enter_context(nc.psum_tensor(f"pU{ih}", [128, 1024], F32))
                    RSC = stq.enter_context(nc.sbuf_tensor(f"RSC{ih}", [128, 512], F32))
                    gdone = em.sem(f"gdone{ih}")
                    em.barrier([vdone], ["tensor", "gpsimd"])
                    for c in range(16):
                        gc = 16 * ih + c
                        t0 = 64 * gc
                        tcl = c // 2
                        rh = (gc % 2) * 64
                        if c >= 2:
                            em.wait("tensor", vdone, smsnap[gc - 2])
                        for h in range(H):
                            hh = (h % 2) * 64
                            cb = (h % 2) * 512 + 64 * (h // 2)
                            em.mm(pS[rh:rh + 64, cb:cb + 64],
                                  XHAT[hh:hh + 64, tcol(h // 2, t0 - 1):tcol(h // 2, t0 - 1) + 64],
                                  XHAT[hh:hh + 64, tcol(h // 2, t0):tcol(h // 2, t0) + 64],
                                  tp=(hh, rh), sem=(tdone if h == H - 1 else None))
                        ssnap[gc] = tdone.n
                        em.do("vector", "tensor_tensor", VG[rh:rh + 64, :],
                              VTOK[rh:rh + 64, 1024 * tcl:1024 * tcl + 1024], GW[rh:rh + 64, :], ALU.mult)
                        em.wait("vector", tdone, ssnap[gc])
                        em.do("vector", "tensor_tensor", SMH[rh:rh + 64, :],
                              pS[rh:rh + 64, :], MHT[rh:rh + 64, :], ALU.mult, sem=vdone)
                        smsnap[gc] = vdone.n
                        if gc >= 1:
                            em.wait("tensor", vdone, stsnap[gc - 1])
                        for h in range(H):
                            hh = (h % 2) * 64
                            cb = (h % 2) * 512 + 64 * (h // 2)
                            em.mm(pRi[hh:hh + 64, cb:cb + 64],
                                  STATE[hh:hh + 64, 64 * (h // 2):64 * (h // 2) + 64],
                                  STRM[hh:hh + 64, tcol(h // 2, t0):tcol(h // 2, t0) + 64],
                                  tp=(hh, hh), sem=(tdone if h == H - 1 else None))
                        isnap[gc] = tdone.n
                        em.wait("gpsimd", tdone, isnap[gc])
                        em.do("gpsimd", "tensor_tensor", STATE[0:64, :],
                              STATE[0:64, :], GC[0:64, :], ALU.mult)
                        em.do("gpsimd", "tensor_tensor", STATE[64:128, :],
                              STATE[64:128, :], GC[64:128, :], ALU.mult,
                              nofence=True, sem=gdone)
                        gsnap[gc] = gdone.n
                        em.wait("tensor", vdone, smsnap[gc])
                        for h in range(H):
                            hh = (h % 2) * 64
                            cb = (h % 2) * 512 + 64 * (h // 2)
                            em.mm(pRa[hh:hh + 64, cb:cb + 64],
                                  VTOK[rh:rh + 64, 1024 * tcl + 64 * h:1024 * tcl + 64 * h + 64],
                                  SMH[rh:rh + 64, cb:cb + 64],
                                  tp=(rh, hh), sem=(tdone if h == H - 1 else None))
                        for h in range(H):
                            hh = (h % 2) * 64
                            cb = (h % 2) * 512 + 64 * (h // 2)
                            em.mm(pU[hh:hh + 64, cb:cb + 64],
                                  XPTOK[rh:rh + 64, 1024 * tcl + 64 * h:1024 * tcl + 64 * h + 64],
                                  VG[rh:rh + 64, 64 * h:64 * h + 64],
                                  tp=(rh, hh), sem=(tdone if h == H - 1 else None))
                        usnap[gc] = tdone.n
                        em.wait("vector", tdone, usnap[gc])
                        tl0 = 64 * c
                        for par in range(2):
                            pr = 64 * par
                            cbase = 512 * par
                            em.do("vector", "tensor_tensor", RSC[pr:pr + 64, :],
                                  pRi[pr:pr + 64, cbase:cbase + 512], GP[pr:pr + 64, cbase:cbase + 512],
                                  ALU.mult, nofence=(par == 1))
                        for par in range(2):
                            pr = 64 * par
                            cbase = 512 * par
                            em.do("vector", "tensor_tensor",
                                  READS[pr:pr + 64, :].rearrange("p (dk n) -> p dk n", n=1024)[:, :, tl0:tl0 + 64],
                                  RSC[pr:pr + 64, :].rearrange("p (dk n) -> p dk n", n=64),
                                  pRa[pr:pr + 64, cbase:cbase + 512].rearrange("p (dk n) -> p dk n", n=64),
                                  ALU.add, nofence=(par == 1))
                        em.wait("vector", gdone, gsnap[gc])
                        em.do("vector", "tensor_tensor", STATE[0:64, :],
                              STATE[0:64, :], pU[0:64, 0:512], ALU.add)
                        em.do("vector", "tensor_tensor", STATE[64:128, :],
                              STATE[64:128, :], pU[64:128, 512:1024], ALU.add,
                              nofence=True, sem=vdone)
                        stsnap[gc] = vdone.n
                if dev and ih == 1:
                    em.barrier([vdone], ["gpsimd", "sync"])
                    em.dma("gpsimd", dbg["d_vtok"][:, :], VTOK[:, :], sem=gdump)
                    em.dma("gpsimd", dbg["d_xptok"][:, :], XPTOK[:, :], sem=gdump)
                    em.dma("gpsimd", dbg["d_reads"][:, :], READS[:, :], sem=gdump)
                    em.dma("gpsimd", dbg["d_smh"][:, :], SMH[:, :], sem=gdump)
                    em.dma("sync", dbg["d_state"][:, :], STATE[:, :], sem=stdone)
                    em.dma("sync", dbg["d_mht"][:, :], MHT[:, :], sem=stdone)
                    em.dma("sync", dbg["d_rep"][:, :], REP[:, :], sem=stdone)
                    em.dma("sync", dbg["d_rowb"][:, :], ROWB[:, :], sem=stdone)
                    em.dma("sync", dbg["d_gp"][:, :], GP[:, :], sem=stdone)
                    em.dma("gpsimd", dbg["d_gw"][:, :], GW[:, :], sem=gdump)
                    em.dma("sync", dbg["d_gc"][:, :], GC[:, :], sem=stdone)
                    em.barrier([stdone, gdump], ["vector", "tensor"])
                if L0SUB < 4:
                    continue
                with contextlib.ExitStack() as stq:
                    pW = stq.enter_context(nc.psum_tensor(f"pW{ih}", [128, 512], F32))
                    pW2 = stq.enter_context(nc.psum_tensor(f"pW2{ih}", [128, 512], F32))
                    em.barrier([vdone], ["tensor"])
                    vd0 = vdone.n
                    ii = 0
                    for ecO in range(NCH):
                        for tq in range(2):
                            pw = pW if ii % 2 == 0 else pW2
                            if ii >= 2:
                                em.wait("tensor", vdone, vd0 + ii - 1)
                            for dk in range(8):
                                em.mm(pw[:, :], WRT[:, 1024 * dk + 128 * ecO:1024 * dk + 128 * ecO + 128],
                                      READS[:, 1024 * dk + 512 * tq:1024 * dk + 512 * tq + 512],
                                      start=(dk == 0), stop=(dk == 7),
                                      sem=(tdone if dk == 7 else None))
                            em.wait("vector", tdone)
                            sl = STRM[:, tcol(ecO, T0h + 512 * tq):tcol(ecO, T0h + 512 * tq) + 512]
                            em.do("vector", "scalar_tensor_tensor", sl, sl, cc(2.0), pw[:, :],
                                  ALU.mult, ALU.add, sem=vdone)
                            ii += 1

        if STAGE_LIMIT >= 3:
         with contextlib.ExitStack() as stp:
            sbp = lambda n, s, d=F32: stp.enter_context(nc.sbuf_tensor(n, list(s), d))
            emit_ln("l1")

        if dev:
            em.barrier([vdone], ["sync"])
            em.dma("sync", dbg["d_strm1"][:, :], STRM[:, :], sem=stdone)
            em.barrier([stdone], ["vector", "tensor", "scalar"])

        if STAGE_LIMIT >= 4:
         with contextlib.ExitStack() as stp:
            sbp = lambda n, s, d=F32: stp.enter_context(nc.sbuf_tensor(n, list(s), d))
            LN0G = sbp("LN0G", (128, NCH))
            LN0B = sbp("LN0B", (128, NCH))
            B1 = sbp("B1", (128, FF // 128))
            HB2 = sbp("HB2", (128, NCH))
            with contextlib.ExitStack() as stq:

                ldb = em.sem("ldb")
                em.barrier([tdone, vdone, sdone], ["sync"])
                em.dma("sync", LN0G[:, :], ap["ln0g"][:, :], sem=ldb)
                em.dma("sync", LN0B[:, :], ap["ln0b"][:, :], sem=ldb)
                em.dma("sync", B1[:, :], ap["b1"][:, :], sem=ldb)
                em.dma("sync", HB2[:, :], ap["b2"][:, :], sem=ldb)
                em.barrier([ldb], ["vector", "scalar"])
                em.do("vector", "tensor_scalar", HB2[:, :], HB2[:, :], cc(0.5), None, ALU.mult)
                emit_ln("lf", affine=(LN0G, LN0B))

            with contextlib.ExitStack() as stq:
                sbq = lambda n, s, d=F32: stq.enter_context(nc.sbuf_tensor(n, list(s), d))
                W1U = sbq("W1U", (128, 2 * FF), U8)
                SC1 = sbq("SC1", (128, FF), BF16)
                W1B = sbq("W1B", (128, 2 * FF), BF16)
                HSTG = sbq("HSTG", (128, 2 * 4 * T), BF16)
                W2U = sbq("W2U", (128, 4 * 1024), U8)
                SC2 = sbq("SC2", (128, 4 * 1024), BF16)
                W2T = sbq("W2T", (128, 4 * 1024), BF16)
                ph1 = stq.enter_context(nc.psum_tensor("ph1", [128, 512], F32))
                ph2 = stq.enter_context(nc.psum_tensor("ph2", [128, 512], F32))
                lw1 = em.sem("lw1")
                lc1 = em.sem("lc1")
                lw2 = em.sem("lw2")
                dq1 = em.sem("dq1")
                hout = em.sem("hout")
                w2st = em.sem("w2st")
                w2scr = nc.dram_tensor("w2scr", [FF, D], BF16, kind="Internal").ap()

                em.barrier([vdone, sdone], ["sync", "gpsimd"])
                em.barrier([vdone], ["scalar"])
                warm(AF.Silu)

                w2cpy = [0]

                def load_w2(r):
                    em.wait("sync", sdone, w2cpy[0])
                    em.wait("gpsimd", dq1)
                    for j in range(4):
                        fc = 4 * r + j
                        em.dma("sync", W2U[:, 1024 * j:1024 * j + 1024],
                               ap["fc2q"][128 * fc:128 * fc + 128, :], sem=lw2)
                        em.dma("gpsimd", SC2[:, 1024 * j:1024 * j + 1024],
                               ap["fc2s"][128 * fc:128 * fc + 128, :], sem=lc1)

                def w2_round(r):
                    if r == 0:
                        load_w2(0)
                    em.wait("scalar", lw2, 16 * 4 * (r + 1))
                    em.wait("scalar", lc1, lc1.n)
                    em.wait("scalar", w2st, w2st.n)
                    em.do("scalar", "activation", W2T[:, :], W2U[:, :], AF.Copy,
                          bias=-128.0, scale=1.0, sem=sdone)
                    w2cpy[0] = sdone.n
                    em.wait("vector", sdone)
                    em.do("vector", "tensor_scalar", SC2[:, :], SC2[:, :], cc(0.5), None,
                          ALU.mult)
                    em.do("vector", "tensor_tensor", W2T[:, :], W2T[:, :], SC2[:, :],
                          ALU.mult, sem=dq1)
                    if r + 1 < 8:
                        load_w2(r + 1)
                    em.wait("sync", dq1)
                    for j in range(4):
                        fc = 4 * r + j
                        em.dma("sync", w2scr[128 * fc:128 * fc + 128, :],
                               W2T[:, 1024 * j:1024 * j + 1024], sem=w2st)

                def load_fc1(fg, cpy_snap):
                    rg = fg % 2
                    em.wait("sync", sdone, cpy_snap)
                    em.wait("gpsimd", dq1)
                    for ec in range(NCH):
                        em.dma("sync", W1U[:, FF * rg + 512 * ec:FF * rg + 512 * ec + 512],
                               ap["fc1q"][fg, 128 * ec:128 * ec + 128, :], sem=lw1)
                        em.dma("gpsimd", SC1[:, 512 * ec:512 * ec + 512],
                               ap["fc1s"][fg, 128 * ec:128 * ec + 128, :], sem=lc1)

                mm_snap = {}
                h_snap = {}
                dq_s = {}

                def dq_fc1(fg):
                    rg = fg % 2
                    em.wait("scalar", lw1, 16 * NCH * (fg + 1))
                    if fg >= 2:
                        em.wait("scalar", tdone, mm_snap[fg - 2])
                    em.do("scalar", "activation", W1B[:, FF * rg:FF * rg + FF],
                          W1U[:, FF * rg:FF * rg + FF], AF.Copy, bias=-128.0, scale=1.0,
                          sem=sdone)
                    cpy_snap = sdone.n
                    em.wait("vector", lc1, lc1.n)
                    em.wait("vector", sdone, cpy_snap)
                    em.do("vector", "tensor_tensor", W1B[:, FF * rg:FF * rg + FF],
                          W1B[:, FF * rg:FF * rg + FF], SC1[:, :],
                          ALU.mult, sem=dq1)
                    dq_s[fg] = dq1.n
                    if fg + 1 < 8:
                        load_fc1(fg + 1, cpy_snap)

                load_fc1(0, 0)
                dq_fc1(0)
                sil_s = {}
                for fg in range(8):
                    rg = fg % 2
                    em.wait("tensor", dq1, dq_s[fg])
                    if fg >= 1:
                        em.wait("tensor", sdone, sil_s[fg - 1] - 1)
                    for fc in range(4):
                        for tq in range(4):
                            ph = ph1 if (fc * 4 + tq) % 2 == 0 else ph2
                            if fc or tq:
                                em.wait("tensor", sdone, sdone.n - 1)
                            for ec in range(NCH):
                                em.mm(ph[:, :],
                                      W1B[:, FF * rg + 512 * ec + 128 * fc:FF * rg + 512 * ec + 128 * fc + 128],
                                      XHAT[:, tcol(ec, 512 * tq):tcol(ec, 512 * tq) + 512],
                                      start=(ec == 0), stop=(ec == NCH - 1),
                                      sem=(tdone if ec == NCH - 1 else None))
                            em.wait("scalar", tdone)
                            if fc == 0 and tq == 0 and fg >= 2:
                                em.wait("scalar", hout, h_snap[fg - 2])
                            em.do("scalar", "activation",
                                  HSTG[:, 8192 * rg + 2048 * fc + 512 * tq:8192 * rg + 2048 * fc + 512 * tq + 512],
                                  ph[:, :], AF.Silu, bias=B1[:, 4 * fg + fc:4 * fg + fc + 1],
                                  scale=1.0, sem=sdone)
                    mm_snap[fg] = tdone.n
                    silu_snap = sil_s[fg] = sdone.n
                    if fg + 1 < 8:
                        dq_fc1(fg + 1)
                    em.wait("sync", sdone, silu_snap)
                    for fc in range(4):
                        em.dma("sync", hscr[512 * fg + 128 * fc:512 * fg + 128 * fc + 128, :],
                               HSTG[:, 8192 * rg + 2048 * fc:8192 * rg + 2048 * fc + 2048],
                               sem=hout)
                    h_snap[fg] = hout.n
                    w2_round(fg)
                em.barrier([hout, w2st, dq1, sdone], ["sync", "vector", "scalar", "tensor"])

            with contextlib.ExitStack() as stq:
                sbq = lambda n, s, d=F32: stq.enter_context(nc.sbuf_tensor(n, list(s), d))
                HBUF = sbq("HBUF", (128, 32 * 512), BF16)
                W2F = sbq("W2F", (128, 32 * 1024), BF16)
                pz1 = stq.enter_context(nc.psum_tensor("pz1", [128, 512], F32))
                pz2 = stq.enter_context(nc.psum_tensor("pz2", [128, 512], F32))
                lwg = [em.sem("lwg%d" % g) for g in range(4)]
                lhg = [em.sem("lhg%d" % g) for g in range(4)]
                em.barrier([sdone], ["scalar"])
                warm(AF.Exp)
                for g in range(4):
                    for fc in range(8 * g, 8 * g + 8):
                        em.dma("sync", W2F[:, 1024 * fc:1024 * fc + 1024],
                               w2scr[128 * fc:128 * fc + 128, :], sem=lwg[g])
                for tq in range(4):
                    if tq >= 1:
                        em.wait("sync", tdone)
                    hb_base = [s.n for s in lhg]
                    for g in range(4):
                        em.dma("sync",
                               HBUF[:, 4096 * g:4096 * g + 4096].rearrange(
                                   "p (c n) -> p c n", n=512),
                               hscr[1024 * g:1024 * g + 1024,
                                    512 * tq:512 * tq + 512].rearrange(
                                   "(c p) n -> p c n", p=128),
                               sem=lhg[g])
                    for ec in range(NCH):
                        pz = pz1 if ec % 2 == 0 else pz2
                        if ec >= 2:
                            em.wait("tensor", vdone, vdone.n - 1)
                        for fc in range(32):
                            g = fc // 8
                            if fc % 8 == 0:
                                if tq == 0 and ec == 0:
                                    em.wait("tensor", lwg[g], 8 * 16)
                                if ec == 0:
                                    em.wait("tensor", lhg[g], hb_base[g] + 16)
                            em.mm(pz[:, :],
                                  W2F[:, 1024 * fc + 128 * ec:1024 * fc + 128 * ec + 128],
                                  HBUF[:, 512 * fc:512 * fc + 512],
                                  start=(fc == 0), stop=(fc == 31),
                                  sem=(tdone if fc == 31 else None))
                        em.wait("vector", tdone)
                        sl = STRM[:, tcol(ec, 512 * tq):tcol(ec, 512 * tq) + 512]
                        em.do("vector", "scalar_tensor_tensor", sl, pz[:, :],
                              HB2[:, ec:ec + 1], sl, ALU.add, ALU.add, sem=vdone)
                em.barrier([vdone], ["tensor", "scalar"])

        if STAGE_LIMIT >= 5:
         with contextlib.ExitStack() as stp:
            emit_ln("l2")

        if dev:
            em.barrier([vdone], ["sync"])
            em.dma("sync", dbg["d_strm2"][:, :], STRM[:, :], sem=stdone)
            em.barrier([stdone], ["vector", "tensor", "scalar"])

        if STAGE_LIMIT >= 6:
         with contextlib.ExitStack() as stp:
            sbp = lambda n, s, d=F32: stp.enter_context(nc.sbuf_tensor(n, list(s), d))
            with contextlib.ExitStack() as pre:
                sbr = lambda n, s, d=F32: pre.enter_context(nc.sbuf_tensor(n, list(s), d))
                WDT = sbr("WDT", (128, 128))
                BDR = sbr("BDR", (1, H))
                MEANV = sbr("MEANV", (128, NCH))
                SR = sbr("SR", (1, 64))
                ppd = pre.enter_context(nc.psum_tensor("ppd", [1, 64], F32))
                prep = pre.enter_context(nc.psum_tensor("prep", [128, 32], F32))
                lpd = em.sem("lpd")
                em.barrier([tdone, vdone, sdone], ["sync"])
                em.dma("sync", WDT[:, 0:H * NCH].rearrange("p (c n) -> p c n", n=H),
                       ap["wdT"][:, :].rearrange("(c p) n -> p c n", p=128), sem=lpd)
                em.dma("sync", BDR[:, :], ap["bd_row"][:, :], sem=lpd)
                em.barrier([lpd, vdone], ["vector"])
                for k in range(NCH):
                    em.do("vector", "reduce_sum", MEANV[:, k:k + 1],
                          STRM[:, tcol(k):tcol(k) + T], mybir.AxisListType.X)
                em.do("vector", "engine_nop", sem=vdone)
                em.barrier([vdone], ["tensor"])
                for k in range(NCH):
                    em.mm(ppd[0:1, 0:H], MEANV[:, k:k + 1], WDT[:, H * k:H * k + H],
                          start=(k == 0), stop=(k == NCH - 1),
                          sem=(tdone if k == NCH - 1 else None))
                em.wait("vector", tdone)
                em.do("vector", "tensor_scalar", SR[0:1, 0:H], ppd[0:1, 0:H],
                      CONST[0:1, CIDX[1.0 / T]:CIDX[1.0 / T] + 1], None, ALU.mult)
                em.do("vector", "tensor_tensor", SR[0:1, 0:H], SR[0:1, 0:H], BDR[0:1, :],
                      ALU.add, sem=vdone)
                em.wait("scalar", vdone)
                em.do("scalar", "activation", SR[0:1, 16:32], SR[0:1, 0:16], AF.Exp,
                      scale=-1.0, sem=sdone)
                em.wait("vector", sdone)
                em.do("vector", "tensor_scalar", SR[0:1, 16:32], SR[0:1, 16:32],
                      CONST[0:1, CIDX[1.0]:CIDX[1.0] + 1], None, ALU.add)
                em.do("vector", "reciprocal", out=SR[0:1, 32:48], in_=SR[0:1, 16:32], sem=vdone)
                em.wait("scalar", vdone)
                em.do("scalar", "activation", SR[0:1, 16:32], SR[0:1, 32:48], AF.Exp,
                      scale=float(-8.0 * np.log(2.0)),
                      bias=CONST[0:1, CIDX[-5.0 * np.log(2.0)]:CIDX[-5.0 * np.log(2.0)] + 1],
                      sem=sdone)
                em.wait("vector", sdone)
                em.do("vector", "tensor_scalar", SR[0:1, 16:32], SR[0:1, 16:32],
                      cc1(-1.0), cc1(1.0), ALU.mult, ALU.add, sem=vdone)
                em.wait("scalar", vdone)
                em.do("scalar", "activation", SR[0:1, 0:16], SR[0:1, 16:32], AF.Ln, sem=sdone)
                em.wait("vector", sdone)
                em.do("vector", "tensor_scalar", SR[0:1, 16:32], SR[0:1, 0:16],
                      cc1(-1.0), None, ALU.mult, sem=vdone)
                em.wait("tensor", vdone)
                em.mm(prep[:, :], ONES[0:1, :], SR[0:1, 0:32], sem=tdone)
                em.wait("vector", tdone)
                em.do("vector", "tensor_copy", out=LDS[:, :], in_=prep[:, :], sem=vdone)
                em.barrier([vdone, tdone, sdone], ["sync"])
                em.dma("sync", strmscr[:, :], STRM[:, :], sem=stdone)
                em.barrier([stdone], ["vector", "tensor", "scalar", "gpsimd"])
            STRMSTK[0].close()

            GON = sbp("GON", (128, NCH * T), BF16)
            GAMT = sbp("GAMT", (128, H // 2))
            BETT = sbp("BETT", (128, H // 2))
            BQ = sbp("BQl", (128, NCH))
            BK = sbp("BKl", (128, NCH))
            BO = sbp("BOl", (128, NCH))
            BVREP = sbp("BVREP", (128, D), BF16)
            BVR = sbp("BVR", (1, D))
            IOTAR = sbp("IOTAR", (1, T))
            MPATS = sbp("MPATS", (128, 1024), BF16)
            COSB = sbp("COSB", (128, T), BF16)
            SINF = sbp("SINF", (128, T), BF16)
            BVALL = sbp("BVALL", (128, H * 16))
            IOT2 = sbp("IOT2", (128, 16))
            la = em.sem("la")
            lg = em.sem("lg")
            for nm, buf in [("gamT", GAMT), ("betT", BETT), ("bq", BQ), ("bk", BK),
                            ("bo", BO), ("bv_row", BVR), ("c_IOTAROW", IOTAR),
                            ("c_IOTA2D", IOT2)]:
                em.dma("sync", buf[:, :], ap[nm][:, :], sem=la)
            em.dma("gpsimd", SINF[:, :], ap["c_SINQ"][:, :], sem=lg)
            em.dma("gpsimd", MPATS[:, :], ap["c_MPAT"][:, :], sem=lg)
            em.dma("gpsimd", COSB[:, :], ap["c_COSQ"][:, :], sem=lg)
            em.barrier([la, lg], ["vector"])
            for h in range(H):
                em.do("vector", "tensor_scalar", BVALL[:, 16 * h:16 * h + 16],
                      IOT2[:, :], LDS[:, h:h + 1], None, ALU.mult)
            em.do("vector", "engine_nop", sem=vdone)

            with contextlib.ExitStack() as stg:
                WGB = stg.enter_context(nc.sbuf_tensor("WGB", [128, NCH * 1024], BF16))
                BG = stg.enter_context(nc.sbuf_tensor("BGl", [128, NCH], F32))
                pg1 = stg.enter_context(nc.psum_tensor("pg1", [128, 512], F32))
                pg2 = stg.enter_context(nc.psum_tensor("pg2", [128, 512], F32))
                lw = em.sem("lw")
                em.dma("sync", BG[:, :], ap["bg"][:, :], sem=la)
                for dk in range(NCH):
                    em.dma("gpsimd", WGB[:, 1024 * dk:1024 * dk + 1024],
                           ap["wgT"][128 * dk:128 * dk + 128, :], sem=lw)
                em.barrier([la, lw], ["tensor", "scalar"])
                warm(AF.Silu)
                for eo in range(NCH):
                    for tq in range(4):
                        pg = pg1 if (eo * 4 + tq) % 2 == 0 else pg2
                        if eo or tq:
                            em.wait("tensor", sdone, sdone.n - 1)
                        for ei in range(NCH):
                            em.mm(pg[:, :], WGB[:, 1024 * ei + 128 * eo:1024 * ei + 128 * eo + 128],
                                  XHAT[:, tcol(ei, 512 * tq):tcol(ei, 512 * tq) + 512],
                                  start=(ei == 0), stop=(ei == NCH - 1),
                                  sem=(tdone if ei == NCH - 1 else None))
                        em.wait("scalar", tdone)
                        em.do("scalar", "activation", GON[:, 2048 * eo + 512 * tq:2048 * eo + 512 * tq + 512],
                              pg[:, :], AF.Silu, bias=BG[:, eo:eo + 1], scale=1.0, sem=sdone)
                em.barrier([sdone], ["tensor", "vector", "scalar", "gpsimd"])
                warm(AF.Exp)

            with contextlib.ExitStack() as stb:
                WQB = stb.enter_context(nc.sbuf_tensor("WQB", [128, NCH * 1024], BF16))
                WKB = stb.enter_context(nc.sbuf_tensor("WKB", [128, NCH * 1024], BF16))
                WVB = stb.enter_context(nc.sbuf_tensor("WVB", [128, NCH * 1024], BF16))
                lw2a = em.sem("lw2a")
                for nm, buf in [("wqT", WQB), ("wkT", WKB), ("wvT", WVB)]:
                    for dk in range(NCH):
                        em.dma("gpsimd", buf[:, 1024 * dk:1024 * dk + 1024],
                               ap[nm][128 * dk:128 * dk + 128, :], sem=lw2a)

                with contextlib.ExitStack() as stc:
                    sbc = lambda n, s, d=F32: stc.enter_context(nc.sbuf_tensor(n, list(s), d))
                    VH = sbc("VH", (128, 16 * 1040), BF16)
                    QF = sbc("QF", (128, T), BF16)
                    KF = sbc("KF", (128, T), BF16)
                    QRAW = sbc("QRAW", (128, T), BF16)
                    T2S = sbc("T2S", (128, T), BF16)
                    PRB = sbc("PRB", (128, 3 * 1024), BF16)
                    MH = sbc("MH", (128, 1024), BF16)
                    OG = sbc("OG", (128, T), BF16)
                    CMB = sbc("CMB", (65, 1024))
                    FREP = sbc("FREP", (65, 1024), BF16)
                    ACC = sbc("ACC", (128, 4))
                    REPM = sbc("REPM", (128, 2))
                    SROW = sbc("SROW", (1, 8))
                    pA = stc.enter_context(nc.psum_tensor("pA", [128, 1024], F32))
                    pB = stc.enter_context(nc.psum_tensor("pB", [128, 1024], F32))
                    pP = stc.enter_context(nc.psum_tensor("pP", [65, 1024], F32))
                    pF = stc.enter_context(nc.psum_tensor("pF", [65, 1024], F32))

                    em.barrier([la, vdone], ["tensor"])
                    em.mm(pA[:, 0:512], ONES[0:1, :], BVR[0:1, 0:512])
                    em.mm(pA[:, 512:1024], ONES[0:1, :], BVR[0:1, 512:1024], sem=tdone)
                    em.wait("vector", tdone)
                    em.do("vector", "tensor_copy", out=BVREP[:, :], in_=pA[:, :])
                    em.do("vector", "memset",
                          VH[:, :].rearrange("p (tc h c) -> p tc h c", h=16, c=65)[:, :, :, 64:65], 1.0)
                    em.do("vector", "engine_nop", sem=vdone)
                    em.barrier([lw2a, vdone], ["tensor"])
                    vvd0 = vdone.n
                    for tc in range(16):
                        pv = pA if tc % 2 == 0 else pB
                        if tc >= 2:
                            em.wait("tensor", vdone, vvd0 + tc - 1)
                        for dvh in range(2):
                            for ec in range(NCH):
                                em.mm(pv[:, 512 * dvh:512 * dvh + 512],
                                      XHAT[:, tcol(ec, 128 * tc):tcol(ec, 128 * tc) + 128],
                                      WVB[:, 1024 * ec + 512 * dvh:1024 * ec + 512 * dvh + 512],
                                      start=(ec == 0), stop=(ec == NCH - 1),
                                      sem=(tdone if (ec == NCH - 1 and dvh == 1) else None))
                        em.wait("vector", tdone)
                        em.do("vector", "tensor_tensor",
                              VH[:, 1040 * tc:1040 * tc + 1040].rearrange("p (h c) -> p h c", c=65)[:, :, 0:64],
                              pv[:, :].rearrange("p (h c) -> p h c", c=64),
                              BVREP[:, :].rearrange("p (h c) -> p h c", c=64),
                              ALU.add, sem=vdone)

                    for hp in range(8):
                        for (WB, BB, OUT) in ((WQB, BQ, QF), (WKB, BK, KF)):
                            em.barrier([vdone, sdone], ["tensor"])
                            for qt in range(4):
                                pq = pA if qt % 2 == 0 else pB
                                psl = pq[:, 0:512]
                                if qt >= 2:
                                    em.wait("tensor", vdone, vdone.n - 1)
                                for ec in range(NCH):
                                    em.mm(psl, WB[:, 1024 * ec + 128 * hp:1024 * ec + 128 * hp + 128],
                                          XHAT[:, tcol(ec, 512 * qt):tcol(ec, 512 * qt) + 512],
                                          start=(ec == 0), stop=(ec == NCH - 1),
                                          sem=(tdone if ec == NCH - 1 else None))
                                em.wait("vector", tdone)
                                em.do("vector", "tensor_scalar", QRAW[:, 512 * qt:512 * qt + 512],
                                      psl, BB[:, hp:hp + 1], None, ALU.add, sem=vdone)
                            em.barrier([vdone], ["tensor"])
                            em.do("vector", "tensor_tensor", OUT[:, :], QRAW[:, :], COSB[:, :],
                                  ALU.mult)
                            for qt in range(4):
                                pq = pA if qt % 2 == 0 else pB
                                if qt >= 2:
                                    em.wait("tensor", vdone, vdone.n - 1)
                                em.mm(pq[:, 0:512], R2T[:, :], QRAW[:, 512 * qt:512 * qt + 512],
                                      sem=tdone)
                                em.wait("vector", tdone)
                                em.do("vector", "tensor_tensor", T2S[:, 512 * qt:512 * qt + 512],
                                      pq[:, 0:512], SINF[:, 512 * qt:512 * qt + 512], ALU.mult,
                                      sem=vdone)
                            em.do("vector", "tensor_tensor", OUT[:, :], OUT[:, :], T2S[:, :],
                                  ALU.add, sem=vdone)
                        vsnap_kf = vdone.n
                        for hi in range(2):
                            h = 2 * hp + hi
                            hh = 64 * hi
                            em.barrier([vdone], ["scalar"])
                            em.do("scalar", "activation", MH[:, :], MPATS[:, :], AF.Exp,
                                  scale=LDS[:, h:h + 1], sem=sdone)
                            for qh in range(2):
                                q0 = 1024 * qh
                                first_fut = 8 * qh
                                ssnap = {}
                                avsnap = {}
                                esnap = {}
                                vsnap0 = vdone.n

                                def emit_scores(kt):
                                    pS = pA if kt % 2 == 0 else pB
                                    if kt == 0 and hi == 0 and qh == 0:
                                        em.wait("tensor", vdone, vsnap_kf)
                                    if kt == 1:
                                        em.wait("tensor", vdone, vsnap0)
                                    if kt >= 2:
                                        em.wait("tensor", sdone, esnap[kt - 2])
                                    for j in range(2):
                                        em.mm(pS[:, 512 * j:512 * j + 512],
                                              KF[hh:hh + 64, 128 * kt:128 * kt + 128],
                                              QF[hh:hh + 64, q0 + 512 * j:q0 + 512 * j + 512],
                                              sem=(tdone if j == 1 else None))
                                    ssnap[kt] = tdone.n

                                emit_scores(0)
                                for kt in range(16):
                                    pS = pA if kt % 2 == 0 else pB
                                    r = kt % 3
                                    if kt >= 3:
                                        em.wait("scalar", tdone, avsnap[kt - 3])
                                    em.wait("scalar", tdone, ssnap[kt])
                                    if kt >= first_fut:
                                        em.do("scalar", "activation", PRB[:, 1024 * r:1024 * r + 1024],
                                              pS[:, :], AF.Exp, bias=BVALL[:, 16 * h + kt:16 * h + kt + 1],
                                              scale=0.125, sem=sdone)
                                    else:
                                        em.do("scalar", "activation", PRB[:, 1024 * r:1024 * r + 1024],
                                              pS[:, :], AF.Exp, scale=0.125, sem=sdone)
                                    esnap[kt] = sdone.n
                                    off = 128 * kt - q0
                                    diag = 0 <= off < 1024
                                    if diag:
                                        em.wait("vector", sdone, esnap[kt])
                                        em.do("vector", "tensor_tensor",
                                              PRB[:, 1024 * r + off:1024 * r + 1024],
                                              PRB[:, 1024 * r + off:1024 * r + 1024],
                                              MH[:, 0:1024 - off], ALU.mult, sem=vdone)
                                    if kt + 1 < 16:
                                        emit_scores(kt + 1)
                                    if diag:
                                        em.wait("tensor", vdone)
                                    else:
                                        em.wait("tensor", sdone, esnap[kt])
                                    pT = pF if kt >= first_fut else pP
                                    st_flag = (kt == first_fut) or (qh == 1 and kt == 0)
                                    for j in range(2):
                                        em.mm(pT[:, 512 * j:512 * j + 512],
                                              VH[:, 1040 * kt + 65 * h:1040 * kt + 65 * h + 65],
                                              PRB[:, 1024 * r + 512 * j:1024 * r + 512 * j + 512],
                                              start=st_flag, stop=(kt == 15 or (kt == first_fut - 1)),
                                              sem=(tdone if j == 1 else None))
                                    avsnap[kt] = tdone.n
                                if qh == 0:
                                    em.wait("scalar", tdone)
                                    em.do("scalar", "activation", CMB[64:65, :], pF[64:65, :],
                                          AF.Ln, sem=sdone)
                                    em.wait("tensor", sdone)
                                    em.mm(pB[0:64, 0:512], ONES[64:65, 0:64], CMB[64:65, 0:512])
                                    em.mm(pB[0:64, 512:1024], ONES[64:65, 0:64],
                                          CMB[64:65, 512:1024], sem=tdone)
                                    em.wait("scalar", tdone)
                                    em.do("scalar", "activation", FREP[0:64, :], pB[0:64, :],
                                          AF.Exp, scale=-1.0, sem=sdone)
                                    em.wait("vector", sdone)
                                    em.do("vector", "tensor_tensor",
                                          OG[64 * hi:64 * hi + 64, q0:q0 + 1024],
                                          pF[0:64, :], FREP[0:64, :], ALU.mult, sem=vdone)
                                else:
                                    em.mm(pA[0:65, 0:512], ONES[0:1, 0:65], IOTAR[0:1, q0:q0 + 512])
                                    em.mm(pA[0:65, 512:1024], ONES[0:1, 0:65],
                                          IOTAR[0:1, q0 + 512:q0 + 1024], sem=tdone)
                                    em.wait("scalar", tdone)
                                    em.do("scalar", "activation", FREP[:, :], pA[0:65, :], AF.Exp,
                                          scale=LDS[0:65, 16 + h:17 + h], sem=sdone)
                                    em.wait("vector", sdone)
                                    em.do("vector", "tensor_tensor", CMB[:, :], pF[:, :], FREP[:, :],
                                          ALU.mult)
                                    em.do("vector", "tensor_tensor", CMB[:, :], CMB[:, :], pP[:, :],
                                          ALU.add, sem=vdone)
                                    em.wait("scalar", vdone)
                                    em.do("scalar", "activation", CMB[64:65, :], CMB[64:65, :],
                                          AF.Ln, sem=sdone)
                                    em.wait("tensor", sdone)
                                    em.mm(pB[0:64, 0:512], ONES[64:65, 0:64], CMB[64:65, 0:512])
                                    em.mm(pB[0:64, 512:1024], ONES[64:65, 0:64],
                                          CMB[64:65, 512:1024], sem=tdone)
                                    em.wait("scalar", tdone)
                                    em.do("scalar", "activation", FREP[0:64, :], pB[0:64, :],
                                          AF.Exp, scale=-1.0, sem=sdone)
                                    em.wait("vector", sdone)
                                    em.do("vector", "tensor_tensor",
                                          OG[64 * hi:64 * hi + 64, q0:q0 + 1024],
                                          CMB[0:64, :], FREP[0:64, :], ALU.mult, sem=vdone)
                            if hi == 1:
                                em.barrier([vdone], ["scalar"])
                                ONF = QRAW[:, :]
                                em.do("scalar", "activation", ONF, OG[:, :], AF.Square,
                                      accum_out=ACC[:, 0:1])
                                em.do("scalar", "activation", ONF, OG[:, :], AF.Copy,
                                      accum_out=ACC[:, 1:2], sem=sdone)
                                em.wait("tensor", sdone)
                                em.mm(pA[0:64, 0:2], ONES[0:64, 0:64], ACC[0:64, 0:2],
                                      tp=(0, 0))
                                em.mm(pA[64:128, 512:514], ONES[64:128, 0:64],
                                      ACC[64:128, 0:2], tp=(64, 64), sem=tdone)
                                em.wait("vector", tdone)
                                em.do("vector", "tensor_scalar", REPM[0:64, 0:2], pA[0:64, 0:2],
                                      CONST[0:64, CIDX[1.0 / (T * HD)]:CIDX[1.0 / (T * HD)] + 1],
                                      None, ALU.mult)
                                em.do("vector", "tensor_scalar", REPM[64:128, 0:2],
                                      pA[64:128, 512:514],
                                      CONST[64:128, CIDX[1.0 / (T * HD)]:CIDX[1.0 / (T * HD)] + 1],
                                      None, ALU.mult, nofence=True)
                                em.do("vector", "tensor_tensor", ACC[:, 2:3], REPM[:, 1:2],
                                      REPM[:, 1:2], ALU.mult)
                                em.do("vector", "tensor_sub", ACC[:, 2:3], REPM[:, 0:1],
                                      ACC[:, 2:3], sem=vdone)
                                em.wait("scalar", vdone)
                                em.do("scalar", "activation", ACC[:, 3:4], ACC[:, 2:3], AF.Ln,
                                      bias=cc(1e-5), scale=1.0)
                                em.do("scalar", "activation", ACC[:, 3:4], ACC[:, 3:4], AF.Exp,
                                      scale=-0.5, sem=sdone)
                                em.wait("vector", sdone)
                                em.do("vector", "tensor_tensor", ACC[:, 2:3], ACC[:, 3:4],
                                      GAMT[:, hp:hp + 1], ALU.mult)
                                em.do("vector", "tensor_tensor", ACC[:, 3:4], REPM[:, 1:2],
                                      ACC[:, 2:3], ALU.mult)
                                em.do("vector", "tensor_sub", ACC[:, 3:4], BETT[:, hp:hp + 1],
                                      ACC[:, 3:4])
                                em.do("vector", "tensor_scalar", ONF, OG[:, :],
                                      ACC[:, 2:3], ACC[:, 3:4], ALU.mult, ALU.add, sem=vdone)
                                em.do("vector", "tensor_tensor",
                                      GON[:, 2048 * hp:2048 * hp + 2048],
                                      GON[:, 2048 * hp:2048 * hp + 2048], ONF, ALU.mult,
                                      sem=vdone)
                pass
            with contextlib.ExitStack() as stw:
                WOB = stw.enter_context(nc.sbuf_tensor("WOB", [128, NCH * 1024], BF16))
                DSTG = stw.enter_context(nc.sbuf_tensor("DSTG", [128, 2 * T], F32))
                pw1 = stw.enter_context(nc.psum_tensor("pwo1", [128, 512], F32))
                pw2 = stw.enter_context(nc.psum_tensor("pwo2", [128, 512], F32))
                acst = em.sem("acst")
                em.barrier([vdone, tdone, sdone], ["gpsimd"])
                for dk in range(NCH):
                    em.dma("gpsimd", WOB[:, 1024 * dk:1024 * dk + 1024],
                           ap["woT"][128 * dk:128 * dk + 128, :], sem=lg)
                em.barrier([lg], ["tensor"])
                for eo in range(NCH):
                    rg = eo % 2
                    if eo >= 2:
                        em.wait("tensor", acst)
                    for tq in range(4):
                        pw = pw1 if (eo * 4 + tq) % 2 == 0 else pw2
                        if eo or tq:
                            em.wait("tensor", vdone, vdone.n - 1)
                        for ei in range(NCH):
                            em.mm(pw[:, :], WOB[:, 1024 * ei + 128 * eo:1024 * ei + 128 * eo + 128],
                                  GON[:, 2048 * ei + 512 * tq:2048 * ei + 512 * tq + 512],
                                  start=(ei == 0), stop=(ei == NCH - 1),
                                  sem=(tdone if ei == NCH - 1 else None))
                        em.wait("vector", tdone)
                        em.do("vector", "tensor_scalar",
                              DSTG[:, T * rg + 512 * tq:T * rg + 512 * tq + 512],
                              pw[:, :], BO[:, eo:eo + 1], None, ALU.add, sem=vdone)
                    em.wait("gpsimd", vdone)
                    em.do("gpsimd", "dma_start",
                          out=strmscr[:, CW * eo + 1:CW * eo + 1 + T],
                          in_=DSTG[:, T * rg:T * rg + T],
                          accum_op=ALU.add, sem=acst, inc=16)
                em.barrier([acst, vdone], ["sync", "tensor", "vector", "scalar"])
         strm_st2 = contextlib.ExitStack()
         STRM = strm_st2.enter_context(nc.sbuf_tensor("STRM2", [128, NCH * CW], F32))
         STRMSTK[0] = strm_st2
         rst = em.sem("rst")
         em.dma("sync", STRM[:, :], strmscr[:, :], sem=rst)
         em.barrier([rst], ["tensor", "vector", "scalar"])

        if STAGE_LIMIT >= 7:
         with contextlib.ExitStack() as stp:
            emit_ln("l3")

        if dev:
            em.barrier([vdone], ["sync"])
            em.dma("sync", dbg["d_strm3"][:, :], STRM[:, :], sem=stdone)
            em.barrier([stdone], ["vector", "tensor", "scalar"])

        if STAGE_LIMIT >= 8:
         with contextlib.ExitStack() as stp:
            sbp = lambda n, s, d=F32: stp.enter_context(nc.sbuf_tensor(n, list(s), d))
            XK = sbp("XK", (128, NCH * CW), BF16)
            RHAT = sbp("RHAT", (128, NCH * T), BF16)
            MUR = sbp("MUR", (128, NCH))
            MUK = sbp("MUK", (128, NCH))
            DTMP = sbp("DTMP", (128, T))
            ZSC = sbp("ZSC", (128, 512))
            WRB = sbp("WRB", (128, NCH * 1024), BF16)
            lf = em.sem("lf")
            lfw = em.sem("lfw")
            em.barrier([tdone, vdone, sdone], ["sync", "gpsimd"])
            em.dma("sync", MUR[:, :], ap["mur"][:, :], sem=lf)
            em.dma("sync", MUK[:, :], ap["muk"][:, :], sem=lf)
            for dk in range(NCH):
                em.dma("gpsimd", WRB[:, 1024 * dk:1024 * dk + 1024],
                       ap["WrT"][128 * dk:128 * dk + 128, :], sem=lfw)
            gf = em.sem("gf")
            em.barrier([lf, vdone], ["vector", "gpsimd"])
            for k in range(NCH):
                for eng, c0 in (("vector", 0), ("gpsimd", 1024)):
                    sh = STRM[:, tcol(k, c0 - 1):tcol(k, c0 - 1) + 1024]
                    cu = STRM[:, tcol(k, c0):tcol(k, c0) + 1024]
                    dt = DTMP[:, c0:c0 + 1024]
                    xr = XHAT[:, tcol(k, c0):tcol(k, c0) + 1024]
                    xk = XK[:, tcol(k, c0):tcol(k, c0) + 1024]
                    em.do(eng, "tensor_sub", dt, cu, sh)
                    if eng == "vector":
                        em.do(eng, "scalar_tensor_tensor", xr,
                              dt, MUR[:, k:k + 1], sh, ALU.mult, ALU.add)
                        em.do(eng, "scalar_tensor_tensor", xk,
                              dt, MUK[:, k:k + 1], sh, ALU.mult, ALU.add)
                    else:
                        em.do(eng, "tensor_tensor", xr, dt,
                              MUR[:, k:k + 1].to_broadcast((128, 1024)), ALU.mult)
                        em.do(eng, "tensor_add", xr, xr, sh)
                        em.do(eng, "tensor_tensor", xk, dt,
                              MUK[:, k:k + 1].to_broadcast((128, 1024)), ALU.mult)
                        em.do(eng, "tensor_add", xk, xk, sh)
            em.do("vector", "engine_nop", sem=vdone)
            em.do("gpsimd", "engine_nop", sem=gf)
            em.barrier([vdone, gf], ["scalar", "tensor", "vector"])
            warm(AF.Sigmoid)
            with contextlib.ExitStack() as stq:
                pr1 = stq.enter_context(nc.psum_tensor("pr1", [128, 512], F32))
                pr2 = stq.enter_context(nc.psum_tensor("pr2", [128, 512], F32))
                em.barrier([lfw, vdone], ["tensor"])
                for eo in range(NCH):
                    for tq in range(4):
                        pr = pr1 if (eo * 4 + tq) % 2 == 0 else pr2
                        if eo or tq:
                            em.wait("tensor", sdone, sdone.n - 1)
                        for ei in range(NCH):
                            em.mm(pr[:, :], WRB[:, 1024 * ei + 128 * eo:1024 * ei + 128 * eo + 128],
                                  XHAT[:, tcol(ei, 512 * tq):tcol(ei, 512 * tq) + 512],
                                  start=(ei == 0), stop=(ei == NCH - 1),
                                  sem=(tdone if ei == NCH - 1 else None))
                        em.wait("scalar", tdone)
                        em.do("scalar", "activation", RHAT[:, 2048 * eo + 512 * tq:2048 * eo + 512 * tq + 512],
                              pr[:, :], AF.Sigmoid, sem=sdone)
                em.barrier([sdone], ["gpsimd"])
                for dk in range(NCH):
                    em.dma("gpsimd", WRB[:, 1024 * dk:1024 * dk + 1024],
                           ap["WkT"][128 * dk:128 * dk + 128, :], sem=lfw)
                em.barrier([lfw], ["tensor"])
                for eo in range(NCH):
                    for tq in range(4):
                        pr = pr1 if (eo * 4 + tq) % 2 == 0 else pr2
                        em.wait("tensor", sdone, sdone.n - 1)
                        for ei in range(NCH):
                            em.mm(pr[:, :], WRB[:, 1024 * ei + 128 * eo:1024 * ei + 128 * eo + 128],
                                  XK[:, tcol(ei, 512 * tq):tcol(ei, 512 * tq) + 512],
                                  start=(ei == 0), stop=(ei == NCH - 1),
                                  sem=(tdone if ei == NCH - 1 else None))
                        em.wait("scalar", tdone)
                        em.do("scalar", "activation", ZSC[:, :], pr[:, :], AF.Relu)
                        em.do("scalar", "activation",
                              XHAT[:, tcol(eo, 512 * tq):tcol(eo, 512 * tq) + 512],
                              ZSC[:, :], AF.Square, sem=sdone)
                em.barrier([sdone], ["gpsimd"])
                for dk in range(NCH):
                    em.dma("gpsimd", WRB[:, 1024 * dk:1024 * dk + 1024],
                           ap["WvT"][128 * dk:128 * dk + 128, :], sem=lfw)
                em.barrier([lfw, sdone], ["tensor"])
                for eo in range(NCH):
                    for tq in range(4):
                        pr = pr1 if (eo * 4 + tq) % 2 == 0 else pr2
                        if eo or tq:
                            em.wait("tensor", vdone, vdone.n - 1)
                        for ei in range(NCH):
                            em.mm(pr[:, :], WRB[:, 1024 * ei + 128 * eo:1024 * ei + 128 * eo + 128],
                                  XHAT[:, tcol(ei, 512 * tq):tcol(ei, 512 * tq) + 512],
                                  start=(ei == 0), stop=(ei == NCH - 1),
                                  sem=(tdone if ei == NCH - 1 else None))
                        em.wait("vector", tdone)
                        em.do("vector", "tensor_tensor", DTMP[:, 0:512], pr[:, :],
                              RHAT[:, 2048 * eo + 512 * tq:2048 * eo + 512 * tq + 512], ALU.mult)
                        sl = STRM[:, tcol(eo, 512 * tq):tcol(eo, 512 * tq) + 512]
                        em.do("vector", "tensor_add", sl, sl, DTMP[:, 0:512], sem=vdone)
                em.barrier([vdone], ["tensor", "scalar"])
                warm(AF.Exp)

        if STAGE_LIMIT >= 9:
         with contextlib.ExitStack() as stp:
            emit_ln("l4", write_xhat=False)

        if dev:
            em.barrier([vdone], ["sync"])
            em.dma("sync", dbg["d_strm4"][:, :], STRM[:, :], sem=stdone)
            em.barrier([stdone], ["vector", "tensor", "scalar"])

        with contextlib.ExitStack() as stp:
            sbp = lambda n, s, d=F32: stp.enter_context(nc.sbuf_tensor(n, list(s), d))
            psp = lambda n, s: stp.enter_context(nc.psum_tensor(n, list(s), F32))
            obuf = sbp("obuf", (128, 2 * D))
            pto = psp("pto", (128, 1024))
            pto2 = psp("pto2", (128, 1024))
            em.barrier([vdone], ["tensor"])
            td0 = tdone.n
            vd0 = vdone.n
            st0c = stdone.n
            for tc in range(16):
                pt = pto if tc % 2 == 0 else pto2
                if tc >= 2:
                    em.wait("tensor", vdone, vd0 + tc - 1)
                for ec in range(NCH):
                    em.do("tensor", "transpose",
                          pt[:, 128 * ec:128 * ec + 128],
                          STRM[:, tcol(ec, 128 * tc):tcol(ec, 128 * tc) + 128],
                          ID[:, :], sem=(tdone if ec == NCH - 1 else None))
                em.wait("vector", tdone, td0 + tc + 1)
                if tc >= 2:
                    em.wait("vector", stdone)
                em.do("vector", "tensor_copy", out=obuf[:, (tc % 2) * D:(tc % 2) * D + D],
                      in_=pt[:, :], sem=vdone)
                em.wait("sync", vdone, vd0 + tc + 1)
                em.dma("sync", y[128 * tc:128 * tc + 128, :],
                       obuf[:, (tc % 2) * D:(tc % 2) * D + D], sem=stdone)
            em.wait("sync", stdone)

        STRMSTK[0].close()

        block = st0.enter_context(nc.Block())
        for eng_name, meth in [("sync", block.sync), ("gpsimd", block.gpsimd),
                               ("tensor", block.tensor), ("vector", block.vector),
                               ("scalar", block.scalar)]:
            def body(e, _ops=em.ops[eng_name]):
                for fn in _ops:
                    fn(e)
            meth(body)

    return nc


_CACHE = {}


def kernel(**inputs):
    n = 8
    key = "nc"
    if key not in _CACHE:
        _CACHE[key] = build_nc(DEV)
    nc = _CACHE[key]
    in_maps = [host_inputs(inputs, b) for b in range(n)]
    res = run_bass_kernel_spmd(nc, in_maps, core_ids=list(range(n)))
    out = np.stack([res.results[b]["y"] for b in range(n)], 0)
    if DEV:
        kernel._dev_results = res.results
    return out


# version 24
# speedup vs baseline: 1.0091x; 1.0091x over previous
import contextlib
import numpy as np
import ml_dtypes

import concourse.bass as bass
import concourse.mybir as mybir
from concourse.bass_utils import run_bass_kernel_spmd

F32 = mybir.dt.float32
BF16 = mybir.dt.bfloat16
U8 = mybir.dt.uint8
AF = mybir.ActivationFunctionType
ALU = mybir.AluOpType

T = 2048
D = 1024
NCH = 8
CW = 2049
H = 16
HD = 64
SC = 64
NSC = 32
FF = 4096

DEV = False
STAGE_LIMIT = 99
L0SUB = 4

ENGS = ("sync", "gpsimd", "tensor", "vector", "scalar")


def tcol(k, t0=0):
    return CW * k + 1 + t0


class Sem:
    __slots__ = ("h", "n", "name")

    def __init__(self, h, name):
        self.h = h
        self.n = 0
        self.name = name


class Em:

    def __init__(self, nc, stack):
        self.nc = nc
        self.stack = stack
        self.ops = {e: [] for e in ENGS}
        self._semi = 0
        self._last_out = {}

    def sem(self, name=None):
        self._semi += 1
        h = self.stack.enter_context(self.nc.semaphore(name or f"s{self._semi}"))
        return Sem(h, name or f"s{self._semi}")

    def emit(self, eng, fn):
        self.ops[eng].append(fn)

    def wait(self, eng, sem, val=None):
        v = sem.n if val is None else val
        self.emit(eng, lambda e, h=sem.h, v=v: e.wait_ge(h, v))

    _FENCED = ("vector", "scalar", "gpsimd")
    _NOFENCE_METHS = ("nop", "drain", "sem_inc", "wait_ge", "dma_start")

    def _ap_tensors(self, vals):
        names = set()
        for v in vals:
            t = getattr(v, "tensor", None)
            if t is not None:
                names.add(t.name)
        return names

    def do(self, eng, meth, *a, sem=None, inc=1, nofence=False, **kw):
        if sem is not None:
            sem.n += inc
        fence_capable = eng in self._FENCED and meth not in self._NOFENCE_METHS
        need_fence = False
        if eng in self._FENCED and meth not in ("nop", "drain", "sem_inc", "wait_ge"):
            outs = self._ap_tensors([kw.get("out"), a[0] if a else None])
            uses = self._ap_tensors(list(a) + list(kw.values()))
            hist = self._last_out.setdefault(eng, [])
            if any(prev & uses for prev in hist) and not nofence:
                need_fence = True
            if fence_capable:
                hist.append(outs)
            else:
                hist.append(set())
            del hist[:-2]

        def fn(e, meth=meth, a=a, kw=kw, s=sem, inc=inc, pre=need_fence):
            if pre:
                e.drain()
            ins = getattr(e, meth)(*a, **kw)
            if s is not None:
                ins.then_inc(s.h, inc)

        self.emit(eng, fn)

    def dma(self, eng, out, in_, sem=None):
        self.do(eng, "dma_start", out=out, in_=in_, sem=sem, inc=16)

    def mm(self, out, lhsT, rhs, start=True, stop=True, tp=None, sem=None):
        kw = dict(start=start, stop=stop)
        if tp is not None:
            kw["tile_position"] = tp
        self.do("tensor", "matmul", out, lhsT, rhs, sem=sem, **kw)

    def barrier(self, sems_from, engs_to):
        for e in engs_to:
            for s in sems_from:
                self.wait(e, s)


def host_constants():
    c = {}
    c["ID128"] = np.eye(128, dtype=np.float32)
    c["ONES"] = np.ones((128, 128), np.float32)
    r = np.zeros((64, 64), np.float32)
    for i in range(32):
        r[2 * i, 2 * i + 1] = -1.0
        r[2 * i + 1, 2 * i] = 1.0
    R = np.zeros((128, 128), np.float32)
    R[0:64, 0:64] = r
    R[64:128, 64:128] = r
    c["R2T"] = np.ascontiguousarray(R.T)
    inv = 1.0 / (10000.0 ** (np.arange(0, HD, 2, dtype=np.float64) / HD))
    tt = np.arange(T, dtype=np.float64)
    ang = tt[None, :] * inv[(np.arange(64) % 32)][:, None]
    cos = np.cos(ang).astype(np.float32)
    sin = np.sin(ang).astype(np.float32)
    c["COSQ"] = np.concatenate([cos, cos], 0)
    c["SINQ"] = np.concatenate([sin, sin], 0)
    p64 = np.arange(128) % 64
    c1 = np.arange(64)
    c["DELTA64"] = (c1[None, :] - 1 - p64[:, None]).astype(np.float32)
    c["TRIMASK64"] = (c1[None, :] > p64[:, None]).astype(np.float32)
    c["C1PAT"] = np.broadcast_to(c1[None, :], (128, 64)).astype(np.float32)
    c["P63"] = (63.0 - p64).astype(np.float32).reshape(128, 1)
    c["IOTA2D"] = (np.arange(128)[:, None] + 128.0 * np.arange(16)[None, :]).astype(np.float32)
    c["IOTAROW"] = np.arange(T, dtype=np.float32).reshape(1, T)
    u = np.arange(1024, dtype=np.float32)
    c["MPAT"] = np.maximum(u[None, :] - np.arange(128)[:, None], 0.0).astype(np.float32)
    vals = [0.0, 1.0, -1.0, 2.0, 0.5, 1.0 / D, 1e-5, 1.0 / (T * HD),
            -8.0 * np.log(2.0), -5.0 * np.log(2.0), 1.0 / 8.0, 64.0, 1.0 / T]
    c["IDUP"] = np.zeros((128, 128), np.float32)
    for i in range(64):
        c["IDUP"][i, 64 + i] = 1.0
    c["CONSTCOL"] = np.stack([np.full(128, v, np.float32) for v in vals], 1)
    c["_constidx"] = {v: i for i, v in enumerate(vals)}
    return c


_CONSTS = host_constants()
CIDX = _CONSTS["_constidx"]


def host_inputs(inputs, b):
    d = {}
    d["x"] = np.ascontiguousarray(inputs["L0_attn_out"][b]).astype(np.float32, copy=False)
    tr = lambda w: np.ascontiguousarray(np.asarray(w).T)
    d["WwT"] = tr(inputs["L0_attn_W_write"])
    d["WreadT"] = tr(inputs["L0_attn_W_read"])
    d["decay_row"] = np.asarray(inputs["L0_attn_decay"]).reshape(1, H)
    d["logalpha_row"] = np.asarray(inputs["L0_attn_log_alpha"]).reshape(1, H)
    d["ln0g"] = np.ascontiguousarray(np.asarray(inputs["L0_ffn_ln_gamma"]).reshape(NCH, 128).T)
    d["ln0b"] = np.ascontiguousarray(np.asarray(inputs["L0_ffn_ln_beta"]).reshape(NCH, 128).T)
    f1q = np.asarray(inputs["L0_ffn_fc1_q"]).astype(np.uint8)
    d["fc1q"] = np.ascontiguousarray(f1q.reshape(D, 8, 512).transpose(1, 0, 2))
    d["fc2q"] = np.asarray(inputs["L0_ffn_fc2_q"]).astype(np.uint8)
    f1s = np.repeat(np.asarray(inputs["L0_ffn_fc1_scales"]), 128, 0).astype(np.float32)
    d["fc1s"] = np.ascontiguousarray(f1s.reshape(D, 8, 512).transpose(1, 0, 2))
    d["fc2s"] = np.ascontiguousarray(np.repeat(np.asarray(inputs["L0_ffn_fc2_scales"]), 128, 0).astype(np.float32))
    d["b1"] = np.ascontiguousarray(np.asarray(inputs["L0_ffn_fc1_bias"]).reshape(FF // 128, 128).T)
    d["b2"] = np.ascontiguousarray(np.asarray(inputs["L0_ffn_fc2_bias"]).reshape(NCH, 128).T)
    for nm in ("wq", "wk", "wv", "wg", "wo"):
        d[nm + "T"] = tr(inputs[f"L1_attn_{nm}"])
    for nm in ("bq", "bk", "bv", "bg", "bo"):
        d[nm] = np.ascontiguousarray(np.asarray(inputs[f"L1_attn_{nm}"]).reshape(NCH, 128).T)
    d["bv_row"] = np.asarray(inputs["L1_attn_bv"]).reshape(1, D)
    d["wdT"] = tr(inputs["L1_attn_wd"])
    d["bd_row"] = np.asarray(inputs["L1_attn_bd"]).reshape(1, H)
    gmt = np.asarray(inputs["L1_attn_gamma"]).reshape(H, HD).T
    btt = np.asarray(inputs["L1_attn_beta"]).reshape(H, HD).T
    d["gamT"] = np.ascontiguousarray(np.concatenate([gmt[:, 0::2], gmt[:, 1::2]], 0))
    d["betT"] = np.ascontiguousarray(np.concatenate([btt[:, 0::2], btt[:, 1::2]], 0))
    d["WrT"] = tr(inputs["L1_ffn_W_r"])
    d["WkT"] = tr(inputs["L1_ffn_W_k"])
    d["WvT"] = tr(inputs["L1_ffn_W_v"])
    d["mur"] = np.ascontiguousarray(np.asarray(inputs["L1_ffn_mu_r"]).reshape(NCH, 128).T)
    d["muk"] = np.ascontiguousarray(np.asarray(inputs["L1_ffn_mu_k"]).reshape(NCH, 128).T)
    for k, v in _CONSTS.items():
        if k != "_constidx":
            d["c_" + k] = v
    return d


def build_nc(dev=False):
    nc = bass.Bass("TRN2", target_bir_lowering=False, detect_race_conditions=False)

    ap = {}

    def din(name, shape, dt=F32):
        ap[name] = nc.dram_tensor(name, list(shape), dt, kind="ExternalInput").ap()

    din("x", (T, D))
    din("WwT", (D, D)); din("WreadT", (D, D))
    din("decay_row", (1, H)); din("logalpha_row", (1, H))
    din("ln0g", (128, NCH)); din("ln0b", (128, NCH))
    din("fc1q", (8, D, 512), U8); din("fc2q", (FF, D), U8)
    din("fc1s", (8, D, 512)); din("fc2s", (FF, D))
    din("b1", (128, FF // 128)); din("b2", (128, NCH))
    for nm in ("wqT", "wkT", "wvT", "wgT", "woT"):
        din(nm, (D, D))
    for nm in ("bq", "bk", "bv", "bg", "bo"):
        din(nm, (128, NCH))
    din("wdT", (D, H)); din("bd_row", (1, H)); din("bv_row", (1, D))
    din("gamT", (128, H // 2)); din("betT", (128, H // 2))
    din("WrT", (D, D)); din("WkT", (D, D)); din("WvT", (D, D))
    din("mur", (128, NCH)); din("muk", (128, NCH))
    din("c_ID128", (128, 128)); din("c_ONES", (128, 128))
    din("c_R2T", (128, 128))
    din("c_COSQ", (128, T)); din("c_SINQ", (128, T))
    din("c_DELTA64", (128, 64)); din("c_TRIMASK64", (128, 64))
    din("c_C1PAT", (128, 64)); din("c_P63", (128, 1))
    din("c_IOTA2D", (128, 16)); din("c_IOTAROW", (1, T))
    din("c_MPAT", (128, 1024)); din("c_CONSTCOL", (128, 13)); din("c_IDUP", (128, 128))

    y = nc.dram_tensor("y", [T, D], F32, kind="ExternalOutput").ap()
    hscr = nc.dram_tensor("hscr", [FF, T], BF16, kind="Internal").ap()
    strmscr = nc.dram_tensor("strmscr", [128, NCH * CW], F32, kind="Internal").ap()
    dbg = {}
    if dev:
        for nm, shape in [("d_strm0", (128, NCH * CW)), ("d_strm1", (128, NCH * CW)),
                          ("d_strm2", (128, NCH * CW)), ("d_strm3", (128, NCH * CW)),
                          ("d_strm4", (128, NCH * CW))]:
            dbg[nm] = nc.dram_tensor(nm, list(shape), F32, kind="ExternalOutput").ap()
        for nm, shape, dt in [("d_vtok", (128, 8192), BF16), ("d_xptok", (128, 8192), BF16),
                              ("d_reads", (128, 8192), BF16), ("d_state", (128, 512), F32),
                              ("d_mht", (128, 1024), F32), ("d_smh", (128, 1024), BF16),
                              ("d_rep", (128, 64), F32), ("d_rowb", (1, 64), F32),
                              ("d_gp", (128, 1024), F32), ("d_gw", (128, 1024), BF16),
                              ("d_gc", (128, 512), F32)]:
            dbg[nm] = nc.dram_tensor(nm, list(shape), F32 if dt == F32 else F32, kind="ExternalOutput").ap()

    with contextlib.ExitStack() as st0:
        em = Em(nc, st0)
        sb = lambda n, s, d=F32: st0.enter_context(nc.sbuf_tensor(n, list(s), d))
        ps = lambda n, s: st0.enter_context(nc.psum_tensor(n, list(s), F32))

        CONST = sb("CONST", (128, 13))
        ID = sb("ID", (128, 128))
        ONES = sb("ONESS", (128, 128))
        R2T = sb("R2TS", (128, 128), BF16)
        XHAT = sb("XHAT", (128, NCH * CW), BF16)
        WARM = sb("WARM", (128, 1024), BF16)
        IDBF = sb("IDBF", (128, 128), BF16)
        LDS = sb("LDS", (128, 32))

        def warm(func):
            em.do("scalar", "activation", WARM[:, :],
                  CONST[:, 0:1].to_broadcast((128, 1024)), func)

        cc = lambda v: CONST[:, CIDX[v]:CIDX[v] + 1]
        cc1 = lambda v: CONST[0:1, CIDX[v]:CIDX[v] + 1]

        ld = em.sem("ld")
        ldc = em.sem("ldc")
        lx0 = em.sem("lx0")
        lx1 = em.sem("lx1")
        wld = em.sem("wld")
        main = em.sem("main")
        tdone = em.sem("tdone")
        vdone = em.sem("vdone")
        sdone = em.sem("sdone")
        stdone = em.sem("stdone")
        gdump = em.sem("gdump")

        for nm, buf in [("c_CONSTCOL", CONST), ("c_ID128", ID), ("c_ONES", ONES)]:
            em.dma("sync", buf[:, :], ap[nm][:, :], sem=ldc)
        em.dma("gpsimd", R2T[:, :], ap["c_R2T"][:, :], sem=wld)

        with contextlib.ExitStack() as stp:
            sbp = lambda n, s, d=F32: stp.enter_context(nc.sbuf_tensor(n, list(s), d))
            psp = lambda n, s: stp.enter_context(nc.psum_tensor(n, list(s), F32))
            strm_st = contextlib.ExitStack()
            STRM = strm_st.enter_context(nc.sbuf_tensor("STRM", [128, NCH * CW], F32))
            STRMSTK = [strm_st]

            xtok = sbp("xtok", (128, 2 * D))
            ptr = psp("ptr", (128, 1024))
            ptr2 = psp("ptr2", (128, 1024))

            em.wait("vector", ldc)
            em.do("vector", "tensor_copy", out=IDBF[:, :], in_=ID[:, :])
            em.do("vector", "memset", STRM[:, :].rearrange("p (k w) -> p k w", w=CW)[:, :, 0:1], 0.0, sem=vdone)

            td0 = tdone.n
            vd0 = vdone.n
            lxs = [lx0, lx1]
            for tc in range(16):
                if tc >= 2:
                    em.wait("sync", tdone, td0 + tc - 1)
                em.dma("sync", xtok[:, (tc % 2) * D:(tc % 2) * D + D],
                       ap["x"][128 * tc:128 * tc + 128, :], sem=lxs[tc % 2])
            em.barrier([ldc], ["tensor"])
            for tc in range(16):
                em.wait("tensor", lxs[tc % 2], 16 * (tc // 2 + 1))
                if tc >= 2:
                    em.wait("tensor", vdone, vd0 + tc - 1)
                pt = ptr if tc % 2 == 0 else ptr2
                for ec in range(NCH):
                    em.do("tensor", "transpose",
                          pt[:, 128 * ec:128 * ec + 128],
                          xtok[:, (tc % 2) * D + 128 * ec:(tc % 2) * D + 128 * ec + 128],
                          ID[:, :], sem=(tdone if ec == NCH - 1 else None))

            for tc in range(16):
                em.wait("vector", tdone, td0 + tc + 1)
                pt = ptr if tc % 2 == 0 else ptr2
                outap = STRM[:, :].rearrange("p (k w) -> p k w", w=CW)[:, :, 1 + 128 * tc:1 + 128 * tc + 128]
                em.do("vector", "tensor_copy", out=outap, in_=pt[:, :].rearrange("p (k w) -> p k w", w=128),
                      sem=vdone)
            em.wait("vector", vdone)
            em.do("vector", "tensor_copy", out=XHAT[:, :], in_=STRM[:, :], sem=vdone)

        if dev:
            em.wait("sync", vdone)
            em.dma("sync", dbg["d_strm0"][:, :], STRM[:, :], sem=stdone)
            em.barrier([stdone], ["vector", "tensor", "scalar"])

        def emit_ln(tag, affine=None, write_strm=True, write_xhat=True):
            with contextlib.ExitStack() as stl:
                ROWS = stl.enter_context(nc.sbuf_tensor(tag + "ROWS", [1, 4 * 2048], F32))
                YSQA = stl.enter_context(nc.sbuf_tensor(tag + "YSQA", [128, NCH * 512], F32))
                YSQB = stl.enter_context(nc.sbuf_tensor(tag + "YSQB", [128, NCH * 512], F32))
                pLNa = stl.enter_context(nc.psum_tensor(tag + "pLNa", [1, 512], F32))
                pLNb = stl.enter_context(nc.psum_tensor(tag + "pLNb", [1, 512], F32))
                pMUr = [stl.enter_context(nc.psum_tensor(tag + "pMU%d" % i, [128, 512], F32))
                        for i in range(2)]
                pRSr = [stl.enter_context(nc.psum_tensor(tag + "pRS%d" % i, [128, 512], F32))
                        for i in range(2)]
                MUB = [stl.enter_context(nc.sbuf_tensor(tag + "MUB%d" % i, [128, 512], F32))
                       for i in range(2)]
                RSB = [stl.enter_context(nc.sbuf_tensor(tag + "RSB%d" % i, [128, 512], F32))
                       for i in range(2)]
                gdn = em.sem(tag + "gdn")
                em.barrier([vdone], ["scalar", "tensor"])
                sq_s = {}
                sum_s = {}
                mr_s = {}
                rep_s = {}
                nv_s = {}
                ng_s = {}
                for qt in range(4):
                    q0 = 512 * qt
                    YSQ = YSQA if qt % 2 == 0 else YSQB
                    pMU = pMUr[qt % 2]
                    pRS = pRSr[qt % 2]
                    if qt >= 2:
                        em.wait("scalar", vdone, nv_s[qt - 2])
                        em.wait("scalar", gdn, ng_s[qt - 2])
                    for k in range(NCH):
                        em.do("scalar", "activation", YSQ[:, 512 * k:512 * k + 512],
                              STRM[:, tcol(k, q0):tcol(k, q0) + 512], AF.Square,
                              sem=(sdone if k == NCH - 1 else None))
                    sq_s[qt] = sdone.n
                    em.wait("tensor", sdone, sq_s[qt])
                    for k in range(NCH):
                        em.mm(pLNa[0:1, :], ONES[:, 0:1], STRM[:, tcol(k, q0):tcol(k, q0) + 512],
                              start=(k == 0), stop=(k == NCH - 1))
                    for k in range(NCH):
                        em.mm(pLNb[0:1, :], ONES[:, 0:1], YSQ[:, 512 * k:512 * k + 512],
                              start=(k == 0), stop=(k == NCH - 1),
                              sem=(tdone if k == NCH - 1 else None))
                    sum_s[qt] = tdone.n
                    em.wait("vector", tdone, sum_s[qt])
                    R0 = 2048 * qt
                    mu = ROWS[0:1, R0:R0 + 512]
                    m2 = ROWS[0:1, R0 + 512:R0 + 1024]
                    var = ROWS[0:1, R0 + 1024:R0 + 1536]
                    rstd = ROWS[0:1, R0 + 1536:R0 + 2048]
                    em.do("vector", "tensor_scalar", mu, pLNa[0:1, :],
                          CONST[0:1, CIDX[1.0 / D]:CIDX[1.0 / D] + 1], None, ALU.mult)
                    em.do("vector", "tensor_scalar", m2, pLNb[0:1, :],
                          CONST[0:1, CIDX[1.0 / D]:CIDX[1.0 / D] + 1], None, ALU.mult)
                    em.do("vector", "tensor_tensor", var, mu, mu, ALU.mult)
                    em.do("vector", "tensor_tensor", var, m2, var, ALU.subtract, sem=vdone)
                    em.wait("scalar", vdone)
                    em.do("scalar", "activation", rstd, var, AF.Ln,
                          bias=CONST[0:1, CIDX[1e-5]:CIDX[1e-5] + 1], scale=1.0)
                    em.do("scalar", "activation", rstd, rstd, AF.Exp, scale=-0.5, sem=sdone)
                    em.wait("vector", sdone)
                    em.do("vector", "tensor_tensor", mu, mu, rstd, ALU.mult, sem=vdone)
                    mr_s[qt] = vdone.n
                    em.wait("tensor", vdone, mr_s[qt])
                    if qt >= 2:
                        em.wait("tensor", vdone, nv_s[qt - 2])
                        em.wait("tensor", gdn, ng_s[qt - 2])
                    em.mm(pMU[:, :], ONES[0:1, :], mu, sem=None)
                    em.mm(pRS[:, :], ONES[0:1, :], rstd, sem=tdone)
                    rep_s[qt] = tdone.n
                    em.wait("vector", tdone, rep_s[qt])
                    if qt >= 2:
                        em.wait("vector", gdn, ng_s[qt - 2])
                    mub, rsb = MUB[qt % 2][:, :], RSB[qt % 2][:, :]
                    em.do("vector", "tensor_copy", out=mub, in_=pMU[:, :])
                    em.do("vector", "tensor_copy", out=rsb, in_=pRS[:, :], sem=vdone)
                    cp_s = vdone.n
                    em.wait("gpsimd", vdone, cp_s)
                    for k in range(NCH):
                        eng = "vector" if k % 2 == 0 else "gpsimd"
                        mu_t = pMU[:, :] if k % 2 == 0 else mub
                        rs_t = pRS[:, :] if k % 2 == 0 else rsb
                        sl = STRM[:, tcol(k, q0):tcol(k, q0) + 512]
                        t1 = YSQ[:, 512 * k:512 * k + 512]
                        em.do(eng, "tensor_tensor", t1, sl, rs_t, ALU.mult)
                        if affine is None:
                            if write_strm:
                                em.do(eng, "tensor_tensor", sl, t1, mu_t, ALU.subtract)
                                if write_xhat:
                                    em.do(eng, "tensor_copy",
                                          out=XHAT[:, tcol(k, q0):tcol(k, q0) + 512], in_=sl)
                            else:
                                em.do(eng, "tensor_tensor",
                                      XHAT[:, tcol(k, q0):tcol(k, q0) + 512], t1, mu_t,
                                      ALU.subtract)
                        else:
                            gcols, bcols = affine
                            em.do(eng, "tensor_tensor", t1, t1, mu_t, ALU.subtract)
                            em.do(eng, "tensor_scalar",
                                  XHAT[:, tcol(k, q0):tcol(k, q0) + 512], t1,
                                  gcols[:, k:k + 1], bcols[:, k:k + 1], ALU.mult, ALU.add)
                    em.do("vector", "engine_nop", sem=vdone)
                    nv_s[qt] = vdone.n
                    em.do("gpsimd", "engine_nop", sem=gdn)
                    ng_s[qt] = gdn.n
                em.wait("vector", gdn, ng_s[3])
                em.do("vector", "engine_nop", sem=vdone)

        if STAGE_LIMIT >= 2:
         with contextlib.ExitStack() as stp:
            sbp = lambda n, s, d=F32: stp.enter_context(nc.sbuf_tensor(n, list(s), d))
            WWT = sbp("WWT", (128, NCH * 1024), BF16)
            WRT = sbp("WRT", (128, NCH * 1024), BF16)
            WRTRAW = sbp("WRTRAW", (128, 1024))
            ROWB = sbp("ROWB", (1, 64))
            REP = sbp("REP", (128, 64))
            ALPH2 = sbp("ALPH2", (128, 8))
            MHT = sbp("MHT", (128, 1024))
            GP = sbp("GPt", (128, 1024))
            GW = sbp("GWt", (128, 1024), BF16)
            GC = sbp("GCt", (128, 512))
            DELTA = sbp("DELTA", (128, 64))
            TRIM = sbp("TRIM", (128, 64))
            C1P = sbp("C1P", (128, 64))
            P63 = sbp("P63", (128, 1))
            DROW = sbp("DROW", (1, 16))
            LAROW = sbp("LAROW", (1, 16))
            TROW = sbp("TROW", (1, 64))
            STATE = sbp("STATE", (128, 512))
            SMH = sbp("SMH", (128, 1024), BF16)
            VG = sbp("VG", (128, 1024), BF16)
            VTOK = sbp("VTOK", (128, 8 * 1024), BF16)
            XPTOK = sbp("XPTOK", (128, 8 * 1024), BF16)
            READS = sbp("READS", (128, 8 * 1024), BF16)

            em.barrier([tdone, vdone], ["gpsimd", "sync", "scalar", "vector"])
            for dk in range(NCH):
                em.dma("gpsimd", WWT[:, 1024 * dk:1024 * dk + 1024],
                       ap["WwT"][128 * dk:128 * dk + 128, :], sem=wld)
            for nm, buf in [("c_DELTA64", DELTA), ("c_TRIMASK64", TRIM), ("c_C1PAT", C1P),
                            ("c_P63", P63), ("decay_row", DROW), ("logalpha_row", LAROW)]:
                em.dma("sync", buf[:, :], ap[nm][:, :], sem=ld)

            em.barrier([ld], ["scalar"])
            warm(AF.Exp)
            em.do("scalar", "activation", TROW[0:1, 0:16], DROW[0:1, :], AF.Exp, scale=-1.0, sem=sdone)
            em.wait("vector", sdone)
            em.do("vector", "tensor_scalar", TROW[0:1, 0:16], TROW[0:1, 0:16],
                  CONST[0:1, CIDX[1.0]:CIDX[1.0] + 1], None, ALU.add)
            em.do("vector", "reciprocal", out=ROWB[0:1, 0:16], in_=TROW[0:1, 0:16], sem=vdone)
            em.wait("scalar", vdone)
            em.do("scalar", "activation", ROWB[0:1, 16:32], ROWB[0:1, 0:16], AF.Ln)
            em.do("scalar", "activation", ROWB[0:1, 32:48], ROWB[0:1, 16:32], AF.Exp, scale=64.0)
            em.do("scalar", "activation", ROWB[0:1, 48:64], LAROW[0:1, :], AF.Exp, sem=sdone)
            em.wait("tensor", sdone)
            with nc.psum_tensor("pREP", [128, 64], F32) as pREP:
                em.mm(pREP[:, :], ONES[0:1, :], ROWB[0:1, :], sem=tdone)
                em.wait("vector", tdone)
                em.do("vector", "tensor_copy", out=REP[:, :], in_=pREP[:, :], sem=vdone)
            for dk in range(8):
                em.do("vector", "tensor_copy", out=ALPH2[0:64, dk:dk + 1], in_=REP[0:64, 48 + 2 * dk:49 + 2 * dk])
                em.do("vector", "tensor_copy", out=ALPH2[64:128, dk:dk + 1], in_=REP[64:128, 49 + 2 * dk:50 + 2 * dk])
            for h in range(H):
                cb = (h % 2) * 512 + 64 * (h // 2)
                lgc = REP[:, 16 + h:17 + h]
                em.do("vector", "tensor_scalar", MHT[:, cb:cb + 64], DELTA[:, :], lgc, None, ALU.mult)
                em.do("vector", "tensor_scalar", GP[:, cb:cb + 64], C1P[:, :], lgc, None, ALU.mult)
                em.do("vector", "tensor_scalar", GW[:, 64 * h:64 * h + 64],
                      P63[:, 0:1].to_broadcast((128, 64)), lgc, None, ALU.mult)
            em.do("vector", "engine_nop", sem=vdone)
            em.wait("scalar", vdone)
            em.do("scalar", "activation", MHT[:, :], MHT[:, :], AF.Exp)
            em.do("scalar", "activation", GP[:, :], GP[:, :], AF.Exp)
            em.do("scalar", "activation", GW[:, :], GW[:, :], AF.Exp, sem=sdone)
            em.wait("vector", sdone)
            for h in range(H):
                cb = (h % 2) * 512 + 64 * (h // 2)
                rh = (h % 2) * 64
                em.do("vector", "tensor_tensor", MHT[:, cb:cb + 64], MHT[:, cb:cb + 64], TRIM[:, :], ALU.mult)
                em.do("vector", "tensor_scalar", GC[rh:rh + 64, 64 * (h // 2):64 * (h // 2) + 64],
                      ONES[rh:rh + 64, 0:64], REP[rh:rh + 64, 32 + h:33 + h], None, ALU.mult)
            em.do("vector", "memset", STATE[:, :], 0.0)
            em.do("vector", "engine_nop", sem=vdone)

            ld_w = em.sem("ld_w")
            em.barrier([vdone], ["sync"])
            for dk in range(8):
                em.dma("sync", WRTRAW[:, :], ap["WreadT"][128 * dk:128 * dk + 128, :], sem=ld_w)
                em.wait("vector", ld_w)
                em.do("vector", "tensor_scalar", WRT[:, 1024 * dk:1024 * dk + 1024],
                      WRTRAW[:, :], ALPH2[:, dk:dk + 1], None, ALU.mult, sem=vdone)
                if dk < 7:
                    em.wait("sync", vdone)

            ssnap = {}
            smsnap = {}
            isnap = {}
            gsnap = {}
            usnap = {}
            stsnap = {}
            for ih in (range(2) if L0SUB >= 1 else []):
                T0h = 1024 * ih
                with contextlib.ExitStack() as stq:
                    pV = stq.enter_context(nc.psum_tensor(f"pV{ih}", [128, 1024], F32))
                    pV2 = stq.enter_context(nc.psum_tensor(f"pV2{ih}", [128, 1024], F32))
                    pX = stq.enter_context(nc.psum_tensor(f"pX{ih}", [128, 1024], BF16))
                    pX2 = stq.enter_context(nc.psum_tensor(f"pX2{ih}", [128, 1024], BF16))
                    em.barrier([vdone, wld], ["tensor"])
                    vd0 = vdone.n
                    td0 = tdone.n
                    for tc in range(8):
                        pv = pV if tc % 2 == 0 else pV2
                        px = pX if tc % 2 == 0 else pX2
                        if tc >= 2:
                            em.wait("tensor", vdone, vd0 + 2 * (tc - 1))
                        t0 = T0h + 128 * tc
                        for ec in range(NCH):
                            for dvh in range(2):
                                em.mm(pv[:, 512 * dvh:512 * dvh + 512],
                                      XHAT[:, tcol(ec, t0):tcol(ec, t0) + 128],
                                      WWT[:, 1024 * ec + 512 * dvh:1024 * ec + 512 * dvh + 512],
                                      start=(ec == 0), stop=(ec == NCH - 1),
                                      sem=(tdone if (ec == NCH - 1 and dvh == 1) else None))
                        for ec in (range(NCH) if L0SUB >= 2 else []):
                            em.do("tensor", "transpose", px[:, 128 * ec:128 * ec + 128],
                                  XHAT[:, tcol(ec, t0 - 1):tcol(ec, t0 - 1) + 128], IDBF[:, :],
                                  sem=(tdone if ec == NCH - 1 else None))
                        if L0SUB >= 2:
                            em.wait("vector", tdone, td0 + 2 * tc + 1)
                            em.do("vector", "tensor_copy", out=VTOK[:, 1024 * tc:1024 * tc + 1024], in_=pv[:, :])
                            em.wait("vector", tdone, td0 + 2 * tc + 2)
                            em.do("vector", "tensor_copy", out=XPTOK[:, 1024 * tc:1024 * tc + 1024], in_=px[:, :],
                                  sem=vdone, inc=2)
                        else:
                            em.wait("vector", tdone, td0 + tc + 1)
                            em.do("vector", "tensor_copy", out=VTOK[:, 1024 * tc:1024 * tc + 1024], in_=pv[:, :],
                                  sem=vdone, inc=2)
                if L0SUB < 3:
                    continue
                with contextlib.ExitStack() as stq:
                    pS = stq.enter_context(nc.psum_tensor(f"pS{ih}", [128, 1024], F32))
                    pRi = stq.enter_context(nc.psum_tensor(f"pRi{ih}", [128, 1024], F32))
                    pRa = stq.enter_context(nc.psum_tensor(f"pRa{ih}", [128, 1024], F32))
                    pU = stq.enter_context(nc.psum_tensor(f"pU{ih}", [128, 1024], F32))
                    RSC = stq.enter_context(nc.sbuf_tensor(f"RSC{ih}", [128, 512], F32))
                    gdone = em.sem(f"gdone{ih}")
                    em.barrier([vdone], ["tensor", "gpsimd"])
                    for c in range(16):
                        gc = 16 * ih + c
                        t0 = 64 * gc
                        tcl = c // 2
                        rh = (gc % 2) * 64
                        if c >= 2:
                            em.wait("tensor", vdone, smsnap[gc - 2])
                        for h in range(H):
                            hh = (h % 2) * 64
                            cb = (h % 2) * 512 + 64 * (h // 2)
                            em.mm(pS[rh:rh + 64, cb:cb + 64],
                                  XHAT[hh:hh + 64, tcol(h // 2, t0 - 1):tcol(h // 2, t0 - 1) + 64],
                                  XHAT[hh:hh + 64, tcol(h // 2, t0):tcol(h // 2, t0) + 64],
                                  tp=(hh, rh), sem=(tdone if h == H - 1 else None))
                        ssnap[gc] = tdone.n
                        em.do("vector", "tensor_tensor", VG[rh:rh + 64, :],
                              VTOK[rh:rh + 64, 1024 * tcl:1024 * tcl + 1024], GW[rh:rh + 64, :], ALU.mult)
                        em.wait("vector", tdone, ssnap[gc])
                        em.do("vector", "tensor_tensor", SMH[rh:rh + 64, :],
                              pS[rh:rh + 64, :], MHT[rh:rh + 64, :], ALU.mult, sem=vdone)
                        smsnap[gc] = vdone.n
                        if gc >= 1:
                            em.wait("tensor", vdone, stsnap[gc - 1])
                        for h in range(H):
                            hh = (h % 2) * 64
                            cb = (h % 2) * 512 + 64 * (h // 2)
                            em.mm(pRi[hh:hh + 64, cb:cb + 64],
                                  STATE[hh:hh + 64, 64 * (h // 2):64 * (h // 2) + 64],
                                  STRM[hh:hh + 64, tcol(h // 2, t0):tcol(h // 2, t0) + 64],
                                  tp=(hh, hh), sem=(tdone if h == H - 1 else None))
                        isnap[gc] = tdone.n
                        em.wait("gpsimd", tdone, isnap[gc])
                        em.do("gpsimd", "tensor_tensor", STATE[0:64, :],
                              STATE[0:64, :], GC[0:64, :], ALU.mult)
                        em.do("gpsimd", "tensor_tensor", STATE[64:128, :],
                              STATE[64:128, :], GC[64:128, :], ALU.mult,
                              nofence=True, sem=gdone)
                        gsnap[gc] = gdone.n
                        em.wait("tensor", vdone, smsnap[gc])
                        for h in range(H):
                            hh = (h % 2) * 64
                            cb = (h % 2) * 512 + 64 * (h // 2)
                            em.mm(pRa[hh:hh + 64, cb:cb + 64],
                                  VTOK[rh:rh + 64, 1024 * tcl + 64 * h:1024 * tcl + 64 * h + 64],
                                  SMH[rh:rh + 64, cb:cb + 64],
                                  tp=(rh, hh), sem=(tdone if h == H - 1 else None))
                        for h in range(H):
                            hh = (h % 2) * 64
                            cb = (h % 2) * 512 + 64 * (h // 2)
                            em.mm(pU[hh:hh + 64, cb:cb + 64],
                                  XPTOK[rh:rh + 64, 1024 * tcl + 64 * h:1024 * tcl + 64 * h + 64],
                                  VG[rh:rh + 64, 64 * h:64 * h + 64],
                                  tp=(rh, hh), sem=(tdone if h == H - 1 else None))
                        usnap[gc] = tdone.n
                        em.wait("vector", tdone, usnap[gc])
                        tl0 = 64 * c
                        for par in range(2):
                            pr = 64 * par
                            cbase = 512 * par
                            em.do("vector", "tensor_tensor", RSC[pr:pr + 64, :],
                                  pRi[pr:pr + 64, cbase:cbase + 512], GP[pr:pr + 64, cbase:cbase + 512],
                                  ALU.mult, nofence=(par == 1))
                        for par in range(2):
                            pr = 64 * par
                            cbase = 512 * par
                            em.do("vector", "tensor_tensor",
                                  READS[pr:pr + 64, :].rearrange("p (dk n) -> p dk n", n=1024)[:, :, tl0:tl0 + 64],
                                  RSC[pr:pr + 64, :].rearrange("p (dk n) -> p dk n", n=64),
                                  pRa[pr:pr + 64, cbase:cbase + 512].rearrange("p (dk n) -> p dk n", n=64),
                                  ALU.add, nofence=(par == 1))
                        em.wait("vector", gdone, gsnap[gc])
                        em.do("vector", "tensor_tensor", STATE[0:64, :],
                              STATE[0:64, :], pU[0:64, 0:512], ALU.add)
                        em.do("vector", "tensor_tensor", STATE[64:128, :],
                              STATE[64:128, :], pU[64:128, 512:1024], ALU.add,
                              nofence=True, sem=vdone)
                        stsnap[gc] = vdone.n
                if dev and ih == 1:
                    em.barrier([vdone], ["gpsimd", "sync"])
                    em.dma("gpsimd", dbg["d_vtok"][:, :], VTOK[:, :], sem=gdump)
                    em.dma("gpsimd", dbg["d_xptok"][:, :], XPTOK[:, :], sem=gdump)
                    em.dma("gpsimd", dbg["d_reads"][:, :], READS[:, :], sem=gdump)
                    em.dma("gpsimd", dbg["d_smh"][:, :], SMH[:, :], sem=gdump)
                    em.dma("sync", dbg["d_state"][:, :], STATE[:, :], sem=stdone)
                    em.dma("sync", dbg["d_mht"][:, :], MHT[:, :], sem=stdone)
                    em.dma("sync", dbg["d_rep"][:, :], REP[:, :], sem=stdone)
                    em.dma("sync", dbg["d_rowb"][:, :], ROWB[:, :], sem=stdone)
                    em.dma("sync", dbg["d_gp"][:, :], GP[:, :], sem=stdone)
                    em.dma("gpsimd", dbg["d_gw"][:, :], GW[:, :], sem=gdump)
                    em.dma("sync", dbg["d_gc"][:, :], GC[:, :], sem=stdone)
                    em.barrier([stdone, gdump], ["vector", "tensor"])
                if L0SUB < 4:
                    continue
                with contextlib.ExitStack() as stq:
                    pW = stq.enter_context(nc.psum_tensor(f"pW{ih}", [128, 512], F32))
                    pW2 = stq.enter_context(nc.psum_tensor(f"pW2{ih}", [128, 512], F32))
                    em.barrier([vdone], ["tensor"])
                    vd0 = vdone.n
                    ii = 0
                    for ecO in range(NCH):
                        for tq in range(2):
                            pw = pW if ii % 2 == 0 else pW2
                            if ii >= 2:
                                em.wait("tensor", vdone, vd0 + ii - 1)
                            for dk in range(8):
                                em.mm(pw[:, :], WRT[:, 1024 * dk + 128 * ecO:1024 * dk + 128 * ecO + 128],
                                      READS[:, 1024 * dk + 512 * tq:1024 * dk + 512 * tq + 512],
                                      start=(dk == 0), stop=(dk == 7),
                                      sem=(tdone if dk == 7 else None))
                            em.wait("vector", tdone)
                            sl = STRM[:, tcol(ecO, T0h + 512 * tq):tcol(ecO, T0h + 512 * tq) + 512]
                            em.do("vector", "scalar_tensor_tensor", sl, sl, cc(2.0), pw[:, :],
                                  ALU.mult, ALU.add, sem=vdone)
                            ii += 1

        if STAGE_LIMIT >= 3:
         with contextlib.ExitStack() as stp:
            sbp = lambda n, s, d=F32: stp.enter_context(nc.sbuf_tensor(n, list(s), d))
            emit_ln("l1")

        if dev:
            em.barrier([vdone], ["sync"])
            em.dma("sync", dbg["d_strm1"][:, :], STRM[:, :], sem=stdone)
            em.barrier([stdone], ["vector", "tensor", "scalar"])

        if STAGE_LIMIT >= 4:
         with contextlib.ExitStack() as stp:
            sbp = lambda n, s, d=F32: stp.enter_context(nc.sbuf_tensor(n, list(s), d))
            LN0G = sbp("LN0G", (128, NCH))
            LN0B = sbp("LN0B", (128, NCH))
            B1 = sbp("B1", (128, FF // 128))
            HB2 = sbp("HB2", (128, NCH))
            with contextlib.ExitStack() as stq:

                ldb = em.sem("ldb")
                em.barrier([tdone, vdone, sdone], ["sync"])
                em.dma("sync", LN0G[:, :], ap["ln0g"][:, :], sem=ldb)
                em.dma("sync", LN0B[:, :], ap["ln0b"][:, :], sem=ldb)
                em.dma("sync", B1[:, :], ap["b1"][:, :], sem=ldb)
                em.dma("sync", HB2[:, :], ap["b2"][:, :], sem=ldb)
                em.barrier([ldb], ["vector", "scalar"])
                em.do("vector", "tensor_scalar", HB2[:, :], HB2[:, :], cc(0.5), None, ALU.mult)
                emit_ln("lf", affine=(LN0G, LN0B))

            with contextlib.ExitStack() as stq:
                sbq = lambda n, s, d=F32: stq.enter_context(nc.sbuf_tensor(n, list(s), d))
                W1U = sbq("W1U", (128, 2 * FF), U8)
                SC1 = sbq("SC1", (128, FF), BF16)
                W1B = sbq("W1B", (128, 2 * FF), BF16)
                HSTG = sbq("HSTG", (128, 2 * 4 * T), BF16)
                W2U = sbq("W2U", (128, 4 * 1024), U8)
                SC2 = sbq("SC2", (128, 4 * 1024), BF16)
                W2T = sbq("W2T", (128, 4 * 1024), BF16)
                ph1 = stq.enter_context(nc.psum_tensor("ph1", [128, 512], F32))
                ph2 = stq.enter_context(nc.psum_tensor("ph2", [128, 512], F32))
                lw1 = em.sem("lw1")
                lc1 = em.sem("lc1")
                lw2 = em.sem("lw2")
                dq1 = em.sem("dq1")
                hout = em.sem("hout")
                w2st = em.sem("w2st")
                w2scr = nc.dram_tensor("w2scr", [FF, D], BF16, kind="Internal").ap()

                em.barrier([vdone, sdone], ["sync", "gpsimd"])
                em.barrier([vdone], ["scalar"])
                warm(AF.Silu)

                w2cpy = [0]

                def load_w2(r):
                    em.wait("sync", sdone, w2cpy[0])
                    em.wait("gpsimd", dq1)
                    for j in range(4):
                        fc = 4 * r + j
                        em.dma("sync", W2U[:, 1024 * j:1024 * j + 1024],
                               ap["fc2q"][128 * fc:128 * fc + 128, :], sem=lw2)
                        em.dma("gpsimd", SC2[:, 1024 * j:1024 * j + 1024],
                               ap["fc2s"][128 * fc:128 * fc + 128, :], sem=lc1)

                def w2_round(r):
                    if r == 0:
                        load_w2(0)
                    em.wait("scalar", lw2, 16 * 4 * (r + 1))
                    em.wait("scalar", lc1, lc1.n)
                    em.wait("scalar", w2st, w2st.n)
                    em.do("scalar", "activation", W2T[:, :], W2U[:, :], AF.Copy,
                          bias=-128.0, scale=1.0, sem=sdone)
                    w2cpy[0] = sdone.n
                    em.wait("vector", sdone)
                    em.do("vector", "tensor_scalar", SC2[:, :], SC2[:, :], cc(0.5), None,
                          ALU.mult)
                    em.do("vector", "tensor_tensor", W2T[:, :], W2T[:, :], SC2[:, :],
                          ALU.mult, sem=dq1)
                    if r + 1 < 8:
                        load_w2(r + 1)
                    em.wait("sync", dq1)
                    for j in range(4):
                        fc = 4 * r + j
                        em.dma("sync", w2scr[128 * fc:128 * fc + 128, :],
                               W2T[:, 1024 * j:1024 * j + 1024], sem=w2st)

                def load_fc1(fg, cpy_snap):
                    rg = fg % 2
                    em.wait("sync", sdone, cpy_snap)
                    em.wait("gpsimd", dq1)
                    for ec in range(NCH):
                        em.dma("sync", W1U[:, FF * rg + 512 * ec:FF * rg + 512 * ec + 512],
                               ap["fc1q"][fg, 128 * ec:128 * ec + 128, :], sem=lw1)
                        em.dma("gpsimd", SC1[:, 512 * ec:512 * ec + 512],
                               ap["fc1s"][fg, 128 * ec:128 * ec + 128, :], sem=lc1)

                mm_snap = {}
                h_snap = {}
                dq_s = {}

                def dq_fc1(fg):
                    rg = fg % 2
                    em.wait("scalar", lw1, 16 * NCH * (fg + 1))
                    if fg >= 2:
                        em.wait("scalar", tdone, mm_snap[fg - 2])
                    em.do("scalar", "activation", W1B[:, FF * rg:FF * rg + FF],
                          W1U[:, FF * rg:FF * rg + FF], AF.Copy, bias=-128.0, scale=1.0,
                          sem=sdone)
                    cpy_snap = sdone.n
                    em.wait("vector", lc1, lc1.n)
                    em.wait("vector", sdone, cpy_snap)
                    em.do("vector", "tensor_tensor", W1B[:, FF * rg:FF * rg + FF],
                          W1B[:, FF * rg:FF * rg + FF], SC1[:, :],
                          ALU.mult, sem=dq1)
                    dq_s[fg] = dq1.n
                    if fg + 1 < 8:
                        load_fc1(fg + 1, cpy_snap)

                load_fc1(0, 0)
                dq_fc1(0)
                sil_s = {}
                for fg in range(8):
                    rg = fg % 2
                    em.wait("tensor", dq1, dq_s[fg])
                    if fg >= 1:
                        em.wait("tensor", sdone, sil_s[fg - 1] - 1)
                    for fc in range(4):
                        for tq in range(4):
                            ph = ph1 if (fc * 4 + tq) % 2 == 0 else ph2
                            if fc or tq:
                                em.wait("tensor", sdone, sdone.n - 1)
                            for ec in range(NCH):
                                em.mm(ph[:, :],
                                      W1B[:, FF * rg + 512 * ec + 128 * fc:FF * rg + 512 * ec + 128 * fc + 128],
                                      XHAT[:, tcol(ec, 512 * tq):tcol(ec, 512 * tq) + 512],
                                      start=(ec == 0), stop=(ec == NCH - 1),
                                      sem=(tdone if ec == NCH - 1 else None))
                            em.wait("scalar", tdone)
                            if fc == 0 and tq == 0 and fg >= 2:
                                em.wait("scalar", hout, h_snap[fg - 2])
                            em.do("scalar", "activation",
                                  HSTG[:, 8192 * rg + 2048 * fc + 512 * tq:8192 * rg + 2048 * fc + 512 * tq + 512],
                                  ph[:, :], AF.Silu, bias=B1[:, 4 * fg + fc:4 * fg + fc + 1],
                                  scale=1.0, sem=sdone)
                    mm_snap[fg] = tdone.n
                    silu_snap = sil_s[fg] = sdone.n
                    if fg + 1 < 8:
                        dq_fc1(fg + 1)
                    em.wait("sync", sdone, silu_snap)
                    for fc in range(4):
                        em.dma("sync", hscr[512 * fg + 128 * fc:512 * fg + 128 * fc + 128, :],
                               HSTG[:, 8192 * rg + 2048 * fc:8192 * rg + 2048 * fc + 2048],
                               sem=hout)
                    h_snap[fg] = hout.n
                    w2_round(fg)
                em.barrier([hout, w2st, dq1, sdone], ["sync", "vector", "scalar", "tensor"])

            with contextlib.ExitStack() as stq:
                sbq = lambda n, s, d=F32: stq.enter_context(nc.sbuf_tensor(n, list(s), d))
                HBUF = sbq("HBUF", (128, 32 * 512), BF16)
                W2F = sbq("W2F", (128, 32 * 1024), BF16)
                pz1 = stq.enter_context(nc.psum_tensor("pz1", [128, 512], F32))
                pz2 = stq.enter_context(nc.psum_tensor("pz2", [128, 512], F32))
                lwg = [em.sem("lwg%d" % g) for g in range(4)]
                lhg = [em.sem("lhg%d" % g) for g in range(4)]
                em.barrier([sdone], ["scalar"])
                warm(AF.Exp)
                for g in range(4):
                    for fc in range(8 * g, 8 * g + 8):
                        em.dma("sync", W2F[:, 1024 * fc:1024 * fc + 1024],
                               w2scr[128 * fc:128 * fc + 128, :], sem=lwg[g])
                for tq in range(4):
                    if tq >= 1:
                        em.wait("sync", tdone)
                    hb_base = [s.n for s in lhg]
                    for g in range(4):
                        em.dma("sync",
                               HBUF[:, 4096 * g:4096 * g + 4096].rearrange(
                                   "p (c n) -> p c n", n=512),
                               hscr[1024 * g:1024 * g + 1024,
                                    512 * tq:512 * tq + 512].rearrange(
                                   "(c p) n -> p c n", p=128),
                               sem=lhg[g])
                    for ec in range(NCH):
                        pz = pz1 if ec % 2 == 0 else pz2
                        if ec >= 2:
                            em.wait("tensor", vdone, vdone.n - 1)
                        for fc in range(32):
                            g = fc // 8
                            if fc % 8 == 0:
                                if tq == 0 and ec == 0:
                                    em.wait("tensor", lwg[g], 8 * 16)
                                if ec == 0:
                                    em.wait("tensor", lhg[g], hb_base[g] + 16)
                            em.mm(pz[:, :],
                                  W2F[:, 1024 * fc + 128 * ec:1024 * fc + 128 * ec + 128],
                                  HBUF[:, 512 * fc:512 * fc + 512],
                                  start=(fc == 0), stop=(fc == 31),
                                  sem=(tdone if fc == 31 else None))
                        em.wait("vector", tdone)
                        sl = STRM[:, tcol(ec, 512 * tq):tcol(ec, 512 * tq) + 512]
                        em.do("vector", "scalar_tensor_tensor", sl, pz[:, :],
                              HB2[:, ec:ec + 1], sl, ALU.add, ALU.add, sem=vdone)
                em.barrier([vdone], ["tensor", "scalar"])

        if STAGE_LIMIT >= 5:
         with contextlib.ExitStack() as stp:
            emit_ln("l2")

        if dev:
            em.barrier([vdone], ["sync"])
            em.dma("sync", dbg["d_strm2"][:, :], STRM[:, :], sem=stdone)
            em.barrier([stdone], ["vector", "tensor", "scalar"])

        if STAGE_LIMIT >= 6:
         with contextlib.ExitStack() as stp:
            sbp = lambda n, s, d=F32: stp.enter_context(nc.sbuf_tensor(n, list(s), d))
            with contextlib.ExitStack() as pre:
                sbr = lambda n, s, d=F32: pre.enter_context(nc.sbuf_tensor(n, list(s), d))
                WDT = sbr("WDT", (128, 128))
                BDR = sbr("BDR", (1, H))
                MEANV = sbr("MEANV", (128, NCH))
                SR = sbr("SR", (1, 64))
                ppd = pre.enter_context(nc.psum_tensor("ppd", [1, 64], F32))
                prep = pre.enter_context(nc.psum_tensor("prep", [128, 32], F32))
                lpd = em.sem("lpd")
                em.barrier([tdone, vdone, sdone], ["sync"])
                em.dma("sync", WDT[:, 0:H * NCH].rearrange("p (c n) -> p c n", n=H),
                       ap["wdT"][:, :].rearrange("(c p) n -> p c n", p=128), sem=lpd)
                em.dma("sync", BDR[:, :], ap["bd_row"][:, :], sem=lpd)
                em.barrier([lpd, vdone], ["vector"])
                for k in range(NCH):
                    em.do("vector", "reduce_sum", MEANV[:, k:k + 1],
                          STRM[:, tcol(k):tcol(k) + T], mybir.AxisListType.X)
                em.do("vector", "engine_nop", sem=vdone)
                em.barrier([vdone], ["tensor"])
                for k in range(NCH):
                    em.mm(ppd[0:1, 0:H], MEANV[:, k:k + 1], WDT[:, H * k:H * k + H],
                          start=(k == 0), stop=(k == NCH - 1),
                          sem=(tdone if k == NCH - 1 else None))
                em.wait("vector", tdone)
                em.do("vector", "tensor_scalar", SR[0:1, 0:H], ppd[0:1, 0:H],
                      CONST[0:1, CIDX[1.0 / T]:CIDX[1.0 / T] + 1], None, ALU.mult)
                em.do("vector", "tensor_tensor", SR[0:1, 0:H], SR[0:1, 0:H], BDR[0:1, :],
                      ALU.add, sem=vdone)
                em.wait("scalar", vdone)
                em.do("scalar", "activation", SR[0:1, 16:32], SR[0:1, 0:16], AF.Exp,
                      scale=-1.0, sem=sdone)
                em.wait("vector", sdone)
                em.do("vector", "tensor_scalar", SR[0:1, 16:32], SR[0:1, 16:32],
                      CONST[0:1, CIDX[1.0]:CIDX[1.0] + 1], None, ALU.add)
                em.do("vector", "reciprocal", out=SR[0:1, 32:48], in_=SR[0:1, 16:32], sem=vdone)
                em.wait("scalar", vdone)
                em.do("scalar", "activation", SR[0:1, 16:32], SR[0:1, 32:48], AF.Exp,
                      scale=float(-8.0 * np.log(2.0)),
                      bias=CONST[0:1, CIDX[-5.0 * np.log(2.0)]:CIDX[-5.0 * np.log(2.0)] + 1],
                      sem=sdone)
                em.wait("vector", sdone)
                em.do("vector", "tensor_scalar", SR[0:1, 16:32], SR[0:1, 16:32],
                      cc1(-1.0), cc1(1.0), ALU.mult, ALU.add, sem=vdone)
                em.wait("scalar", vdone)
                em.do("scalar", "activation", SR[0:1, 0:16], SR[0:1, 16:32], AF.Ln, sem=sdone)
                em.wait("vector", sdone)
                em.do("vector", "tensor_scalar", SR[0:1, 16:32], SR[0:1, 0:16],
                      cc1(-1.0), None, ALU.mult, sem=vdone)
                em.wait("tensor", vdone)
                em.mm(prep[:, :], ONES[0:1, :], SR[0:1, 0:32], sem=tdone)
                em.wait("vector", tdone)
                em.do("vector", "tensor_copy", out=LDS[:, :], in_=prep[:, :], sem=vdone)
                em.barrier([vdone, tdone, sdone], ["sync"])
                em.dma("sync", strmscr[:, :], STRM[:, :], sem=stdone)
                em.barrier([stdone], ["vector", "tensor", "scalar", "gpsimd"])
            STRMSTK[0].close()

            GON = sbp("GON", (128, NCH * T), BF16)
            GAMT = sbp("GAMT", (128, H // 2))
            BETT = sbp("BETT", (128, H // 2))
            BQ = sbp("BQl", (128, NCH))
            BK = sbp("BKl", (128, NCH))
            BO = sbp("BOl", (128, NCH))
            BVREP = sbp("BVREP", (128, D), BF16)
            BVR = sbp("BVR", (1, D))
            IOTAR = sbp("IOTAR", (1, T))
            MPATS = sbp("MPATS", (128, 1024), BF16)
            COSB = sbp("COSB", (128, T), BF16)
            SINF = sbp("SINF", (128, T), BF16)
            BVALL = sbp("BVALL", (128, H * 16))
            IOT2 = sbp("IOT2", (128, 16))
            la = em.sem("la")
            lg = em.sem("lg")
            for nm, buf in [("gamT", GAMT), ("betT", BETT), ("bq", BQ), ("bk", BK),
                            ("bo", BO), ("bv_row", BVR), ("c_IOTAROW", IOTAR),
                            ("c_IOTA2D", IOT2)]:
                em.dma("sync", buf[:, :], ap[nm][:, :], sem=la)
            em.dma("gpsimd", SINF[:, :], ap["c_SINQ"][:, :], sem=lg)
            em.dma("gpsimd", MPATS[:, :], ap["c_MPAT"][:, :], sem=lg)
            em.dma("gpsimd", COSB[:, :], ap["c_COSQ"][:, :], sem=lg)
            em.barrier([la, lg], ["vector"])
            for h in range(H):
                em.do("vector", "tensor_scalar", BVALL[:, 16 * h:16 * h + 16],
                      IOT2[:, :], LDS[:, h:h + 1], None, ALU.mult)
            em.do("vector", "engine_nop", sem=vdone)

            with contextlib.ExitStack() as stg:
                WGB = stg.enter_context(nc.sbuf_tensor("WGB", [128, NCH * 1024], BF16))
                BG = stg.enter_context(nc.sbuf_tensor("BGl", [128, NCH], F32))
                pg1 = stg.enter_context(nc.psum_tensor("pg1", [128, 512], F32))
                pg2 = stg.enter_context(nc.psum_tensor("pg2", [128, 512], F32))
                lw = em.sem("lw")
                em.dma("sync", BG[:, :], ap["bg"][:, :], sem=la)
                for dk in range(NCH):
                    em.dma("gpsimd", WGB[:, 1024 * dk:1024 * dk + 1024],
                           ap["wgT"][128 * dk:128 * dk + 128, :], sem=lw)
                em.barrier([la, lw], ["tensor", "scalar"])
                warm(AF.Silu)
                for eo in range(NCH):
                    for tq in range(4):
                        pg = pg1 if (eo * 4 + tq) % 2 == 0 else pg2
                        if eo or tq:
                            em.wait("tensor", sdone, sdone.n - 1)
                        for ei in range(NCH):
                            em.mm(pg[:, :], WGB[:, 1024 * ei + 128 * eo:1024 * ei + 128 * eo + 128],
                                  XHAT[:, tcol(ei, 512 * tq):tcol(ei, 512 * tq) + 512],
                                  start=(ei == 0), stop=(ei == NCH - 1),
                                  sem=(tdone if ei == NCH - 1 else None))
                        em.wait("scalar", tdone)
                        em.do("scalar", "activation", GON[:, 2048 * eo + 512 * tq:2048 * eo + 512 * tq + 512],
                              pg[:, :], AF.Silu, bias=BG[:, eo:eo + 1], scale=1.0, sem=sdone)
                em.barrier([sdone], ["tensor", "vector", "scalar", "gpsimd"])
                warm(AF.Exp)

            with contextlib.ExitStack() as stb:
                WQB = stb.enter_context(nc.sbuf_tensor("WQB", [128, NCH * 1024], BF16))
                WKB = stb.enter_context(nc.sbuf_tensor("WKB", [128, NCH * 1024], BF16))
                WVB = stb.enter_context(nc.sbuf_tensor("WVB", [128, NCH * 1024], BF16))
                lw2a = em.sem("lw2a")
                for nm, buf in [("wqT", WQB), ("wkT", WKB), ("wvT", WVB)]:
                    for dk in range(NCH):
                        em.dma("gpsimd", buf[:, 1024 * dk:1024 * dk + 1024],
                               ap[nm][128 * dk:128 * dk + 128, :], sem=lw2a)

                with contextlib.ExitStack() as stc:
                    sbc = lambda n, s, d=F32: stc.enter_context(nc.sbuf_tensor(n, list(s), d))
                    VH = sbc("VH", (128, 16 * 1040), BF16)
                    QF = sbc("QF", (128, T), BF16)
                    KF = sbc("KF", (128, T), BF16)
                    QRAW = sbc("QRAW", (128, T), BF16)
                    T2S = sbc("T2S", (128, T), BF16)
                    PRB = sbc("PRB", (128, 3 * 1024), BF16)
                    MH = sbc("MH", (128, 1024), BF16)
                    OG = sbc("OG", (128, T), BF16)
                    CMB = sbc("CMB", (65, 1024))
                    FREP = sbc("FREP", (65, 1024), BF16)
                    ACC = sbc("ACC", (128, 4))
                    REPM = sbc("REPM", (128, 2))
                    SROW = sbc("SROW", (1, 8))
                    pA = stc.enter_context(nc.psum_tensor("pA", [128, 1024], F32))
                    pB = stc.enter_context(nc.psum_tensor("pB", [128, 1024], F32))
                    pP = stc.enter_context(nc.psum_tensor("pP", [65, 1024], F32))
                    pF = stc.enter_context(nc.psum_tensor("pF", [65, 1024], F32))

                    em.barrier([la, vdone], ["tensor"])
                    em.mm(pA[:, 0:512], ONES[0:1, :], BVR[0:1, 0:512])
                    em.mm(pA[:, 512:1024], ONES[0:1, :], BVR[0:1, 512:1024], sem=tdone)
                    em.wait("vector", tdone)
                    em.do("vector", "tensor_copy", out=BVREP[:, :], in_=pA[:, :])
                    em.do("vector", "memset",
                          VH[:, :].rearrange("p (tc h c) -> p tc h c", h=16, c=65)[:, :, :, 64:65], 1.0)
                    em.do("vector", "engine_nop", sem=vdone)
                    em.barrier([lw2a, vdone], ["tensor"])
                    vvd0 = vdone.n
                    for tc in range(16):
                        pv = pA if tc % 2 == 0 else pB
                        if tc >= 2:
                            em.wait("tensor", vdone, vvd0 + tc - 1)
                        for dvh in range(2):
                            for ec in range(NCH):
                                em.mm(pv[:, 512 * dvh:512 * dvh + 512],
                                      XHAT[:, tcol(ec, 128 * tc):tcol(ec, 128 * tc) + 128],
                                      WVB[:, 1024 * ec + 512 * dvh:1024 * ec + 512 * dvh + 512],
                                      start=(ec == 0), stop=(ec == NCH - 1),
                                      sem=(tdone if (ec == NCH - 1 and dvh == 1) else None))
                        em.wait("vector", tdone)
                        em.do("vector", "tensor_tensor",
                              VH[:, 1040 * tc:1040 * tc + 1040].rearrange("p (h c) -> p h c", c=65)[:, :, 0:64],
                              pv[:, :].rearrange("p (h c) -> p h c", c=64),
                              BVREP[:, :].rearrange("p (h c) -> p h c", c=64),
                              ALU.add, sem=vdone)

                    mh_gate = [vdone.n]
                    for hp in range(8):
                        for (WB, BB, OUT) in ((WQB, BQ, QF), (WKB, BK, KF)):
                            em.barrier([vdone, sdone], ["tensor"])
                            for qt in range(4):
                                pq = pA if qt % 2 == 0 else pB
                                psl = pq[:, 0:512]
                                if qt >= 2:
                                    em.wait("tensor", vdone, vdone.n - 1)
                                for ec in range(NCH):
                                    em.mm(psl, WB[:, 1024 * ec + 128 * hp:1024 * ec + 128 * hp + 128],
                                          XHAT[:, tcol(ec, 512 * qt):tcol(ec, 512 * qt) + 512],
                                          start=(ec == 0), stop=(ec == NCH - 1),
                                          sem=(tdone if ec == NCH - 1 else None))
                                em.wait("vector", tdone)
                                em.do("vector", "tensor_scalar", QRAW[:, 512 * qt:512 * qt + 512],
                                      psl, BB[:, hp:hp + 1], None, ALU.add, sem=vdone)
                            em.barrier([vdone], ["tensor"])
                            em.do("vector", "tensor_tensor", OUT[:, :], QRAW[:, :], COSB[:, :],
                                  ALU.mult)
                            for qt in range(4):
                                pq = pA if qt % 2 == 0 else pB
                                if qt >= 2:
                                    em.wait("tensor", vdone, vdone.n - 1)
                                em.mm(pq[:, 0:512], R2T[:, :], QRAW[:, 512 * qt:512 * qt + 512],
                                      sem=tdone)
                                em.wait("vector", tdone)
                                em.do("vector", "tensor_tensor", T2S[:, 512 * qt:512 * qt + 512],
                                      pq[:, 0:512], SINF[:, 512 * qt:512 * qt + 512], ALU.mult,
                                      sem=vdone)
                            em.do("vector", "tensor_tensor", OUT[:, :], OUT[:, :], T2S[:, :],
                                  ALU.add, sem=vdone)
                        vsnap_kf = vdone.n
                        for hi in range(2):
                            h = 2 * hp + hi
                            hh = 64 * hi
                            em.wait("scalar", vdone, mh_gate[0])
                            em.do("scalar", "activation", MH[:, :], MPATS[:, :], AF.Exp,
                                  scale=LDS[:, h:h + 1], sem=sdone)
                            for qh in range(2):
                                q0 = 1024 * qh
                                first_fut = 8 * qh
                                ssnap = {}
                                avsnap = {}
                                esnap = {}
                                vsnap0 = vdone.n

                                def emit_scores(kt):
                                    pS = pA if kt % 2 == 0 else pB
                                    if kt == 0 and hi == 0 and qh == 0:
                                        em.wait("tensor", vdone, vsnap_kf)
                                    if kt == 1:
                                        em.wait("tensor", vdone, vsnap0)
                                    if kt >= 2:
                                        em.wait("tensor", sdone, esnap[kt - 2])
                                    for j in range(2):
                                        em.mm(pS[:, 512 * j:512 * j + 512],
                                              KF[hh:hh + 64, 128 * kt:128 * kt + 128],
                                              QF[hh:hh + 64, q0 + 512 * j:q0 + 512 * j + 512],
                                              sem=(tdone if j == 1 else None))
                                    ssnap[kt] = tdone.n

                                emit_scores(0)
                                for kt in range(16):
                                    pS = pA if kt % 2 == 0 else pB
                                    r = kt % 3
                                    if kt >= 3:
                                        em.wait("scalar", tdone, avsnap[kt - 3])
                                    em.wait("scalar", tdone, ssnap[kt])
                                    if kt >= first_fut:
                                        em.do("scalar", "activation", PRB[:, 1024 * r:1024 * r + 1024],
                                              pS[:, :], AF.Exp, bias=BVALL[:, 16 * h + kt:16 * h + kt + 1],
                                              scale=0.125, sem=sdone)
                                    else:
                                        em.do("scalar", "activation", PRB[:, 1024 * r:1024 * r + 1024],
                                              pS[:, :], AF.Exp, scale=0.125, sem=sdone)
                                    esnap[kt] = sdone.n
                                    off = 128 * kt - q0
                                    diag = 0 <= off < 1024
                                    if diag:
                                        em.wait("vector", sdone, esnap[kt])
                                        em.do("vector", "tensor_tensor",
                                              PRB[:, 1024 * r + off:1024 * r + 1024],
                                              PRB[:, 1024 * r + off:1024 * r + 1024],
                                              MH[:, 0:1024 - off], ALU.mult, sem=vdone)
                                    if kt + 1 < 16:
                                        emit_scores(kt + 1)
                                    if diag:
                                        em.wait("tensor", vdone)
                                    else:
                                        em.wait("tensor", sdone, esnap[kt])
                                    pT = pF if kt >= first_fut else pP
                                    st_flag = (kt == first_fut) or (qh == 1 and kt == 0)
                                    for j in range(2):
                                        em.mm(pT[:, 512 * j:512 * j + 512],
                                              VH[:, 1040 * kt + 65 * h:1040 * kt + 65 * h + 65],
                                              PRB[:, 1024 * r + 512 * j:1024 * r + 512 * j + 512],
                                              start=st_flag, stop=(kt == 15 or (kt == first_fut - 1)),
                                              sem=(tdone if j == 1 else None))
                                    avsnap[kt] = tdone.n
                                if qh == 0:
                                    em.wait("scalar", tdone)
                                    em.do("scalar", "activation", CMB[64:65, :], pF[64:65, :],
                                          AF.Ln, sem=sdone)
                                    em.wait("tensor", sdone)
                                    em.mm(pB[0:64, 0:512], ONES[64:65, 0:64], CMB[64:65, 0:512])
                                    em.mm(pB[0:64, 512:1024], ONES[64:65, 0:64],
                                          CMB[64:65, 512:1024], sem=tdone)
                                    em.wait("scalar", tdone)
                                    em.do("scalar", "activation", FREP[0:64, :], pB[0:64, :],
                                          AF.Exp, scale=-1.0, sem=sdone)
                                    em.wait("vector", sdone)
                                    em.do("vector", "tensor_tensor",
                                          OG[64 * hi:64 * hi + 64, q0:q0 + 1024],
                                          pF[0:64, :], FREP[0:64, :], ALU.mult, sem=vdone)
                                else:
                                    em.mm(pA[0:65, 0:512], ONES[0:1, 0:65], IOTAR[0:1, q0:q0 + 512])
                                    em.mm(pA[0:65, 512:1024], ONES[0:1, 0:65],
                                          IOTAR[0:1, q0 + 512:q0 + 1024], sem=tdone)
                                    em.wait("scalar", tdone)
                                    em.do("scalar", "activation", FREP[:, :], pA[0:65, :], AF.Exp,
                                          scale=LDS[0:65, 16 + h:17 + h], sem=sdone)
                                    em.wait("vector", sdone)
                                    em.do("vector", "tensor_tensor", CMB[:, :], pF[:, :], FREP[:, :],
                                          ALU.mult)
                                    em.do("vector", "tensor_tensor", CMB[:, :], CMB[:, :], pP[:, :],
                                          ALU.add, sem=vdone)
                                    em.wait("scalar", vdone)
                                    em.do("scalar", "activation", CMB[64:65, :], CMB[64:65, :],
                                          AF.Ln, sem=sdone)
                                    em.wait("tensor", sdone)
                                    em.mm(pB[0:64, 0:512], ONES[64:65, 0:64], CMB[64:65, 0:512])
                                    em.mm(pB[0:64, 512:1024], ONES[64:65, 0:64],
                                          CMB[64:65, 512:1024], sem=tdone)
                                    em.wait("scalar", tdone)
                                    em.do("scalar", "activation", FREP[0:64, :], pB[0:64, :],
                                          AF.Exp, scale=-1.0, sem=sdone)
                                    em.wait("vector", sdone)
                                    em.do("vector", "tensor_tensor",
                                          OG[64 * hi:64 * hi + 64, q0:q0 + 1024],
                                          CMB[0:64, :], FREP[0:64, :], ALU.mult, sem=vdone)
                            mh_gate[0] = vdone.n
                            if hi == 1:
                                em.barrier([vdone], ["scalar"])
                                ONF = QRAW[:, :]
                                em.do("scalar", "activation", ONF, OG[:, :], AF.Square,
                                      accum_out=ACC[:, 0:1])
                                em.do("scalar", "activation", ONF, OG[:, :], AF.Copy,
                                      accum_out=ACC[:, 1:2], sem=sdone)
                                em.wait("tensor", sdone)
                                em.mm(pA[0:64, 0:2], ONES[0:64, 0:64], ACC[0:64, 0:2],
                                      tp=(0, 0))
                                em.mm(pA[64:128, 512:514], ONES[64:128, 0:64],
                                      ACC[64:128, 0:2], tp=(64, 64), sem=tdone)
                                em.wait("vector", tdone)
                                em.do("vector", "tensor_scalar", REPM[0:64, 0:2], pA[0:64, 0:2],
                                      CONST[0:64, CIDX[1.0 / (T * HD)]:CIDX[1.0 / (T * HD)] + 1],
                                      None, ALU.mult)
                                em.do("vector", "tensor_scalar", REPM[64:128, 0:2],
                                      pA[64:128, 512:514],
                                      CONST[64:128, CIDX[1.0 / (T * HD)]:CIDX[1.0 / (T * HD)] + 1],
                                      None, ALU.mult, nofence=True)
                                em.do("vector", "tensor_tensor", ACC[:, 2:3], REPM[:, 1:2],
                                      REPM[:, 1:2], ALU.mult)
                                em.do("vector", "tensor_sub", ACC[:, 2:3], REPM[:, 0:1],
                                      ACC[:, 2:3], sem=vdone)
                                em.wait("scalar", vdone)
                                em.do("scalar", "activation", ACC[:, 3:4], ACC[:, 2:3], AF.Ln,
                                      bias=cc(1e-5), scale=1.0)
                                em.do("scalar", "activation", ACC[:, 3:4], ACC[:, 3:4], AF.Exp,
                                      scale=-0.5, sem=sdone)
                                em.wait("vector", sdone)
                                em.do("vector", "tensor_tensor", ACC[:, 2:3], ACC[:, 3:4],
                                      GAMT[:, hp:hp + 1], ALU.mult)
                                em.do("vector", "tensor_tensor", ACC[:, 3:4], REPM[:, 1:2],
                                      ACC[:, 2:3], ALU.mult)
                                em.do("vector", "tensor_sub", ACC[:, 3:4], BETT[:, hp:hp + 1],
                                      ACC[:, 3:4])
                                em.do("vector", "tensor_scalar", ONF, OG[:, :],
                                      ACC[:, 2:3], ACC[:, 3:4], ALU.mult, ALU.add, sem=vdone)
                                em.do("vector", "tensor_tensor",
                                      GON[:, 2048 * hp:2048 * hp + 2048],
                                      GON[:, 2048 * hp:2048 * hp + 2048], ONF, ALU.mult,
                                      sem=vdone)
                pass
            with contextlib.ExitStack() as stw:
                WOB = stw.enter_context(nc.sbuf_tensor("WOB", [128, NCH * 1024], BF16))
                DSTG = stw.enter_context(nc.sbuf_tensor("DSTG", [128, 2 * T], F32))
                pw1 = stw.enter_context(nc.psum_tensor("pwo1", [128, 512], F32))
                pw2 = stw.enter_context(nc.psum_tensor("pwo2", [128, 512], F32))
                acst = em.sem("acst")
                em.barrier([vdone, tdone, sdone], ["gpsimd"])
                for dk in range(NCH):
                    em.dma("gpsimd", WOB[:, 1024 * dk:1024 * dk + 1024],
                           ap["woT"][128 * dk:128 * dk + 128, :], sem=lg)
                em.barrier([lg], ["tensor"])
                for eo in range(NCH):
                    rg = eo % 2
                    if eo >= 2:
                        em.wait("tensor", acst)
                    for tq in range(4):
                        pw = pw1 if (eo * 4 + tq) % 2 == 0 else pw2
                        if eo or tq:
                            em.wait("tensor", vdone, vdone.n - 1)
                        for ei in range(NCH):
                            em.mm(pw[:, :], WOB[:, 1024 * ei + 128 * eo:1024 * ei + 128 * eo + 128],
                                  GON[:, 2048 * ei + 512 * tq:2048 * ei + 512 * tq + 512],
                                  start=(ei == 0), stop=(ei == NCH - 1),
                                  sem=(tdone if ei == NCH - 1 else None))
                        em.wait("vector", tdone)
                        em.do("vector", "tensor_scalar",
                              DSTG[:, T * rg + 512 * tq:T * rg + 512 * tq + 512],
                              pw[:, :], BO[:, eo:eo + 1], None, ALU.add, sem=vdone)
                    em.wait("gpsimd", vdone)
                    em.do("gpsimd", "dma_start",
                          out=strmscr[:, CW * eo + 1:CW * eo + 1 + T],
                          in_=DSTG[:, T * rg:T * rg + T],
                          accum_op=ALU.add, sem=acst, inc=16)
                em.barrier([acst, vdone], ["sync", "tensor", "vector", "scalar"])
         strm_st2 = contextlib.ExitStack()
         STRM = strm_st2.enter_context(nc.sbuf_tensor("STRM2", [128, NCH * CW], F32))
         STRMSTK[0] = strm_st2
         rst = em.sem("rst")
         em.dma("sync", STRM[:, :], strmscr[:, :], sem=rst)
         em.barrier([rst], ["tensor", "vector", "scalar"])

        if STAGE_LIMIT >= 7:
         with contextlib.ExitStack() as stp:
            emit_ln("l3")

        if dev:
            em.barrier([vdone], ["sync"])
            em.dma("sync", dbg["d_strm3"][:, :], STRM[:, :], sem=stdone)
            em.barrier([stdone], ["vector", "tensor", "scalar"])

        if STAGE_LIMIT >= 8:
         with contextlib.ExitStack() as stp:
            sbp = lambda n, s, d=F32: stp.enter_context(nc.sbuf_tensor(n, list(s), d))
            XK = sbp("XK", (128, NCH * CW), BF16)
            RHAT = sbp("RHAT", (128, NCH * T), BF16)
            MUR = sbp("MUR", (128, NCH))
            MUK = sbp("MUK", (128, NCH))
            DTMP = sbp("DTMP", (128, T))
            ZSC = sbp("ZSC", (128, 512))
            WRB = sbp("WRB", (128, NCH * 1024), BF16)
            lf = em.sem("lf")
            lfw = em.sem("lfw")
            em.barrier([tdone, vdone, sdone], ["sync", "gpsimd"])
            em.dma("sync", MUR[:, :], ap["mur"][:, :], sem=lf)
            em.dma("sync", MUK[:, :], ap["muk"][:, :], sem=lf)
            for dk in range(NCH):
                em.dma("gpsimd", WRB[:, 1024 * dk:1024 * dk + 1024],
                       ap["WrT"][128 * dk:128 * dk + 128, :], sem=lfw)
            em.barrier([lf, vdone], ["vector"])
            for k in range(NCH):
                sh = STRM[:, tcol(k, -1):tcol(k, -1) + T]
                cu = STRM[:, tcol(k):tcol(k) + T]
                em.do("vector", "tensor_sub", DTMP[:, :], cu, sh)
                em.do("vector", "scalar_tensor_tensor", XHAT[:, tcol(k):tcol(k) + T],
                      DTMP[:, :], MUR[:, k:k + 1], sh, ALU.mult, ALU.add)
                em.do("vector", "scalar_tensor_tensor", XK[:, tcol(k):tcol(k) + T],
                      DTMP[:, :], MUK[:, k:k + 1], sh, ALU.mult, ALU.add)
            em.do("vector", "engine_nop", sem=vdone)
            em.barrier([vdone], ["scalar"])
            warm(AF.Sigmoid)
            with contextlib.ExitStack() as stq:
                pr1 = stq.enter_context(nc.psum_tensor("pr1", [128, 512], F32))
                pr2 = stq.enter_context(nc.psum_tensor("pr2", [128, 512], F32))
                em.barrier([lfw, vdone], ["tensor"])
                for eo in range(NCH):
                    for tq in range(4):
                        pr = pr1 if (eo * 4 + tq) % 2 == 0 else pr2
                        if eo or tq:
                            em.wait("tensor", sdone, sdone.n - 1)
                        for ei in range(NCH):
                            em.mm(pr[:, :], WRB[:, 1024 * ei + 128 * eo:1024 * ei + 128 * eo + 128],
                                  XHAT[:, tcol(ei, 512 * tq):tcol(ei, 512 * tq) + 512],
                                  start=(ei == 0), stop=(ei == NCH - 1),
                                  sem=(tdone if ei == NCH - 1 else None))
                        em.wait("scalar", tdone)
                        em.do("scalar", "activation", RHAT[:, 2048 * eo + 512 * tq:2048 * eo + 512 * tq + 512],
                              pr[:, :], AF.Sigmoid, sem=sdone)
                em.barrier([sdone], ["gpsimd"])
                for dk in range(NCH):
                    em.dma("gpsimd", WRB[:, 1024 * dk:1024 * dk + 1024],
                           ap["WkT"][128 * dk:128 * dk + 128, :], sem=lfw)
                em.barrier([lfw], ["tensor"])
                for eo in range(NCH):
                    for tq in range(4):
                        pr = pr1 if (eo * 4 + tq) % 2 == 0 else pr2
                        em.wait("tensor", sdone, sdone.n - 1)
                        for ei in range(NCH):
                            em.mm(pr[:, :], WRB[:, 1024 * ei + 128 * eo:1024 * ei + 128 * eo + 128],
                                  XK[:, tcol(ei, 512 * tq):tcol(ei, 512 * tq) + 512],
                                  start=(ei == 0), stop=(ei == NCH - 1),
                                  sem=(tdone if ei == NCH - 1 else None))
                        em.wait("scalar", tdone)
                        em.do("scalar", "activation", ZSC[:, :], pr[:, :], AF.Relu)
                        em.do("scalar", "activation",
                              XHAT[:, tcol(eo, 512 * tq):tcol(eo, 512 * tq) + 512],
                              ZSC[:, :], AF.Square, sem=sdone)
                em.barrier([sdone], ["gpsimd"])
                for dk in range(NCH):
                    em.dma("gpsimd", WRB[:, 1024 * dk:1024 * dk + 1024],
                           ap["WvT"][128 * dk:128 * dk + 128, :], sem=lfw)
                em.barrier([lfw, sdone], ["tensor"])
                for eo in range(NCH):
                    for tq in range(4):
                        pr = pr1 if (eo * 4 + tq) % 2 == 0 else pr2
                        if eo or tq:
                            em.wait("tensor", vdone, vdone.n - 1)
                        for ei in range(NCH):
                            em.mm(pr[:, :], WRB[:, 1024 * ei + 128 * eo:1024 * ei + 128 * eo + 128],
                                  XHAT[:, tcol(ei, 512 * tq):tcol(ei, 512 * tq) + 512],
                                  start=(ei == 0), stop=(ei == NCH - 1),
                                  sem=(tdone if ei == NCH - 1 else None))
                        em.wait("vector", tdone)
                        em.do("vector", "tensor_tensor", DTMP[:, 0:512], pr[:, :],
                              RHAT[:, 2048 * eo + 512 * tq:2048 * eo + 512 * tq + 512], ALU.mult)
                        sl = STRM[:, tcol(eo, 512 * tq):tcol(eo, 512 * tq) + 512]
                        em.do("vector", "tensor_add", sl, sl, DTMP[:, 0:512], sem=vdone)
                em.barrier([vdone], ["tensor", "scalar"])
                warm(AF.Exp)

        if STAGE_LIMIT >= 9:
         with contextlib.ExitStack() as stp:
            emit_ln("l4", write_xhat=False)

        if dev:
            em.barrier([vdone], ["sync"])
            em.dma("sync", dbg["d_strm4"][:, :], STRM[:, :], sem=stdone)
            em.barrier([stdone], ["vector", "tensor", "scalar"])

        with contextlib.ExitStack() as stp:
            sbp = lambda n, s, d=F32: stp.enter_context(nc.sbuf_tensor(n, list(s), d))
            psp = lambda n, s: stp.enter_context(nc.psum_tensor(n, list(s), F32))
            obuf = sbp("obuf", (128, 2 * D))
            pto = psp("pto", (128, 1024))
            pto2 = psp("pto2", (128, 1024))
            em.barrier([vdone], ["tensor"])
            td0 = tdone.n
            vd0 = vdone.n
            st0c = stdone.n
            for tc in range(16):
                pt = pto if tc % 2 == 0 else pto2
                if tc >= 2:
                    em.wait("tensor", vdone, vd0 + tc - 1)
                for ec in range(NCH):
                    em.do("tensor", "transpose",
                          pt[:, 128 * ec:128 * ec + 128],
                          STRM[:, tcol(ec, 128 * tc):tcol(ec, 128 * tc) + 128],
                          ID[:, :], sem=(tdone if ec == NCH - 1 else None))
                em.wait("vector", tdone, td0 + tc + 1)
                if tc >= 2:
                    em.wait("vector", stdone)
                em.do("vector", "tensor_copy", out=obuf[:, (tc % 2) * D:(tc % 2) * D + D],
                      in_=pt[:, :], sem=vdone)
                em.wait("sync", vdone, vd0 + tc + 1)
                em.dma("sync", y[128 * tc:128 * tc + 128, :],
                       obuf[:, (tc % 2) * D:(tc % 2) * D + D], sem=stdone)
            em.wait("sync", stdone)

        STRMSTK[0].close()

        block = st0.enter_context(nc.Block())
        for eng_name, meth in [("sync", block.sync), ("gpsimd", block.gpsimd),
                               ("tensor", block.tensor), ("vector", block.vector),
                               ("scalar", block.scalar)]:
            def body(e, _ops=em.ops[eng_name]):
                for fn in _ops:
                    fn(e)
            meth(body)

    return nc


_CACHE = {}


def kernel(**inputs):
    n = 8
    key = "nc"
    if key not in _CACHE:
        _CACHE[key] = build_nc(DEV)
    nc = _CACHE[key]
    in_maps = [host_inputs(inputs, b) for b in range(n)]
    res = run_bass_kernel_spmd(nc, in_maps, core_ids=list(range(n)))
    out = np.stack([res.results[b]["y"] for b in range(n)], 0)
    if DEV:
        kernel._dev_results = res.results
    return out


# version 25 (final)
# speedup vs baseline: 1.0195x; 1.0103x over previous
import contextlib
import numpy as np
import ml_dtypes

import concourse.bass as bass
import concourse.mybir as mybir
from concourse.bass_utils import run_bass_kernel_spmd

F32 = mybir.dt.float32
BF16 = mybir.dt.bfloat16
U8 = mybir.dt.uint8
AF = mybir.ActivationFunctionType
ALU = mybir.AluOpType

T = 2048
D = 1024
NCH = 8
CW = 2049
H = 16
HD = 64
SC = 64
NSC = 32
FF = 4096

DEV = False
STAGE_LIMIT = 99
L0SUB = 4

ENGS = ("sync", "gpsimd", "tensor", "vector", "scalar")


def tcol(k, t0=0):
    return CW * k + 1 + t0


class Sem:
    __slots__ = ("h", "n", "name")

    def __init__(self, h, name):
        self.h = h
        self.n = 0
        self.name = name


class Em:

    def __init__(self, nc, stack):
        self.nc = nc
        self.stack = stack
        self.ops = {e: [] for e in ENGS}
        self._semi = 0
        self._last_out = {}

    def sem(self, name=None):
        self._semi += 1
        h = self.stack.enter_context(self.nc.semaphore(name or f"s{self._semi}"))
        return Sem(h, name or f"s{self._semi}")

    def emit(self, eng, fn):
        self.ops[eng].append(fn)

    def wait(self, eng, sem, val=None):
        v = sem.n if val is None else val
        self.emit(eng, lambda e, h=sem.h, v=v: e.wait_ge(h, v))

    _FENCED = ("vector", "scalar", "gpsimd")
    _NOFENCE_METHS = ("nop", "drain", "sem_inc", "wait_ge", "dma_start")

    def _ap_tensors(self, vals):
        names = set()
        for v in vals:
            t = getattr(v, "tensor", None)
            if t is not None:
                names.add(t.name)
        return names

    def do(self, eng, meth, *a, sem=None, inc=1, nofence=False, **kw):
        if sem is not None:
            sem.n += inc
        fence_capable = eng in self._FENCED and meth not in self._NOFENCE_METHS
        need_fence = False
        if eng in self._FENCED and meth not in ("nop", "drain", "sem_inc", "wait_ge"):
            outs = self._ap_tensors([kw.get("out"), a[0] if a else None])
            uses = self._ap_tensors(list(a) + list(kw.values()))
            hist = self._last_out.setdefault(eng, [])
            if any(prev & uses for prev in hist) and not nofence:
                need_fence = True
            if fence_capable:
                hist.append(outs)
            else:
                hist.append(set())
            del hist[:-2]

        def fn(e, meth=meth, a=a, kw=kw, s=sem, inc=inc, pre=need_fence):
            if pre:
                e.drain()
            ins = getattr(e, meth)(*a, **kw)
            if s is not None:
                ins.then_inc(s.h, inc)

        self.emit(eng, fn)

    def dma(self, eng, out, in_, sem=None):
        self.do(eng, "dma_start", out=out, in_=in_, sem=sem, inc=16)

    def mm(self, out, lhsT, rhs, start=True, stop=True, tp=None, sem=None):
        kw = dict(start=start, stop=stop)
        if tp is not None:
            kw["tile_position"] = tp
        self.do("tensor", "matmul", out, lhsT, rhs, sem=sem, **kw)

    def barrier(self, sems_from, engs_to):
        for e in engs_to:
            for s in sems_from:
                self.wait(e, s)


def host_constants():
    c = {}
    c["ID128"] = np.eye(128, dtype=np.float32)
    c["ONES"] = np.ones((128, 128), np.float32)
    r = np.zeros((64, 64), np.float32)
    for i in range(32):
        r[2 * i, 2 * i + 1] = -1.0
        r[2 * i + 1, 2 * i] = 1.0
    R = np.zeros((128, 128), np.float32)
    R[0:64, 0:64] = r
    R[64:128, 64:128] = r
    c["R2T"] = np.ascontiguousarray(R.T)
    inv = 1.0 / (10000.0 ** (np.arange(0, HD, 2, dtype=np.float64) / HD))
    tt = np.arange(T, dtype=np.float64)
    ang = tt[None, :] * inv[(np.arange(64) % 32)][:, None]
    cos = np.cos(ang).astype(np.float32)
    sin = np.sin(ang).astype(np.float32)
    c["COSQ"] = np.concatenate([cos, cos], 0)
    c["SINQ"] = np.concatenate([sin, sin], 0)
    p64 = np.arange(128) % 64
    c1 = np.arange(64)
    c["DELTA64"] = (c1[None, :] - 1 - p64[:, None]).astype(np.float32)
    c["TRIMASK64"] = (c1[None, :] > p64[:, None]).astype(np.float32)
    c["C1PAT"] = np.broadcast_to(c1[None, :], (128, 64)).astype(np.float32)
    c["P63"] = (63.0 - p64).astype(np.float32).reshape(128, 1)
    c["IOTA2D"] = (np.arange(128)[:, None] + 128.0 * np.arange(16)[None, :]).astype(np.float32)
    c["IOTAROW"] = np.arange(T, dtype=np.float32).reshape(1, T)
    u = np.arange(1024, dtype=np.float32)
    c["MPAT"] = np.maximum(u[None, :] - np.arange(128)[:, None], 0.0).astype(np.float32)
    vals = [0.0, 1.0, -1.0, 2.0, 0.5, 1.0 / D, 1e-5, 1.0 / (T * HD),
            -8.0 * np.log(2.0), -5.0 * np.log(2.0), 1.0 / 8.0, 64.0, 1.0 / T]
    c["IDUP"] = np.zeros((128, 128), np.float32)
    for i in range(64):
        c["IDUP"][i, 64 + i] = 1.0
    c["CONSTCOL"] = np.stack([np.full(128, v, np.float32) for v in vals], 1)
    c["_constidx"] = {v: i for i, v in enumerate(vals)}
    return c


_CONSTS = host_constants()
CIDX = _CONSTS["_constidx"]


def host_inputs(inputs, b):
    d = {}
    d["x"] = np.ascontiguousarray(inputs["L0_attn_out"][b]).astype(np.float32, copy=False)
    tr = lambda w: np.ascontiguousarray(np.asarray(w).T)
    d["WwT"] = tr(inputs["L0_attn_W_write"])
    d["WreadT"] = tr(inputs["L0_attn_W_read"])
    d["decay_row"] = np.asarray(inputs["L0_attn_decay"]).reshape(1, H)
    d["logalpha_row"] = np.asarray(inputs["L0_attn_log_alpha"]).reshape(1, H)
    d["ln0g"] = np.ascontiguousarray(np.asarray(inputs["L0_ffn_ln_gamma"]).reshape(NCH, 128).T)
    d["ln0b"] = np.ascontiguousarray(np.asarray(inputs["L0_ffn_ln_beta"]).reshape(NCH, 128).T)
    f1q = np.asarray(inputs["L0_ffn_fc1_q"]).astype(np.uint8)
    d["fc1q"] = np.ascontiguousarray(f1q.reshape(D, 8, 512).transpose(1, 0, 2))
    d["fc2q"] = np.asarray(inputs["L0_ffn_fc2_q"]).astype(np.uint8)
    f1s = np.repeat(np.asarray(inputs["L0_ffn_fc1_scales"]), 128, 0).astype(np.float32)
    d["fc1s"] = np.ascontiguousarray(f1s.reshape(D, 8, 512).transpose(1, 0, 2))
    d["fc2s"] = np.ascontiguousarray(np.repeat(np.asarray(inputs["L0_ffn_fc2_scales"]), 128, 0).astype(np.float32))
    d["b1"] = np.ascontiguousarray(np.asarray(inputs["L0_ffn_fc1_bias"]).reshape(FF // 128, 128).T)
    d["b2"] = np.ascontiguousarray(np.asarray(inputs["L0_ffn_fc2_bias"]).reshape(NCH, 128).T)
    for nm in ("wq", "wk", "wv", "wg", "wo"):
        d[nm + "T"] = tr(inputs[f"L1_attn_{nm}"])
    for nm in ("bq", "bk", "bv", "bg", "bo"):
        d[nm] = np.ascontiguousarray(np.asarray(inputs[f"L1_attn_{nm}"]).reshape(NCH, 128).T)
    d["bv_row"] = np.asarray(inputs["L1_attn_bv"]).reshape(1, D)
    d["wdT"] = tr(inputs["L1_attn_wd"])
    d["bd_row"] = np.asarray(inputs["L1_attn_bd"]).reshape(1, H)
    gmt = np.asarray(inputs["L1_attn_gamma"]).reshape(H, HD).T
    btt = np.asarray(inputs["L1_attn_beta"]).reshape(H, HD).T
    d["gamT"] = np.ascontiguousarray(np.concatenate([gmt[:, 0::2], gmt[:, 1::2]], 0))
    d["betT"] = np.ascontiguousarray(np.concatenate([btt[:, 0::2], btt[:, 1::2]], 0))
    d["WrT"] = tr(inputs["L1_ffn_W_r"])
    d["WkT"] = tr(inputs["L1_ffn_W_k"])
    d["WvT"] = tr(inputs["L1_ffn_W_v"])
    d["mur"] = np.ascontiguousarray(np.asarray(inputs["L1_ffn_mu_r"]).reshape(NCH, 128).T)
    d["muk"] = np.ascontiguousarray(np.asarray(inputs["L1_ffn_mu_k"]).reshape(NCH, 128).T)
    for k, v in _CONSTS.items():
        if k != "_constidx":
            d["c_" + k] = v
    return d


def build_nc(dev=False):
    nc = bass.Bass("TRN2", target_bir_lowering=False, detect_race_conditions=False)

    ap = {}

    def din(name, shape, dt=F32):
        ap[name] = nc.dram_tensor(name, list(shape), dt, kind="ExternalInput").ap()

    din("x", (T, D))
    din("WwT", (D, D)); din("WreadT", (D, D))
    din("decay_row", (1, H)); din("logalpha_row", (1, H))
    din("ln0g", (128, NCH)); din("ln0b", (128, NCH))
    din("fc1q", (8, D, 512), U8); din("fc2q", (FF, D), U8)
    din("fc1s", (8, D, 512)); din("fc2s", (FF, D))
    din("b1", (128, FF // 128)); din("b2", (128, NCH))
    for nm in ("wqT", "wkT", "wvT", "wgT", "woT"):
        din(nm, (D, D))
    for nm in ("bq", "bk", "bv", "bg", "bo"):
        din(nm, (128, NCH))
    din("wdT", (D, H)); din("bd_row", (1, H)); din("bv_row", (1, D))
    din("gamT", (128, H // 2)); din("betT", (128, H // 2))
    din("WrT", (D, D)); din("WkT", (D, D)); din("WvT", (D, D))
    din("mur", (128, NCH)); din("muk", (128, NCH))
    din("c_ID128", (128, 128)); din("c_ONES", (128, 128))
    din("c_R2T", (128, 128))
    din("c_COSQ", (128, T)); din("c_SINQ", (128, T))
    din("c_DELTA64", (128, 64)); din("c_TRIMASK64", (128, 64))
    din("c_C1PAT", (128, 64)); din("c_P63", (128, 1))
    din("c_IOTA2D", (128, 16)); din("c_IOTAROW", (1, T))
    din("c_MPAT", (128, 1024)); din("c_CONSTCOL", (128, 13)); din("c_IDUP", (128, 128))

    y = nc.dram_tensor("y", [T, D], F32, kind="ExternalOutput").ap()
    hscr = nc.dram_tensor("hscr", [FF, T], BF16, kind="Internal").ap()
    strmscr = nc.dram_tensor("strmscr", [128, NCH * CW], F32, kind="Internal").ap()
    dbg = {}
    if dev:
        for nm, shape in [("d_strm0", (128, NCH * CW)), ("d_strm1", (128, NCH * CW)),
                          ("d_strm2", (128, NCH * CW)), ("d_strm3", (128, NCH * CW)),
                          ("d_strm4", (128, NCH * CW))]:
            dbg[nm] = nc.dram_tensor(nm, list(shape), F32, kind="ExternalOutput").ap()
        for nm, shape, dt in [("d_vtok", (128, 8192), BF16), ("d_xptok", (128, 8192), BF16),
                              ("d_reads", (128, 8192), BF16), ("d_state", (128, 512), F32),
                              ("d_mht", (128, 1024), F32), ("d_smh", (128, 1024), BF16),
                              ("d_rep", (128, 64), F32), ("d_rowb", (1, 64), F32),
                              ("d_gp", (128, 1024), F32), ("d_gw", (128, 1024), BF16),
                              ("d_gc", (128, 512), F32)]:
            dbg[nm] = nc.dram_tensor(nm, list(shape), F32 if dt == F32 else F32, kind="ExternalOutput").ap()

    with contextlib.ExitStack() as st0:
        em = Em(nc, st0)
        sb = lambda n, s, d=F32: st0.enter_context(nc.sbuf_tensor(n, list(s), d))
        ps = lambda n, s: st0.enter_context(nc.psum_tensor(n, list(s), F32))

        CONST = sb("CONST", (128, 13))
        ID = sb("ID", (128, 128))
        ONES = sb("ONESS", (128, 128))
        R2T = sb("R2TS", (128, 128), BF16)
        XHAT = sb("XHAT", (128, NCH * CW), BF16)
        WARM = sb("WARM", (128, 1024), BF16)
        IDBF = sb("IDBF", (128, 128), BF16)
        LDS = sb("LDS", (128, 32))

        def warm(func):
            em.do("scalar", "activation", WARM[:, :],
                  CONST[:, 0:1].to_broadcast((128, 1024)), func)

        cc = lambda v: CONST[:, CIDX[v]:CIDX[v] + 1]
        cc1 = lambda v: CONST[0:1, CIDX[v]:CIDX[v] + 1]

        ld = em.sem("ld")
        ldc = em.sem("ldc")
        lx0 = em.sem("lx0")
        lx1 = em.sem("lx1")
        wld = em.sem("wld")
        main = em.sem("main")
        tdone = em.sem("tdone")
        vdone = em.sem("vdone")
        sdone = em.sem("sdone")
        stdone = em.sem("stdone")
        gdump = em.sem("gdump")

        for nm, buf in [("c_CONSTCOL", CONST), ("c_ID128", ID), ("c_ONES", ONES)]:
            em.dma("sync", buf[:, :], ap[nm][:, :], sem=ldc)
        em.dma("gpsimd", R2T[:, :], ap["c_R2T"][:, :], sem=wld)

        with contextlib.ExitStack() as stp:
            sbp = lambda n, s, d=F32: stp.enter_context(nc.sbuf_tensor(n, list(s), d))
            psp = lambda n, s: stp.enter_context(nc.psum_tensor(n, list(s), F32))
            strm_st = contextlib.ExitStack()
            STRM = strm_st.enter_context(nc.sbuf_tensor("STRM", [128, NCH * CW], F32))
            STRMSTK = [strm_st]

            xtok = sbp("xtok", (128, 2 * D))
            ptr = psp("ptr", (128, 1024))
            ptr2 = psp("ptr2", (128, 1024))

            em.wait("vector", ldc)
            em.do("vector", "tensor_copy", out=IDBF[:, :], in_=ID[:, :])
            em.do("vector", "memset", STRM[:, :].rearrange("p (k w) -> p k w", w=CW)[:, :, 0:1], 0.0, sem=vdone)

            td0 = tdone.n
            vd0 = vdone.n
            lxs = [lx0, lx1]
            for tc in range(16):
                if tc >= 2:
                    em.wait("sync", tdone, td0 + tc - 1)
                em.dma("sync", xtok[:, (tc % 2) * D:(tc % 2) * D + D],
                       ap["x"][128 * tc:128 * tc + 128, :], sem=lxs[tc % 2])
            em.barrier([ldc], ["tensor"])
            for tc in range(16):
                em.wait("tensor", lxs[tc % 2], 16 * (tc // 2 + 1))
                if tc >= 2:
                    em.wait("tensor", vdone, vd0 + tc - 1)
                pt = ptr if tc % 2 == 0 else ptr2
                for ec in range(NCH):
                    em.do("tensor", "transpose",
                          pt[:, 128 * ec:128 * ec + 128],
                          xtok[:, (tc % 2) * D + 128 * ec:(tc % 2) * D + 128 * ec + 128],
                          ID[:, :], sem=(tdone if ec == NCH - 1 else None))

            for tc in range(16):
                em.wait("vector", tdone, td0 + tc + 1)
                pt = ptr if tc % 2 == 0 else ptr2
                outap = STRM[:, :].rearrange("p (k w) -> p k w", w=CW)[:, :, 1 + 128 * tc:1 + 128 * tc + 128]
                em.do("vector", "tensor_copy", out=outap, in_=pt[:, :].rearrange("p (k w) -> p k w", w=128),
                      sem=vdone)
            em.wait("vector", vdone)
            em.do("vector", "tensor_copy", out=XHAT[:, :], in_=STRM[:, :], sem=vdone)

        if dev:
            em.wait("sync", vdone)
            em.dma("sync", dbg["d_strm0"][:, :], STRM[:, :], sem=stdone)
            em.barrier([stdone], ["vector", "tensor", "scalar"])

        def emit_ln(tag, affine=None, write_strm=True, write_xhat=True):
            with contextlib.ExitStack() as stl:
                ROWS = stl.enter_context(nc.sbuf_tensor(tag + "ROWS", [1, 4 * 2048], F32))
                YSQA = stl.enter_context(nc.sbuf_tensor(tag + "YSQA", [128, NCH * 512], F32))
                YSQB = stl.enter_context(nc.sbuf_tensor(tag + "YSQB", [128, NCH * 512], F32))
                pLNa = stl.enter_context(nc.psum_tensor(tag + "pLNa", [1, 512], F32))
                pLNb = stl.enter_context(nc.psum_tensor(tag + "pLNb", [1, 512], F32))
                pMUr = [stl.enter_context(nc.psum_tensor(tag + "pMU%d" % i, [128, 512], F32))
                        for i in range(2)]
                pRSr = [stl.enter_context(nc.psum_tensor(tag + "pRS%d" % i, [128, 512], F32))
                        for i in range(2)]
                MUB = [stl.enter_context(nc.sbuf_tensor(tag + "MUB%d" % i, [128, 512], F32))
                       for i in range(2)]
                RSB = [stl.enter_context(nc.sbuf_tensor(tag + "RSB%d" % i, [128, 512], F32))
                       for i in range(2)]
                gdn = em.sem(tag + "gdn")
                em.barrier([vdone], ["scalar", "tensor"])
                sq_s = {}
                sum_s = {}
                mr_s = {}
                rep_s = {}
                nv_s = {}
                ng_s = {}
                for qt in range(4):
                    q0 = 512 * qt
                    YSQ = YSQA if qt % 2 == 0 else YSQB
                    pMU = pMUr[qt % 2]
                    pRS = pRSr[qt % 2]
                    if qt >= 2:
                        em.wait("scalar", vdone, nv_s[qt - 2])
                        em.wait("scalar", gdn, ng_s[qt - 2])
                    for k in range(NCH):
                        em.do("scalar", "activation", YSQ[:, 512 * k:512 * k + 512],
                              STRM[:, tcol(k, q0):tcol(k, q0) + 512], AF.Square,
                              sem=(sdone if k == NCH - 1 else None))
                    sq_s[qt] = sdone.n
                    em.wait("tensor", sdone, sq_s[qt])
                    for k in range(NCH):
                        em.mm(pLNa[0:1, :], ONES[:, 0:1], STRM[:, tcol(k, q0):tcol(k, q0) + 512],
                              start=(k == 0), stop=(k == NCH - 1))
                    for k in range(NCH):
                        em.mm(pLNb[0:1, :], ONES[:, 0:1], YSQ[:, 512 * k:512 * k + 512],
                              start=(k == 0), stop=(k == NCH - 1),
                              sem=(tdone if k == NCH - 1 else None))
                    sum_s[qt] = tdone.n
                    em.wait("vector", tdone, sum_s[qt])
                    R0 = 2048 * qt
                    mu = ROWS[0:1, R0:R0 + 512]
                    m2 = ROWS[0:1, R0 + 512:R0 + 1024]
                    var = ROWS[0:1, R0 + 1024:R0 + 1536]
                    rstd = ROWS[0:1, R0 + 1536:R0 + 2048]
                    em.do("vector", "tensor_scalar", mu, pLNa[0:1, :],
                          CONST[0:1, CIDX[1.0 / D]:CIDX[1.0 / D] + 1], None, ALU.mult)
                    em.do("vector", "tensor_scalar", m2, pLNb[0:1, :],
                          CONST[0:1, CIDX[1.0 / D]:CIDX[1.0 / D] + 1], None, ALU.mult)
                    em.do("vector", "tensor_tensor", var, mu, mu, ALU.mult)
                    em.do("vector", "tensor_tensor", var, m2, var, ALU.subtract, sem=vdone)
                    em.wait("scalar", vdone)
                    em.do("scalar", "activation", rstd, var, AF.Ln,
                          bias=CONST[0:1, CIDX[1e-5]:CIDX[1e-5] + 1], scale=1.0)
                    em.do("scalar", "activation", rstd, rstd, AF.Exp, scale=-0.5, sem=sdone)
                    em.wait("vector", sdone)
                    em.do("vector", "tensor_tensor", mu, mu, rstd, ALU.mult, sem=vdone)
                    mr_s[qt] = vdone.n
                    em.wait("tensor", vdone, mr_s[qt])
                    if qt >= 2:
                        em.wait("tensor", vdone, nv_s[qt - 2])
                        em.wait("tensor", gdn, ng_s[qt - 2])
                    em.mm(pMU[:, :], ONES[0:1, :], mu, sem=None)
                    em.mm(pRS[:, :], ONES[0:1, :], rstd, sem=tdone)
                    rep_s[qt] = tdone.n
                    em.wait("vector", tdone, rep_s[qt])
                    if qt >= 2:
                        em.wait("vector", gdn, ng_s[qt - 2])
                    mub, rsb = MUB[qt % 2][:, :], RSB[qt % 2][:, :]
                    em.do("vector", "tensor_copy", out=mub, in_=pMU[:, :])
                    em.do("vector", "tensor_copy", out=rsb, in_=pRS[:, :], sem=vdone)
                    cp_s = vdone.n
                    em.wait("gpsimd", vdone, cp_s)
                    for k in range(NCH):
                        eng = "vector" if k % 2 == 0 else "gpsimd"
                        mu_t = pMU[:, :] if k % 2 == 0 else mub
                        rs_t = pRS[:, :] if k % 2 == 0 else rsb
                        sl = STRM[:, tcol(k, q0):tcol(k, q0) + 512]
                        t1 = YSQ[:, 512 * k:512 * k + 512]
                        em.do(eng, "tensor_tensor", t1, sl, rs_t, ALU.mult)
                        if affine is None:
                            if write_strm:
                                em.do(eng, "tensor_tensor", sl, t1, mu_t, ALU.subtract)
                                if write_xhat:
                                    em.do(eng, "tensor_copy",
                                          out=XHAT[:, tcol(k, q0):tcol(k, q0) + 512], in_=sl)
                            else:
                                em.do(eng, "tensor_tensor",
                                      XHAT[:, tcol(k, q0):tcol(k, q0) + 512], t1, mu_t,
                                      ALU.subtract)
                        else:
                            gcols, bcols = affine
                            em.do(eng, "tensor_tensor", t1, t1, mu_t, ALU.subtract)
                            em.do(eng, "tensor_scalar",
                                  XHAT[:, tcol(k, q0):tcol(k, q0) + 512], t1,
                                  gcols[:, k:k + 1], bcols[:, k:k + 1], ALU.mult, ALU.add)
                    em.do("vector", "engine_nop", sem=vdone)
                    nv_s[qt] = vdone.n
                    em.do("gpsimd", "engine_nop", sem=gdn)
                    ng_s[qt] = gdn.n
                em.wait("vector", gdn, ng_s[3])
                em.do("vector", "engine_nop", sem=vdone)

        if STAGE_LIMIT >= 2:
         with contextlib.ExitStack() as stp:
            sbp = lambda n, s, d=F32: stp.enter_context(nc.sbuf_tensor(n, list(s), d))
            WWT = sbp("WWT", (128, NCH * 1024), BF16)
            WRT = sbp("WRT", (128, NCH * 1024), BF16)
            WRTRAW = sbp("WRTRAW", (128, 1024))
            ROWB = sbp("ROWB", (1, 64))
            REP = sbp("REP", (128, 64))
            ALPH2 = sbp("ALPH2", (128, 8))
            MHT = sbp("MHT", (128, 1024))
            GP = sbp("GPt", (128, 1024))
            GW = sbp("GWt", (128, 1024), BF16)
            GC = sbp("GCt", (128, 512))
            DELTA = sbp("DELTA", (128, 64))
            TRIM = sbp("TRIM", (128, 64))
            C1P = sbp("C1P", (128, 64))
            P63 = sbp("P63", (128, 1))
            DROW = sbp("DROW", (1, 16))
            LAROW = sbp("LAROW", (1, 16))
            TROW = sbp("TROW", (1, 64))
            STATE = sbp("STATE", (128, 512))
            SMH = sbp("SMH", (128, 1024), BF16)
            VG = sbp("VG", (128, 1024), BF16)
            VTOK = sbp("VTOK", (128, 8 * 1024), BF16)
            XPTOK = sbp("XPTOK", (128, 8 * 1024), BF16)
            READS = sbp("READS", (128, 8 * 1024), BF16)

            em.barrier([tdone, vdone], ["gpsimd", "sync", "scalar", "vector"])
            for dk in range(NCH):
                em.dma("gpsimd", WWT[:, 1024 * dk:1024 * dk + 1024],
                       ap["WwT"][128 * dk:128 * dk + 128, :], sem=wld)
            for nm, buf in [("c_DELTA64", DELTA), ("c_TRIMASK64", TRIM), ("c_C1PAT", C1P),
                            ("c_P63", P63), ("decay_row", DROW), ("logalpha_row", LAROW)]:
                em.dma("sync", buf[:, :], ap[nm][:, :], sem=ld)

            em.barrier([ld], ["scalar"])
            warm(AF.Exp)
            em.do("scalar", "activation", TROW[0:1, 0:16], DROW[0:1, :], AF.Exp, scale=-1.0, sem=sdone)
            em.wait("vector", sdone)
            em.do("vector", "tensor_scalar", TROW[0:1, 0:16], TROW[0:1, 0:16],
                  CONST[0:1, CIDX[1.0]:CIDX[1.0] + 1], None, ALU.add)
            em.do("vector", "reciprocal", out=ROWB[0:1, 0:16], in_=TROW[0:1, 0:16], sem=vdone)
            em.wait("scalar", vdone)
            em.do("scalar", "activation", ROWB[0:1, 16:32], ROWB[0:1, 0:16], AF.Ln)
            em.do("scalar", "activation", ROWB[0:1, 32:48], ROWB[0:1, 16:32], AF.Exp, scale=64.0)
            em.do("scalar", "activation", ROWB[0:1, 48:64], LAROW[0:1, :], AF.Exp, sem=sdone)
            em.wait("tensor", sdone)
            with nc.psum_tensor("pREP", [128, 64], F32) as pREP:
                em.mm(pREP[:, :], ONES[0:1, :], ROWB[0:1, :], sem=tdone)
                em.wait("vector", tdone)
                em.do("vector", "tensor_copy", out=REP[:, :], in_=pREP[:, :], sem=vdone)
            for dk in range(8):
                em.do("vector", "tensor_copy", out=ALPH2[0:64, dk:dk + 1], in_=REP[0:64, 48 + 2 * dk:49 + 2 * dk])
                em.do("vector", "tensor_copy", out=ALPH2[64:128, dk:dk + 1], in_=REP[64:128, 49 + 2 * dk:50 + 2 * dk])
            for h in range(H):
                cb = (h % 2) * 512 + 64 * (h // 2)
                lgc = REP[:, 16 + h:17 + h]
                em.do("vector", "tensor_scalar", MHT[:, cb:cb + 64], DELTA[:, :], lgc, None, ALU.mult)
                em.do("vector", "tensor_scalar", GP[:, cb:cb + 64], C1P[:, :], lgc, None, ALU.mult)
                em.do("vector", "tensor_scalar", GW[:, 64 * h:64 * h + 64],
                      P63[:, 0:1].to_broadcast((128, 64)), lgc, None, ALU.mult)
            em.do("vector", "engine_nop", sem=vdone)
            em.wait("scalar", vdone)
            em.do("scalar", "activation", MHT[:, :], MHT[:, :], AF.Exp)
            em.do("scalar", "activation", GP[:, :], GP[:, :], AF.Exp)
            em.do("scalar", "activation", GW[:, :], GW[:, :], AF.Exp, sem=sdone)
            em.wait("vector", sdone)
            for h in range(H):
                cb = (h % 2) * 512 + 64 * (h // 2)
                rh = (h % 2) * 64
                em.do("vector", "tensor_tensor", MHT[:, cb:cb + 64], MHT[:, cb:cb + 64], TRIM[:, :], ALU.mult)
                em.do("vector", "tensor_scalar", GC[rh:rh + 64, 64 * (h // 2):64 * (h // 2) + 64],
                      ONES[rh:rh + 64, 0:64], REP[rh:rh + 64, 32 + h:33 + h], None, ALU.mult)
            em.do("vector", "memset", STATE[:, :], 0.0)
            em.do("vector", "engine_nop", sem=vdone)

            ld_w = em.sem("ld_w")
            em.barrier([vdone], ["sync"])
            for dk in range(8):
                em.dma("sync", WRTRAW[:, :], ap["WreadT"][128 * dk:128 * dk + 128, :], sem=ld_w)
                em.wait("vector", ld_w)
                em.do("vector", "tensor_scalar", WRT[:, 1024 * dk:1024 * dk + 1024],
                      WRTRAW[:, :], ALPH2[:, dk:dk + 1], None, ALU.mult, sem=vdone)
                if dk < 7:
                    em.wait("sync", vdone)

            ssnap = {}
            smsnap = {}
            isnap = {}
            gsnap = {}
            usnap = {}
            stsnap = {}
            for ih in (range(2) if L0SUB >= 1 else []):
                T0h = 1024 * ih
                with contextlib.ExitStack() as stq:
                    pV = stq.enter_context(nc.psum_tensor(f"pV{ih}", [128, 1024], F32))
                    pV2 = stq.enter_context(nc.psum_tensor(f"pV2{ih}", [128, 1024], F32))
                    pX = stq.enter_context(nc.psum_tensor(f"pX{ih}", [128, 1024], BF16))
                    pX2 = stq.enter_context(nc.psum_tensor(f"pX2{ih}", [128, 1024], BF16))
                    em.barrier([vdone, wld], ["tensor"])
                    vd0 = vdone.n
                    td0 = tdone.n
                    for tc in range(8):
                        pv = pV if tc % 2 == 0 else pV2
                        px = pX if tc % 2 == 0 else pX2
                        if tc >= 2:
                            em.wait("tensor", vdone, vd0 + 2 * (tc - 1))
                        t0 = T0h + 128 * tc
                        for ec in range(NCH):
                            for dvh in range(2):
                                em.mm(pv[:, 512 * dvh:512 * dvh + 512],
                                      XHAT[:, tcol(ec, t0):tcol(ec, t0) + 128],
                                      WWT[:, 1024 * ec + 512 * dvh:1024 * ec + 512 * dvh + 512],
                                      start=(ec == 0), stop=(ec == NCH - 1),
                                      sem=(tdone if (ec == NCH - 1 and dvh == 1) else None))
                        for ec in (range(NCH) if L0SUB >= 2 else []):
                            em.do("tensor", "transpose", px[:, 128 * ec:128 * ec + 128],
                                  XHAT[:, tcol(ec, t0 - 1):tcol(ec, t0 - 1) + 128], IDBF[:, :],
                                  sem=(tdone if ec == NCH - 1 else None))
                        if L0SUB >= 2:
                            em.wait("vector", tdone, td0 + 2 * tc + 1)
                            em.do("vector", "tensor_copy", out=VTOK[:, 1024 * tc:1024 * tc + 1024], in_=pv[:, :])
                            em.wait("vector", tdone, td0 + 2 * tc + 2)
                            em.do("vector", "tensor_copy", out=XPTOK[:, 1024 * tc:1024 * tc + 1024], in_=px[:, :],
                                  sem=vdone, inc=2)
                        else:
                            em.wait("vector", tdone, td0 + tc + 1)
                            em.do("vector", "tensor_copy", out=VTOK[:, 1024 * tc:1024 * tc + 1024], in_=pv[:, :],
                                  sem=vdone, inc=2)
                if L0SUB < 3:
                    continue
                with contextlib.ExitStack() as stq:
                    pS = stq.enter_context(nc.psum_tensor(f"pS{ih}", [128, 1024], F32))
                    pRi = stq.enter_context(nc.psum_tensor(f"pRi{ih}", [128, 1024], F32))
                    pRa = stq.enter_context(nc.psum_tensor(f"pRa{ih}", [128, 1024], F32))
                    pU = stq.enter_context(nc.psum_tensor(f"pU{ih}", [128, 1024], F32))
                    RSC = stq.enter_context(nc.sbuf_tensor(f"RSC{ih}", [128, 512], F32))
                    gdone = em.sem(f"gdone{ih}")
                    em.barrier([vdone], ["tensor", "gpsimd"])
                    for c in range(16):
                        gc = 16 * ih + c
                        t0 = 64 * gc
                        tcl = c // 2
                        rh = (gc % 2) * 64
                        if c >= 2:
                            em.wait("tensor", vdone, smsnap[gc - 2])
                        for h in range(H):
                            hh = (h % 2) * 64
                            cb = (h % 2) * 512 + 64 * (h // 2)
                            em.mm(pS[rh:rh + 64, cb:cb + 64],
                                  XHAT[hh:hh + 64, tcol(h // 2, t0 - 1):tcol(h // 2, t0 - 1) + 64],
                                  XHAT[hh:hh + 64, tcol(h // 2, t0):tcol(h // 2, t0) + 64],
                                  tp=(hh, rh), sem=(tdone if h == H - 1 else None))
                        ssnap[gc] = tdone.n
                        em.do("vector", "tensor_tensor", VG[rh:rh + 64, :],
                              VTOK[rh:rh + 64, 1024 * tcl:1024 * tcl + 1024], GW[rh:rh + 64, :], ALU.mult)
                        em.wait("vector", tdone, ssnap[gc])
                        em.do("vector", "tensor_tensor", SMH[rh:rh + 64, :],
                              pS[rh:rh + 64, :], MHT[rh:rh + 64, :], ALU.mult, sem=vdone)
                        smsnap[gc] = vdone.n
                        if gc >= 1:
                            em.wait("tensor", vdone, stsnap[gc - 1])
                        for h in range(H):
                            hh = (h % 2) * 64
                            cb = (h % 2) * 512 + 64 * (h // 2)
                            em.mm(pRi[hh:hh + 64, cb:cb + 64],
                                  STATE[hh:hh + 64, 64 * (h // 2):64 * (h // 2) + 64],
                                  STRM[hh:hh + 64, tcol(h // 2, t0):tcol(h // 2, t0) + 64],
                                  tp=(hh, hh), sem=(tdone if h == H - 1 else None))
                        isnap[gc] = tdone.n
                        em.wait("gpsimd", tdone, isnap[gc])
                        em.do("gpsimd", "tensor_tensor", STATE[0:64, :],
                              STATE[0:64, :], GC[0:64, :], ALU.mult)
                        em.do("gpsimd", "tensor_tensor", STATE[64:128, :],
                              STATE[64:128, :], GC[64:128, :], ALU.mult,
                              nofence=True, sem=gdone)
                        gsnap[gc] = gdone.n
                        em.wait("tensor", vdone, smsnap[gc])
                        for h in range(H):
                            hh = (h % 2) * 64
                            cb = (h % 2) * 512 + 64 * (h // 2)
                            em.mm(pRa[hh:hh + 64, cb:cb + 64],
                                  VTOK[rh:rh + 64, 1024 * tcl + 64 * h:1024 * tcl + 64 * h + 64],
                                  SMH[rh:rh + 64, cb:cb + 64],
                                  tp=(rh, hh), sem=(tdone if h == H - 1 else None))
                        for h in range(H):
                            hh = (h % 2) * 64
                            cb = (h % 2) * 512 + 64 * (h // 2)
                            em.mm(pU[hh:hh + 64, cb:cb + 64],
                                  XPTOK[rh:rh + 64, 1024 * tcl + 64 * h:1024 * tcl + 64 * h + 64],
                                  VG[rh:rh + 64, 64 * h:64 * h + 64],
                                  tp=(rh, hh), sem=(tdone if h == H - 1 else None))
                        usnap[gc] = tdone.n
                        em.wait("vector", tdone, usnap[gc])
                        tl0 = 64 * c
                        for par in range(2):
                            pr = 64 * par
                            cbase = 512 * par
                            em.do("vector", "tensor_tensor", RSC[pr:pr + 64, :],
                                  pRi[pr:pr + 64, cbase:cbase + 512], GP[pr:pr + 64, cbase:cbase + 512],
                                  ALU.mult, nofence=(par == 1))
                        for par in range(2):
                            pr = 64 * par
                            cbase = 512 * par
                            em.do("vector", "tensor_tensor",
                                  READS[pr:pr + 64, :].rearrange("p (dk n) -> p dk n", n=1024)[:, :, tl0:tl0 + 64],
                                  RSC[pr:pr + 64, :].rearrange("p (dk n) -> p dk n", n=64),
                                  pRa[pr:pr + 64, cbase:cbase + 512].rearrange("p (dk n) -> p dk n", n=64),
                                  ALU.add, nofence=(par == 1))
                        em.wait("vector", gdone, gsnap[gc])
                        em.do("vector", "tensor_tensor", STATE[0:64, :],
                              STATE[0:64, :], pU[0:64, 0:512], ALU.add)
                        em.do("vector", "tensor_tensor", STATE[64:128, :],
                              STATE[64:128, :], pU[64:128, 512:1024], ALU.add,
                              nofence=True, sem=vdone)
                        stsnap[gc] = vdone.n
                if dev and ih == 1:
                    em.barrier([vdone], ["gpsimd", "sync"])
                    em.dma("gpsimd", dbg["d_vtok"][:, :], VTOK[:, :], sem=gdump)
                    em.dma("gpsimd", dbg["d_xptok"][:, :], XPTOK[:, :], sem=gdump)
                    em.dma("gpsimd", dbg["d_reads"][:, :], READS[:, :], sem=gdump)
                    em.dma("gpsimd", dbg["d_smh"][:, :], SMH[:, :], sem=gdump)
                    em.dma("sync", dbg["d_state"][:, :], STATE[:, :], sem=stdone)
                    em.dma("sync", dbg["d_mht"][:, :], MHT[:, :], sem=stdone)
                    em.dma("sync", dbg["d_rep"][:, :], REP[:, :], sem=stdone)
                    em.dma("sync", dbg["d_rowb"][:, :], ROWB[:, :], sem=stdone)
                    em.dma("sync", dbg["d_gp"][:, :], GP[:, :], sem=stdone)
                    em.dma("gpsimd", dbg["d_gw"][:, :], GW[:, :], sem=gdump)
                    em.dma("sync", dbg["d_gc"][:, :], GC[:, :], sem=stdone)
                    em.barrier([stdone, gdump], ["vector", "tensor"])
                if L0SUB < 4:
                    continue
                with contextlib.ExitStack() as stq:
                    pW = stq.enter_context(nc.psum_tensor(f"pW{ih}", [128, 512], F32))
                    pW2 = stq.enter_context(nc.psum_tensor(f"pW2{ih}", [128, 512], F32))
                    em.barrier([vdone], ["tensor"])
                    vd0 = vdone.n
                    ii = 0
                    for ecO in range(NCH):
                        for tq in range(2):
                            pw = pW if ii % 2 == 0 else pW2
                            if ii >= 2:
                                em.wait("tensor", vdone, vd0 + ii - 1)
                            for dk in range(8):
                                em.mm(pw[:, :], WRT[:, 1024 * dk + 128 * ecO:1024 * dk + 128 * ecO + 128],
                                      READS[:, 1024 * dk + 512 * tq:1024 * dk + 512 * tq + 512],
                                      start=(dk == 0), stop=(dk == 7),
                                      sem=(tdone if dk == 7 else None))
                            em.wait("vector", tdone)
                            sl = STRM[:, tcol(ecO, T0h + 512 * tq):tcol(ecO, T0h + 512 * tq) + 512]
                            em.do("vector", "scalar_tensor_tensor", sl, sl, cc(2.0), pw[:, :],
                                  ALU.mult, ALU.add, sem=vdone)
                            ii += 1

        if STAGE_LIMIT >= 3:
         with contextlib.ExitStack() as stp:
            sbp = lambda n, s, d=F32: stp.enter_context(nc.sbuf_tensor(n, list(s), d))
            emit_ln("l1")

        if dev:
            em.barrier([vdone], ["sync"])
            em.dma("sync", dbg["d_strm1"][:, :], STRM[:, :], sem=stdone)
            em.barrier([stdone], ["vector", "tensor", "scalar"])

        if STAGE_LIMIT >= 4:
         with contextlib.ExitStack() as stp:
            sbp = lambda n, s, d=F32: stp.enter_context(nc.sbuf_tensor(n, list(s), d))
            LN0G = sbp("LN0G", (128, NCH))
            LN0B = sbp("LN0B", (128, NCH))
            B1 = sbp("B1", (128, FF // 128))
            HB2 = sbp("HB2", (128, NCH))
            with contextlib.ExitStack() as stq:

                ldb = em.sem("ldb")
                em.barrier([tdone, vdone, sdone], ["sync"])
                em.dma("sync", LN0G[:, :], ap["ln0g"][:, :], sem=ldb)
                em.dma("sync", LN0B[:, :], ap["ln0b"][:, :], sem=ldb)
                em.dma("sync", B1[:, :], ap["b1"][:, :], sem=ldb)
                em.dma("sync", HB2[:, :], ap["b2"][:, :], sem=ldb)
                em.barrier([ldb], ["vector", "scalar"])
                em.do("vector", "tensor_scalar", HB2[:, :], HB2[:, :], cc(0.5), None, ALU.mult)
                emit_ln("lf", affine=(LN0G, LN0B))

            with contextlib.ExitStack() as stq:
                sbq = lambda n, s, d=F32: stq.enter_context(nc.sbuf_tensor(n, list(s), d))
                W1U = sbq("W1U", (128, 2 * FF), U8)
                SC1 = sbq("SC1", (128, FF), BF16)
                W1B = sbq("W1B", (128, 2 * FF), BF16)
                HSTG = sbq("HSTG", (128, 2 * 4 * T), BF16)
                W2U = sbq("W2U", (128, 4 * 1024), U8)
                SC2 = sbq("SC2", (128, 4 * 1024), BF16)
                W2T = sbq("W2T", (128, 4 * 1024), BF16)
                ph1 = stq.enter_context(nc.psum_tensor("ph1", [128, 512], F32))
                ph2 = stq.enter_context(nc.psum_tensor("ph2", [128, 512], F32))
                lw1 = em.sem("lw1")
                lc1 = em.sem("lc1")
                lw2 = em.sem("lw2")
                dq1 = em.sem("dq1")
                hout = em.sem("hout")
                w2st = em.sem("w2st")
                w2scr = nc.dram_tensor("w2scr", [FF, D], BF16, kind="Internal").ap()

                em.barrier([vdone, sdone], ["sync", "gpsimd"])
                em.barrier([vdone], ["scalar"])
                warm(AF.Silu)

                w2cpy = [0]

                def load_w2(r):
                    em.wait("sync", sdone, w2cpy[0])
                    em.wait("gpsimd", dq1)
                    for j in range(4):
                        fc = 4 * r + j
                        em.dma("sync", W2U[:, 1024 * j:1024 * j + 1024],
                               ap["fc2q"][128 * fc:128 * fc + 128, :], sem=lw2)
                        em.dma("gpsimd", SC2[:, 1024 * j:1024 * j + 1024],
                               ap["fc2s"][128 * fc:128 * fc + 128, :], sem=lc1)

                def w2_round(r):
                    if r == 0:
                        load_w2(0)
                    em.wait("scalar", lw2, 16 * 4 * (r + 1))
                    em.wait("scalar", lc1, lc1.n)
                    em.wait("scalar", w2st, w2st.n)
                    em.do("scalar", "activation", W2T[:, :], W2U[:, :], AF.Copy,
                          bias=-128.0, scale=1.0, sem=sdone)
                    w2cpy[0] = sdone.n
                    em.wait("vector", sdone)
                    em.do("vector", "tensor_scalar", SC2[:, :], SC2[:, :], cc(0.5), None,
                          ALU.mult)
                    em.do("vector", "tensor_tensor", W2T[:, :], W2T[:, :], SC2[:, :],
                          ALU.mult, sem=dq1)
                    if r + 1 < 8:
                        load_w2(r + 1)
                    em.wait("sync", dq1)
                    for j in range(4):
                        fc = 4 * r + j
                        em.dma("sync", w2scr[128 * fc:128 * fc + 128, :],
                               W2T[:, 1024 * j:1024 * j + 1024], sem=w2st)

                def load_fc1(fg, cpy_snap):
                    rg = fg % 2
                    em.wait("sync", sdone, cpy_snap)
                    em.wait("gpsimd", dq1)
                    for ec in range(NCH):
                        em.dma("sync", W1U[:, FF * rg + 512 * ec:FF * rg + 512 * ec + 512],
                               ap["fc1q"][fg, 128 * ec:128 * ec + 128, :], sem=lw1)
                        em.dma("gpsimd", SC1[:, 512 * ec:512 * ec + 512],
                               ap["fc1s"][fg, 128 * ec:128 * ec + 128, :], sem=lc1)

                mm_snap = {}
                h_snap = {}
                dq_s = {}

                def dq_fc1(fg):
                    rg = fg % 2
                    em.wait("scalar", lw1, 16 * NCH * (fg + 1))
                    if fg >= 2:
                        em.wait("scalar", tdone, mm_snap[fg - 2])
                    em.do("scalar", "activation", W1B[:, FF * rg:FF * rg + FF],
                          W1U[:, FF * rg:FF * rg + FF], AF.Copy, bias=-128.0, scale=1.0,
                          sem=sdone)
                    cpy_snap = sdone.n
                    em.wait("vector", lc1, lc1.n)
                    em.wait("vector", sdone, cpy_snap)
                    em.do("vector", "tensor_tensor", W1B[:, FF * rg:FF * rg + FF],
                          W1B[:, FF * rg:FF * rg + FF], SC1[:, :],
                          ALU.mult, sem=dq1)
                    dq_s[fg] = dq1.n
                    if fg + 1 < 8:
                        load_fc1(fg + 1, cpy_snap)

                load_fc1(0, 0)
                dq_fc1(0)
                sil_s = {}
                for fg in range(8):
                    rg = fg % 2
                    em.wait("tensor", dq1, dq_s[fg])
                    if fg >= 1:
                        em.wait("tensor", sdone, sil_s[fg - 1] - 1)
                    for fc in range(4):
                        for tq in range(4):
                            ph = ph1 if (fc * 4 + tq) % 2 == 0 else ph2
                            if fc or tq:
                                em.wait("tensor", sdone, sdone.n - 1)
                            for ec in range(NCH):
                                em.mm(ph[:, :],
                                      W1B[:, FF * rg + 512 * ec + 128 * fc:FF * rg + 512 * ec + 128 * fc + 128],
                                      XHAT[:, tcol(ec, 512 * tq):tcol(ec, 512 * tq) + 512],
                                      start=(ec == 0), stop=(ec == NCH - 1),
                                      sem=(tdone if ec == NCH - 1 else None))
                            em.wait("scalar", tdone)
                            if fc == 0 and tq == 0 and fg >= 2:
                                em.wait("scalar", hout, h_snap[fg - 2])
                            em.do("scalar", "activation",
                                  HSTG[:, 8192 * rg + 2048 * fc + 512 * tq:8192 * rg + 2048 * fc + 512 * tq + 512],
                                  ph[:, :], AF.Silu, bias=B1[:, 4 * fg + fc:4 * fg + fc + 1],
                                  scale=1.0, sem=sdone)
                    mm_snap[fg] = tdone.n
                    silu_snap = sil_s[fg] = sdone.n
                    if fg + 1 < 8:
                        dq_fc1(fg + 1)
                    em.wait("sync", sdone, silu_snap)
                    for fc in range(4):
                        em.dma("sync", hscr[512 * fg + 128 * fc:512 * fg + 128 * fc + 128, :],
                               HSTG[:, 8192 * rg + 2048 * fc:8192 * rg + 2048 * fc + 2048],
                               sem=hout)
                    h_snap[fg] = hout.n
                    w2_round(fg)
                em.barrier([hout, w2st, dq1, sdone], ["sync", "vector", "scalar", "tensor"])

            with contextlib.ExitStack() as stq:
                sbq = lambda n, s, d=F32: stq.enter_context(nc.sbuf_tensor(n, list(s), d))
                HBUF = sbq("HBUF", (128, 32 * 512), BF16)
                W2F = sbq("W2F", (128, 32 * 1024), BF16)
                pz1 = stq.enter_context(nc.psum_tensor("pz1", [128, 512], F32))
                pz2 = stq.enter_context(nc.psum_tensor("pz2", [128, 512], F32))
                lwg = [em.sem("lwg%d" % g) for g in range(4)]
                lhg = [em.sem("lhg%d" % g) for g in range(4)]
                em.barrier([sdone], ["scalar"])
                warm(AF.Exp)
                for g in range(4):
                    for fc in range(8 * g, 8 * g + 8):
                        em.dma("sync", W2F[:, 1024 * fc:1024 * fc + 1024],
                               w2scr[128 * fc:128 * fc + 128, :], sem=lwg[g])
                for tq in range(4):
                    if tq >= 1:
                        em.wait("sync", tdone)
                    hb_base = [s.n for s in lhg]
                    for g in range(4):
                        em.dma("sync",
                               HBUF[:, 4096 * g:4096 * g + 4096].rearrange(
                                   "p (c n) -> p c n", n=512),
                               hscr[1024 * g:1024 * g + 1024,
                                    512 * tq:512 * tq + 512].rearrange(
                                   "(c p) n -> p c n", p=128),
                               sem=lhg[g])
                    for ec in range(NCH):
                        pz = pz1 if ec % 2 == 0 else pz2
                        if ec >= 2:
                            em.wait("tensor", vdone, vdone.n - 1)
                        for fc in range(32):
                            g = fc // 8
                            if fc % 8 == 0:
                                if tq == 0 and ec == 0:
                                    em.wait("tensor", lwg[g], 8 * 16)
                                if ec == 0:
                                    em.wait("tensor", lhg[g], hb_base[g] + 16)
                            em.mm(pz[:, :],
                                  W2F[:, 1024 * fc + 128 * ec:1024 * fc + 128 * ec + 128],
                                  HBUF[:, 512 * fc:512 * fc + 512],
                                  start=(fc == 0), stop=(fc == 31),
                                  sem=(tdone if fc == 31 else None))
                        em.wait("vector", tdone)
                        sl = STRM[:, tcol(ec, 512 * tq):tcol(ec, 512 * tq) + 512]
                        em.do("vector", "scalar_tensor_tensor", sl, pz[:, :],
                              HB2[:, ec:ec + 1], sl, ALU.add, ALU.add, sem=vdone)
                em.barrier([vdone], ["tensor", "scalar"])

        if STAGE_LIMIT >= 5:
         with contextlib.ExitStack() as stp:
            emit_ln("l2")

        if dev:
            em.barrier([vdone], ["sync"])
            em.dma("sync", dbg["d_strm2"][:, :], STRM[:, :], sem=stdone)
            em.barrier([stdone], ["vector", "tensor", "scalar"])

        if STAGE_LIMIT >= 6:
         with contextlib.ExitStack() as stp:
            sbp = lambda n, s, d=F32: stp.enter_context(nc.sbuf_tensor(n, list(s), d))
            with contextlib.ExitStack() as pre:
                sbr = lambda n, s, d=F32: pre.enter_context(nc.sbuf_tensor(n, list(s), d))
                WDT = sbr("WDT", (128, 128))
                BDR = sbr("BDR", (1, H))
                MEANV = sbr("MEANV", (128, NCH))
                SR = sbr("SR", (1, 64))
                ppd = pre.enter_context(nc.psum_tensor("ppd", [1, 64], F32))
                prep = pre.enter_context(nc.psum_tensor("prep", [128, 32], F32))
                lpd = em.sem("lpd")
                em.barrier([tdone, vdone, sdone], ["sync"])
                em.dma("sync", WDT[:, 0:H * NCH].rearrange("p (c n) -> p c n", n=H),
                       ap["wdT"][:, :].rearrange("(c p) n -> p c n", p=128), sem=lpd)
                em.dma("sync", BDR[:, :], ap["bd_row"][:, :], sem=lpd)
                em.barrier([lpd, vdone], ["vector"])
                for k in range(NCH):
                    em.do("vector", "reduce_sum", MEANV[:, k:k + 1],
                          STRM[:, tcol(k):tcol(k) + T], mybir.AxisListType.X)
                em.do("vector", "engine_nop", sem=vdone)
                em.barrier([vdone], ["tensor"])
                for k in range(NCH):
                    em.mm(ppd[0:1, 0:H], MEANV[:, k:k + 1], WDT[:, H * k:H * k + H],
                          start=(k == 0), stop=(k == NCH - 1),
                          sem=(tdone if k == NCH - 1 else None))
                em.wait("vector", tdone)
                em.do("vector", "tensor_scalar", SR[0:1, 0:H], ppd[0:1, 0:H],
                      CONST[0:1, CIDX[1.0 / T]:CIDX[1.0 / T] + 1], None, ALU.mult)
                em.do("vector", "tensor_tensor", SR[0:1, 0:H], SR[0:1, 0:H], BDR[0:1, :],
                      ALU.add, sem=vdone)
                em.wait("scalar", vdone)
                em.do("scalar", "activation", SR[0:1, 16:32], SR[0:1, 0:16], AF.Exp,
                      scale=-1.0, sem=sdone)
                em.wait("vector", sdone)
                em.do("vector", "tensor_scalar", SR[0:1, 16:32], SR[0:1, 16:32],
                      CONST[0:1, CIDX[1.0]:CIDX[1.0] + 1], None, ALU.add)
                em.do("vector", "reciprocal", out=SR[0:1, 32:48], in_=SR[0:1, 16:32], sem=vdone)
                em.wait("scalar", vdone)
                em.do("scalar", "activation", SR[0:1, 16:32], SR[0:1, 32:48], AF.Exp,
                      scale=float(-8.0 * np.log(2.0)),
                      bias=CONST[0:1, CIDX[-5.0 * np.log(2.0)]:CIDX[-5.0 * np.log(2.0)] + 1],
                      sem=sdone)
                em.wait("vector", sdone)
                em.do("vector", "tensor_scalar", SR[0:1, 16:32], SR[0:1, 16:32],
                      cc1(-1.0), cc1(1.0), ALU.mult, ALU.add, sem=vdone)
                em.wait("scalar", vdone)
                em.do("scalar", "activation", SR[0:1, 0:16], SR[0:1, 16:32], AF.Ln, sem=sdone)
                em.wait("vector", sdone)
                em.do("vector", "tensor_scalar", SR[0:1, 16:32], SR[0:1, 0:16],
                      cc1(-1.0), None, ALU.mult, sem=vdone)
                em.wait("tensor", vdone)
                em.mm(prep[:, :], ONES[0:1, :], SR[0:1, 0:32], sem=tdone)
                em.wait("vector", tdone)
                em.do("vector", "tensor_copy", out=LDS[:, :], in_=prep[:, :], sem=vdone)
                em.barrier([vdone, tdone, sdone], ["sync"])
                em.dma("sync", strmscr[:, :], STRM[:, :], sem=stdone)
                em.barrier([stdone], ["vector", "tensor", "scalar", "gpsimd"])
            STRMSTK[0].close()

            GON = sbp("GON", (128, NCH * T), BF16)
            GAMT = sbp("GAMT", (128, H // 2))
            BETT = sbp("BETT", (128, H // 2))
            BQ = sbp("BQl", (128, NCH))
            BK = sbp("BKl", (128, NCH))
            BO = sbp("BOl", (128, NCH))
            BVREP = sbp("BVREP", (128, D), BF16)
            BVR = sbp("BVR", (1, D))
            IOTAR = sbp("IOTAR", (1, T))
            MPATS = sbp("MPATS", (128, 1024), BF16)
            COSB = sbp("COSB", (128, T), BF16)
            SINF = sbp("SINF", (128, T), BF16)
            BVALL = sbp("BVALL", (128, H * 16))
            IOT2 = sbp("IOT2", (128, 16))
            la = em.sem("la")
            lg = em.sem("lg")
            for nm, buf in [("gamT", GAMT), ("betT", BETT), ("bq", BQ), ("bk", BK),
                            ("bo", BO), ("bv_row", BVR), ("c_IOTAROW", IOTAR),
                            ("c_IOTA2D", IOT2)]:
                em.dma("sync", buf[:, :], ap[nm][:, :], sem=la)
            em.dma("gpsimd", SINF[:, :], ap["c_SINQ"][:, :], sem=lg)
            em.dma("gpsimd", MPATS[:, :], ap["c_MPAT"][:, :], sem=lg)
            em.dma("gpsimd", COSB[:, :], ap["c_COSQ"][:, :], sem=lg)
            em.barrier([la, lg], ["vector"])
            for h in range(H):
                em.do("vector", "tensor_scalar", BVALL[:, 16 * h:16 * h + 16],
                      IOT2[:, :], LDS[:, h:h + 1], None, ALU.mult)
            em.do("vector", "engine_nop", sem=vdone)

            with contextlib.ExitStack() as stg:
                WGB = stg.enter_context(nc.sbuf_tensor("WGB", [128, NCH * 1024], BF16))
                BG = stg.enter_context(nc.sbuf_tensor("BGl", [128, NCH], F32))
                pg1 = stg.enter_context(nc.psum_tensor("pg1", [128, 512], F32))
                pg2 = stg.enter_context(nc.psum_tensor("pg2", [128, 512], F32))
                lw = em.sem("lw")
                em.dma("sync", BG[:, :], ap["bg"][:, :], sem=la)
                for dk in range(NCH):
                    em.dma("gpsimd", WGB[:, 1024 * dk:1024 * dk + 1024],
                           ap["wgT"][128 * dk:128 * dk + 128, :], sem=lw)
                em.barrier([la, lw], ["tensor", "scalar"])
                warm(AF.Silu)
                for eo in range(NCH):
                    for tq in range(4):
                        pg = pg1 if (eo * 4 + tq) % 2 == 0 else pg2
                        if eo or tq:
                            em.wait("tensor", sdone, sdone.n - 1)
                        for ei in range(NCH):
                            em.mm(pg[:, :], WGB[:, 1024 * ei + 128 * eo:1024 * ei + 128 * eo + 128],
                                  XHAT[:, tcol(ei, 512 * tq):tcol(ei, 512 * tq) + 512],
                                  start=(ei == 0), stop=(ei == NCH - 1),
                                  sem=(tdone if ei == NCH - 1 else None))
                        em.wait("scalar", tdone)
                        em.do("scalar", "activation", GON[:, 2048 * eo + 512 * tq:2048 * eo + 512 * tq + 512],
                              pg[:, :], AF.Silu, bias=BG[:, eo:eo + 1], scale=1.0, sem=sdone)
                em.barrier([sdone], ["tensor", "vector", "scalar", "gpsimd"])
                warm(AF.Exp)

            with contextlib.ExitStack() as stb:
                WQB = stb.enter_context(nc.sbuf_tensor("WQB", [128, NCH * 1024], BF16))
                WKB = stb.enter_context(nc.sbuf_tensor("WKB", [128, NCH * 1024], BF16))
                WVB = stb.enter_context(nc.sbuf_tensor("WVB", [128, NCH * 1024], BF16))
                lw2a = em.sem("lw2a")
                for nm, buf in [("wqT", WQB), ("wkT", WKB), ("wvT", WVB)]:
                    for dk in range(NCH):
                        em.dma("gpsimd", buf[:, 1024 * dk:1024 * dk + 1024],
                               ap[nm][128 * dk:128 * dk + 128, :], sem=lw2a)

                with contextlib.ExitStack() as stc:
                    sbc = lambda n, s, d=F32: stc.enter_context(nc.sbuf_tensor(n, list(s), d))
                    VH = sbc("VH", (128, 16 * 1040), BF16)
                    QF = sbc("QF", (128, T), BF16)
                    KF = sbc("KF", (128, T), BF16)
                    QRAW = sbc("QRAW", (128, T), BF16)
                    T2S = sbc("T2S", (128, T), BF16)
                    PRB = sbc("PRB", (128, 3 * 1024), BF16)
                    MH = sbc("MH", (128, 1024), BF16)
                    OG = sbc("OG", (128, T), BF16)
                    CMB = sbc("CMB", (65, 1024))
                    FREP = sbc("FREP", (65, 1024), BF16)
                    ACC = sbc("ACC", (128, 4))
                    REPM = sbc("REPM", (128, 2))
                    SROW = sbc("SROW", (1, 8))
                    pA = stc.enter_context(nc.psum_tensor("pA", [128, 1024], F32))
                    pB = stc.enter_context(nc.psum_tensor("pB", [128, 1024], F32))
                    pP = stc.enter_context(nc.psum_tensor("pP", [65, 1024], F32))
                    pF = stc.enter_context(nc.psum_tensor("pF", [65, 1024], F32))

                    em.barrier([la, vdone], ["tensor"])
                    em.mm(pA[:, 0:512], ONES[0:1, :], BVR[0:1, 0:512])
                    em.mm(pA[:, 512:1024], ONES[0:1, :], BVR[0:1, 512:1024], sem=tdone)
                    em.wait("vector", tdone)
                    em.do("vector", "tensor_copy", out=BVREP[:, :], in_=pA[:, :])
                    em.do("vector", "memset",
                          VH[:, :].rearrange("p (tc h c) -> p tc h c", h=16, c=65)[:, :, :, 64:65], 1.0)
                    em.do("vector", "engine_nop", sem=vdone)
                    em.barrier([lw2a, vdone], ["tensor"])
                    vvd0 = vdone.n
                    for tc in range(16):
                        pv = pA if tc % 2 == 0 else pB
                        if tc >= 2:
                            em.wait("tensor", vdone, vvd0 + tc - 1)
                        for dvh in range(2):
                            for ec in range(NCH):
                                em.mm(pv[:, 512 * dvh:512 * dvh + 512],
                                      XHAT[:, tcol(ec, 128 * tc):tcol(ec, 128 * tc) + 128],
                                      WVB[:, 1024 * ec + 512 * dvh:1024 * ec + 512 * dvh + 512],
                                      start=(ec == 0), stop=(ec == NCH - 1),
                                      sem=(tdone if (ec == NCH - 1 and dvh == 1) else None))
                        em.wait("vector", tdone)
                        em.do("vector", "tensor_tensor",
                              VH[:, 1040 * tc:1040 * tc + 1040].rearrange("p (h c) -> p h c", c=65)[:, :, 0:64],
                              pv[:, :].rearrange("p (h c) -> p h c", c=64),
                              BVREP[:, :].rearrange("p (h c) -> p h c", c=64),
                              ALU.add, sem=vdone)

                    mh_gate = [vdone.n]
                    pn_gs = [0]
                    pn_gv = [vdone.n]
                    for hp in range(8):
                        for si, (WB, BB, OUT) in enumerate(((WQB, BQ, QF), (WKB, BK, KF))):
                            if si == 0:
                                em.wait("tensor", sdone, pn_gs[0])
                                em.wait("tensor", vdone, pn_gv[0])
                            else:
                                em.barrier([vdone, sdone], ["tensor"])
                            for qt in range(4):
                                pq = pA if qt % 2 == 0 else pB
                                psl = pq[:, 0:512]
                                if qt >= 2:
                                    em.wait("tensor", vdone, vdone.n - 1)
                                for ec in range(NCH):
                                    em.mm(psl, WB[:, 1024 * ec + 128 * hp:1024 * ec + 128 * hp + 128],
                                          XHAT[:, tcol(ec, 512 * qt):tcol(ec, 512 * qt) + 512],
                                          start=(ec == 0), stop=(ec == NCH - 1),
                                          sem=(tdone if ec == NCH - 1 else None))
                                em.wait("vector", tdone)
                                em.do("vector", "tensor_scalar", QRAW[:, 512 * qt:512 * qt + 512],
                                      psl, BB[:, hp:hp + 1], None, ALU.add, sem=vdone)
                            em.barrier([vdone], ["tensor"])
                            em.do("vector", "tensor_tensor", OUT[:, :], QRAW[:, :], COSB[:, :],
                                  ALU.mult)
                            for qt in range(4):
                                pq = pA if qt % 2 == 0 else pB
                                if qt >= 2:
                                    em.wait("tensor", vdone, vdone.n - 1)
                                em.mm(pq[:, 0:512], R2T[:, :], QRAW[:, 512 * qt:512 * qt + 512],
                                      sem=tdone)
                                em.wait("vector", tdone)
                                em.do("vector", "tensor_tensor", T2S[:, 512 * qt:512 * qt + 512],
                                      pq[:, 0:512], SINF[:, 512 * qt:512 * qt + 512], ALU.mult,
                                      sem=vdone)
                            em.do("vector", "tensor_tensor", OUT[:, :], OUT[:, :], T2S[:, :],
                                  ALU.add, sem=vdone)
                        vsnap_kf = vdone.n
                        for hi in range(2):
                            h = 2 * hp + hi
                            hh = 64 * hi
                            em.wait("scalar", vdone, mh_gate[0])
                            em.do("scalar", "activation", MH[:, :], MPATS[:, :], AF.Exp,
                                  scale=LDS[:, h:h + 1], sem=sdone)
                            for qh in range(2):
                                q0 = 1024 * qh
                                first_fut = 8 * qh
                                ssnap = {}
                                avsnap = {}
                                esnap = {}
                                vsnap0 = vdone.n

                                def emit_scores(kt):
                                    pS = pA if kt % 2 == 0 else pB
                                    if kt == 0 and hi == 0 and qh == 0:
                                        em.wait("tensor", vdone, vsnap_kf)
                                    if kt == 1:
                                        em.wait("tensor", vdone, vsnap0)
                                    if kt >= 2:
                                        em.wait("tensor", sdone, esnap[kt - 2])
                                    for j in range(2):
                                        em.mm(pS[:, 512 * j:512 * j + 512],
                                              KF[hh:hh + 64, 128 * kt:128 * kt + 128],
                                              QF[hh:hh + 64, q0 + 512 * j:q0 + 512 * j + 512],
                                              sem=(tdone if j == 1 else None))
                                    ssnap[kt] = tdone.n

                                emit_scores(0)
                                for kt in range(16):
                                    pS = pA if kt % 2 == 0 else pB
                                    r = kt % 3
                                    if kt >= 3:
                                        em.wait("scalar", tdone, avsnap[kt - 3])
                                    em.wait("scalar", tdone, ssnap[kt])
                                    if kt >= first_fut:
                                        em.do("scalar", "activation", PRB[:, 1024 * r:1024 * r + 1024],
                                              pS[:, :], AF.Exp, bias=BVALL[:, 16 * h + kt:16 * h + kt + 1],
                                              scale=0.125, sem=sdone)
                                    else:
                                        em.do("scalar", "activation", PRB[:, 1024 * r:1024 * r + 1024],
                                              pS[:, :], AF.Exp, scale=0.125, sem=sdone)
                                    esnap[kt] = sdone.n
                                    off = 128 * kt - q0
                                    diag = 0 <= off < 1024
                                    if diag:
                                        em.wait("vector", sdone, esnap[kt])
                                        em.do("vector", "tensor_tensor",
                                              PRB[:, 1024 * r + off:1024 * r + 1024],
                                              PRB[:, 1024 * r + off:1024 * r + 1024],
                                              MH[:, 0:1024 - off], ALU.mult, sem=vdone)
                                    if kt + 1 < 16:
                                        emit_scores(kt + 1)
                                    if diag:
                                        em.wait("tensor", vdone)
                                    else:
                                        em.wait("tensor", sdone, esnap[kt])
                                    pT = pF if kt >= first_fut else pP
                                    st_flag = (kt == first_fut) or (qh == 1 and kt == 0)
                                    for j in range(2):
                                        em.mm(pT[:, 512 * j:512 * j + 512],
                                              VH[:, 1040 * kt + 65 * h:1040 * kt + 65 * h + 65],
                                              PRB[:, 1024 * r + 512 * j:1024 * r + 512 * j + 512],
                                              start=st_flag, stop=(kt == 15 or (kt == first_fut - 1)),
                                              sem=(tdone if j == 1 else None))
                                    avsnap[kt] = tdone.n
                                if qh == 0:
                                    em.wait("scalar", tdone)
                                    em.do("scalar", "activation", CMB[64:65, :], pF[64:65, :],
                                          AF.Ln, sem=sdone)
                                    em.wait("tensor", sdone)
                                    em.mm(pB[0:64, 0:512], ONES[64:65, 0:64], CMB[64:65, 0:512])
                                    em.mm(pB[0:64, 512:1024], ONES[64:65, 0:64],
                                          CMB[64:65, 512:1024], sem=tdone)
                                    em.wait("scalar", tdone)
                                    em.do("scalar", "activation", FREP[0:64, :], pB[0:64, :],
                                          AF.Exp, scale=-1.0, sem=sdone)
                                    em.wait("vector", sdone)
                                    em.do("vector", "tensor_tensor",
                                          OG[64 * hi:64 * hi + 64, q0:q0 + 1024],
                                          pF[0:64, :], FREP[0:64, :], ALU.mult, sem=vdone)
                                else:
                                    em.mm(pA[0:65, 0:512], ONES[0:1, 0:65], IOTAR[0:1, q0:q0 + 512])
                                    em.mm(pA[0:65, 512:1024], ONES[0:1, 0:65],
                                          IOTAR[0:1, q0 + 512:q0 + 1024], sem=tdone)
                                    em.wait("scalar", tdone)
                                    em.do("scalar", "activation", FREP[:, :], pA[0:65, :], AF.Exp,
                                          scale=LDS[0:65, 16 + h:17 + h], sem=sdone)
                                    em.wait("vector", sdone)
                                    em.do("vector", "tensor_tensor", CMB[:, :], pF[:, :], FREP[:, :],
                                          ALU.mult)
                                    em.do("vector", "tensor_tensor", CMB[:, :], CMB[:, :], pP[:, :],
                                          ALU.add, sem=vdone)
                                    em.wait("scalar", vdone)
                                    em.do("scalar", "activation", CMB[64:65, :], CMB[64:65, :],
                                          AF.Ln, sem=sdone)
                                    em.wait("tensor", sdone)
                                    em.mm(pB[0:64, 0:512], ONES[64:65, 0:64], CMB[64:65, 0:512])
                                    em.mm(pB[0:64, 512:1024], ONES[64:65, 0:64],
                                          CMB[64:65, 512:1024], sem=tdone)
                                    em.wait("scalar", tdone)
                                    em.do("scalar", "activation", FREP[0:64, :], pB[0:64, :],
                                          AF.Exp, scale=-1.0, sem=sdone)
                                    em.wait("vector", sdone)
                                    em.do("vector", "tensor_tensor",
                                          OG[64 * hi:64 * hi + 64, q0:q0 + 1024],
                                          CMB[0:64, :], FREP[0:64, :], ALU.mult, sem=vdone)
                            mh_gate[0] = vdone.n
                            if hi == 1:
                                em.barrier([vdone], ["scalar"])
                                ONF = QRAW[:, :]
                                em.do("scalar", "activation", ONF, OG[:, :], AF.Square,
                                      accum_out=ACC[:, 0:1])
                                em.do("scalar", "activation", ONF, OG[:, :], AF.Copy,
                                      accum_out=ACC[:, 1:2], sem=sdone)
                                pn_gs[0] = sdone.n
                                em.wait("tensor", sdone)
                                em.mm(pA[0:64, 0:2], ONES[0:64, 0:64], ACC[0:64, 0:2],
                                      tp=(0, 0))
                                em.mm(pA[64:128, 512:514], ONES[64:128, 0:64],
                                      ACC[64:128, 0:2], tp=(64, 64), sem=tdone)
                                em.wait("vector", tdone)
                                em.do("vector", "tensor_scalar", REPM[0:64, 0:2], pA[0:64, 0:2],
                                      CONST[0:64, CIDX[1.0 / (T * HD)]:CIDX[1.0 / (T * HD)] + 1],
                                      None, ALU.mult)
                                em.do("vector", "tensor_scalar", REPM[64:128, 0:2],
                                      pA[64:128, 512:514],
                                      CONST[64:128, CIDX[1.0 / (T * HD)]:CIDX[1.0 / (T * HD)] + 1],
                                      None, ALU.mult, nofence=True)
                                em.do("vector", "tensor_tensor", ACC[:, 2:3], REPM[:, 1:2],
                                      REPM[:, 1:2], ALU.mult)
                                em.do("vector", "tensor_sub", ACC[:, 2:3], REPM[:, 0:1],
                                      ACC[:, 2:3], sem=vdone)
                                pn_gv[0] = vdone.n
                                em.wait("scalar", vdone)
                                em.do("scalar", "activation", ACC[:, 3:4], ACC[:, 2:3], AF.Ln,
                                      bias=cc(1e-5), scale=1.0)
                                em.do("scalar", "activation", ACC[:, 3:4], ACC[:, 3:4], AF.Exp,
                                      scale=-0.5, sem=sdone)
                                em.wait("vector", sdone)
                                em.do("vector", "tensor_tensor", ACC[:, 2:3], ACC[:, 3:4],
                                      GAMT[:, hp:hp + 1], ALU.mult)
                                em.do("vector", "tensor_tensor", ACC[:, 3:4], REPM[:, 1:2],
                                      ACC[:, 2:3], ALU.mult)
                                em.do("vector", "tensor_sub", ACC[:, 3:4], BETT[:, hp:hp + 1],
                                      ACC[:, 3:4])
                                em.do("vector", "tensor_scalar", ONF, OG[:, :],
                                      ACC[:, 2:3], ACC[:, 3:4], ALU.mult, ALU.add, sem=vdone)
                                em.do("vector", "tensor_tensor",
                                      GON[:, 2048 * hp:2048 * hp + 2048],
                                      GON[:, 2048 * hp:2048 * hp + 2048], ONF, ALU.mult,
                                      sem=vdone)
                pass
            with contextlib.ExitStack() as stw:
                WOB = stw.enter_context(nc.sbuf_tensor("WOB", [128, NCH * 1024], BF16))
                DSTG = stw.enter_context(nc.sbuf_tensor("DSTG", [128, 2 * T], F32))
                pw1 = stw.enter_context(nc.psum_tensor("pwo1", [128, 512], F32))
                pw2 = stw.enter_context(nc.psum_tensor("pwo2", [128, 512], F32))
                acst = em.sem("acst")
                em.barrier([vdone, tdone, sdone], ["gpsimd"])
                for dk in range(NCH):
                    em.dma("gpsimd", WOB[:, 1024 * dk:1024 * dk + 1024],
                           ap["woT"][128 * dk:128 * dk + 128, :], sem=lg)
                em.barrier([lg], ["tensor"])
                for eo in range(NCH):
                    rg = eo % 2
                    if eo >= 2:
                        em.wait("tensor", acst)
                    for tq in range(4):
                        pw = pw1 if (eo * 4 + tq) % 2 == 0 else pw2
                        if eo or tq:
                            em.wait("tensor", vdone, vdone.n - 1)
                        for ei in range(NCH):
                            em.mm(pw[:, :], WOB[:, 1024 * ei + 128 * eo:1024 * ei + 128 * eo + 128],
                                  GON[:, 2048 * ei + 512 * tq:2048 * ei + 512 * tq + 512],
                                  start=(ei == 0), stop=(ei == NCH - 1),
                                  sem=(tdone if ei == NCH - 1 else None))
                        em.wait("vector", tdone)
                        em.do("vector", "tensor_scalar",
                              DSTG[:, T * rg + 512 * tq:T * rg + 512 * tq + 512],
                              pw[:, :], BO[:, eo:eo + 1], None, ALU.add, sem=vdone)
                    em.wait("gpsimd", vdone)
                    em.do("gpsimd", "dma_start",
                          out=strmscr[:, CW * eo + 1:CW * eo + 1 + T],
                          in_=DSTG[:, T * rg:T * rg + T],
                          accum_op=ALU.add, sem=acst, inc=16)
                em.barrier([acst, vdone], ["sync", "tensor", "vector", "scalar"])
         strm_st2 = contextlib.ExitStack()
         STRM = strm_st2.enter_context(nc.sbuf_tensor("STRM2", [128, NCH * CW], F32))
         STRMSTK[0] = strm_st2
         rst = em.sem("rst")
         em.dma("sync", STRM[:, :], strmscr[:, :], sem=rst)
         em.barrier([rst], ["tensor", "vector", "scalar"])

        if STAGE_LIMIT >= 7:
         with contextlib.ExitStack() as stp:
            emit_ln("l3")

        if dev:
            em.barrier([vdone], ["sync"])
            em.dma("sync", dbg["d_strm3"][:, :], STRM[:, :], sem=stdone)
            em.barrier([stdone], ["vector", "tensor", "scalar"])

        if STAGE_LIMIT >= 8:
         with contextlib.ExitStack() as stp:
            sbp = lambda n, s, d=F32: stp.enter_context(nc.sbuf_tensor(n, list(s), d))
            XK = sbp("XK", (128, NCH * CW), BF16)
            RHAT = sbp("RHAT", (128, NCH * T), BF16)
            MUR = sbp("MUR", (128, NCH))
            MUK = sbp("MUK", (128, NCH))
            DTMP = sbp("DTMP", (128, T))
            ZSC = sbp("ZSC", (128, 512))
            WRB = sbp("WRB", (128, NCH * 1024), BF16)
            lf = em.sem("lf")
            lfw = em.sem("lfw")
            em.barrier([tdone, vdone, sdone], ["sync", "gpsimd"])
            em.dma("sync", MUR[:, :], ap["mur"][:, :], sem=lf)
            em.dma("sync", MUK[:, :], ap["muk"][:, :], sem=lf)
            for dk in range(NCH):
                em.dma("gpsimd", WRB[:, 1024 * dk:1024 * dk + 1024],
                       ap["WrT"][128 * dk:128 * dk + 128, :], sem=lfw)
            em.barrier([lf, vdone], ["vector"])
            for k in range(NCH):
                sh = STRM[:, tcol(k, -1):tcol(k, -1) + T]
                cu = STRM[:, tcol(k):tcol(k) + T]
                em.do("vector", "tensor_sub", DTMP[:, :], cu, sh)
                em.do("vector", "scalar_tensor_tensor", XHAT[:, tcol(k):tcol(k) + T],
                      DTMP[:, :], MUR[:, k:k + 1], sh, ALU.mult, ALU.add)
                em.do("vector", "scalar_tensor_tensor", XK[:, tcol(k):tcol(k) + T],
                      DTMP[:, :], MUK[:, k:k + 1], sh, ALU.mult, ALU.add)
            em.do("vector", "engine_nop", sem=vdone)
            em.barrier([vdone], ["scalar"])
            warm(AF.Sigmoid)
            with contextlib.ExitStack() as stq:
                pr1 = stq.enter_context(nc.psum_tensor("pr1", [128, 512], F32))
                pr2 = stq.enter_context(nc.psum_tensor("pr2", [128, 512], F32))
                em.barrier([lfw, vdone], ["tensor"])
                for eo in range(NCH):
                    for tq in range(4):
                        pr = pr1 if (eo * 4 + tq) % 2 == 0 else pr2
                        if eo or tq:
                            em.wait("tensor", sdone, sdone.n - 1)
                        for ei in range(NCH):
                            em.mm(pr[:, :], WRB[:, 1024 * ei + 128 * eo:1024 * ei + 128 * eo + 128],
                                  XHAT[:, tcol(ei, 512 * tq):tcol(ei, 512 * tq) + 512],
                                  start=(ei == 0), stop=(ei == NCH - 1),
                                  sem=(tdone if ei == NCH - 1 else None))
                        em.wait("scalar", tdone)
                        em.do("scalar", "activation", RHAT[:, 2048 * eo + 512 * tq:2048 * eo + 512 * tq + 512],
                              pr[:, :], AF.Sigmoid, sem=sdone)
                em.barrier([sdone], ["gpsimd"])
                for dk in range(NCH):
                    em.dma("gpsimd", WRB[:, 1024 * dk:1024 * dk + 1024],
                           ap["WkT"][128 * dk:128 * dk + 128, :], sem=lfw)
                em.barrier([lfw], ["tensor"])
                for eo in range(NCH):
                    for tq in range(4):
                        pr = pr1 if (eo * 4 + tq) % 2 == 0 else pr2
                        em.wait("tensor", sdone, sdone.n - 1)
                        for ei in range(NCH):
                            em.mm(pr[:, :], WRB[:, 1024 * ei + 128 * eo:1024 * ei + 128 * eo + 128],
                                  XK[:, tcol(ei, 512 * tq):tcol(ei, 512 * tq) + 512],
                                  start=(ei == 0), stop=(ei == NCH - 1),
                                  sem=(tdone if ei == NCH - 1 else None))
                        em.wait("scalar", tdone)
                        em.do("scalar", "activation", ZSC[:, :], pr[:, :], AF.Relu)
                        em.do("scalar", "activation",
                              XHAT[:, tcol(eo, 512 * tq):tcol(eo, 512 * tq) + 512],
                              ZSC[:, :], AF.Square, sem=sdone)
                em.barrier([sdone], ["gpsimd"])
                for dk in range(NCH):
                    em.dma("gpsimd", WRB[:, 1024 * dk:1024 * dk + 1024],
                           ap["WvT"][128 * dk:128 * dk + 128, :], sem=lfw)
                em.barrier([lfw, sdone], ["tensor"])
                for eo in range(NCH):
                    for tq in range(4):
                        pr = pr1 if (eo * 4 + tq) % 2 == 0 else pr2
                        if eo or tq:
                            em.wait("tensor", vdone, vdone.n - 1)
                        for ei in range(NCH):
                            em.mm(pr[:, :], WRB[:, 1024 * ei + 128 * eo:1024 * ei + 128 * eo + 128],
                                  XHAT[:, tcol(ei, 512 * tq):tcol(ei, 512 * tq) + 512],
                                  start=(ei == 0), stop=(ei == NCH - 1),
                                  sem=(tdone if ei == NCH - 1 else None))
                        em.wait("vector", tdone)
                        em.do("vector", "tensor_tensor", DTMP[:, 0:512], pr[:, :],
                              RHAT[:, 2048 * eo + 512 * tq:2048 * eo + 512 * tq + 512], ALU.mult)
                        sl = STRM[:, tcol(eo, 512 * tq):tcol(eo, 512 * tq) + 512]
                        em.do("vector", "tensor_add", sl, sl, DTMP[:, 0:512], sem=vdone)
                em.barrier([vdone], ["tensor", "scalar"])
                warm(AF.Exp)

        if STAGE_LIMIT >= 9:
         with contextlib.ExitStack() as stp:
            emit_ln("l4", write_xhat=False)

        if dev:
            em.barrier([vdone], ["sync"])
            em.dma("sync", dbg["d_strm4"][:, :], STRM[:, :], sem=stdone)
            em.barrier([stdone], ["vector", "tensor", "scalar"])

        with contextlib.ExitStack() as stp:
            sbp = lambda n, s, d=F32: stp.enter_context(nc.sbuf_tensor(n, list(s), d))
            psp = lambda n, s: stp.enter_context(nc.psum_tensor(n, list(s), F32))
            obuf = sbp("obuf", (128, 2 * D))
            pto = psp("pto", (128, 1024))
            pto2 = psp("pto2", (128, 1024))
            em.barrier([vdone], ["tensor"])
            td0 = tdone.n
            vd0 = vdone.n
            st0c = stdone.n
            for tc in range(16):
                pt = pto if tc % 2 == 0 else pto2
                if tc >= 2:
                    em.wait("tensor", vdone, vd0 + tc - 1)
                for ec in range(NCH):
                    em.do("tensor", "transpose",
                          pt[:, 128 * ec:128 * ec + 128],
                          STRM[:, tcol(ec, 128 * tc):tcol(ec, 128 * tc) + 128],
                          ID[:, :], sem=(tdone if ec == NCH - 1 else None))
                em.wait("vector", tdone, td0 + tc + 1)
                if tc >= 2:
                    em.wait("vector", stdone)
                em.do("vector", "tensor_copy", out=obuf[:, (tc % 2) * D:(tc % 2) * D + D],
                      in_=pt[:, :], sem=vdone)
                em.wait("sync", vdone, vd0 + tc + 1)
                em.dma("sync", y[128 * tc:128 * tc + 128, :],
                       obuf[:, (tc % 2) * D:(tc % 2) * D + D], sem=stdone)
            em.wait("sync", stdone)

        STRMSTK[0].close()

        block = st0.enter_context(nc.Block())
        for eng_name, meth in [("sync", block.sync), ("gpsimd", block.gpsimd),
                               ("tensor", block.tensor), ("vector", block.vector),
                               ("scalar", block.scalar)]:
            def body(e, _ops=em.ops[eng_name]):
                for fn in _ops:
                    fn(e)
            meth(body)

    return nc


_CACHE = {}


def kernel(**inputs):
    n = 8
    key = "nc"
    if key not in _CACHE:
        _CACHE[key] = build_nc(DEV)
    nc = _CACHE[key]
    in_maps = [host_inputs(inputs, b) for b in range(n)]
    res = run_bass_kernel_spmd(nc, in_maps, core_ids=list(range(n)))
    out = np.stack([res.results[b]["y"] for b in range(n)], 0)
    if DEV:
        kernel._dev_results = res.results
    return out
